# version 08
# baseline (speedup 1.0000x reference)
import contextlib
import numpy as np
import ml_dtypes
import concourse.bass as bass
import concourse.mybir as mybir
from concourse.bass_utils import run_bass_kernel_spmd

F32 = mybir.dt.float32
BF16 = mybir.dt.bfloat16
I32 = mybir.dt.int32
AF = mybir.ActivationFunctionType
ALU = mybir.AluOpType
AX = mybir.AxisListType

C = 2048
T = 2048
NCI = C // 128
TS = 512
NSL = T // TS


ENGINES = ("pe", "act", "dve", "pool", "sp")


class Buf:
    __slots__ = ("ap", "w", "rs")

    def __init__(self, ap, w=None):
        self.ap = ap
        self.w = w
        self.rs = []


class Prog:
    def __init__(self, nc):
        self.nc = nc
        self.ops = {e: [] for e in ENGINES}
        self.cnt = {}
        self.lastwait = {e: {} for e in ENGINES}
        self.sems = {}
        self._stack = contextlib.ExitStack()
        for s in ("pe", "act", "dve", "pool", "q_sp", "q_pool", "q_act"):
            self.sems[s] = self._stack.enter_context(nc.semaphore("sem_" + s))
            self.cnt[s] = 0
        self._drambufs = {}
        self.xdep = None
        self.QW = 12
        self.qn = {"sp": 0, "pool": 0, "act": 0}
        for q in ("sp", "pool", "act"):
            for i in range(self.QW):
                nme = f"q_{q}_{i}"
                self.sems[nme] = self._stack.enter_context(nc.semaphore("sem_" + nme))
                self.cnt[nme] = 0

    def rotate_queues(self, thresh=0):
        pass

    def stage_floor(self):
        return [(s, self.cnt[s]) for s in self.cnt]

    def drambuf(self, name):
        if name not in self._drambufs:
            self._drambufs[name] = Buf(None)
        return self._drambufs[name]

    _ESEM = {"pe": "pe", "act": "act", "dve": "dve", "pool": "pool"}
    _QENG = {"sp": "sp", "pool": "pool", "act": "act"}

    def _wait(self, eng, tok):
        if tok is None:
            return
        if isinstance(tok, list):
            for t in tok:
                self._wait(eng, t)
            return
        sem, val = tok
        if val <= 0 or self.lastwait[eng].get(sem, 0) >= val:
            return
        self.lastwait[eng][sem] = val
        h = self.sems[sem]
        self.ops[eng].append(lambda e, h=h, v=val: e.wait_ge(h, v))

    def op(self, eng, build, reads=(), writes=(), deps=(), inc=None):
        if inc is None:
            inc = bool(reads or writes)
        deps = list(deps)
        for b in reads:
            deps.append(b.w)
        for b in writes:
            deps.append(b.w)
            deps.extend(b.rs)
        for d in deps:
            self._wait(eng, d)
        tok = None
        if inc:
            s = self._ESEM[eng]
            self.cnt[s] += 1
            tok = (s, self.cnt[s])
            h = self.sems[s]
            self.ops[eng].append(lambda e, b=build, h=h: b(e).then_inc(h, 1))
        else:
            self.ops[eng].append(lambda e, b=build: b(e))
        for b in reads:
            if tok:
                b.rs.append(tok)
        for b in writes:
            b.w = tok
            b.rs = []
        return tok

    def dma(self, q, out, in_, reads=(), writes=(), deps=()):
        eng = self._QENG[q]
        s = f"q_{q}_{self.qn[q] % self.QW}"
        self.qn[q] += 1
        deps = list(deps)
        for b in reads:
            deps.append(b.w)
        for b in writes:
            deps.append(b.w)
            deps.extend(b.rs)
        deps.append((s, self.cnt[s]))
        for d in deps:
            self._wait(eng, d)
        self.cnt[s] += 16
        tok = (s, self.cnt[s])
        h = self.sems[s]
        self.ops[eng].append(
            lambda e, o=out, i=in_, h=h: e.dma_start(out=o, in_=i).then_inc(h, 16))
        for b in reads:
            b.rs.append(tok)
        for b in writes:
            b.w = tok
            b.rs = []
        return tok

    def mm_group(self, mms, reads=(), out_buf=None):
        deps = [b.w for b in reads]
        if out_buf is not None:
            deps.append(out_buf.w)
            deps.extend(out_buf.rs)
        for i, m in enumerate(mms):
            if i < len(mms) - 1:
                self.op("pe", m, deps=deps if i == 0 else (), inc=False)
            else:
                tok = self.op("pe", m, deps=deps if len(mms) == 1 else (), inc=True)
        for b in reads:
            b.rs.append(tok)
        if out_buf is not None:
            out_buf.w = tok
            out_buf.rs = []
        return tok

    def finalize(self, final_deps):
        for d in final_deps:
            self._wait("sp", d)
        nc, ops = self.nc, self.ops
        with nc.Block() as block:
            @block.sync
            def _(e):
                for f in ops["sp"]:
                    f(e)

            @block.tensor
            def _(e):
                for f in ops["pe"]:
                    f(e)

            @block.vector
            def _(e):
                for f in ops["dve"]:
                    f(e)

            @block.scalar
            def _(e):
                for f in ops["act"]:
                    f(e)

            @block.gpsimd
            def _(e):
                for f in ops["pool"]:
                    f(e)
        self._stack.close()


class Arena:
    def __init__(self, nc, name, width, dtype):
        self.t = nc.alloc_sbuf_tensor(name, [128, width], dtype)
        self.width = width
        self.off = 0
        self.marks = []
        self.floor = []

    def alloc(self, w):
        assert self.off + w <= self.width, f"arena {self.off}+{w}>{self.width}"
        ap = self.t[:, self.off:self.off + w]
        self.off += w
        self.hwm = max(getattr(self, "hwm", 0), self.off)
        return ap

    def buf(self, w):
        b = Buf(self.alloc(w))
        b.rs = list(self.floor)
        return b

    def bufs(self, n, w):
        return [self.buf(w) for _ in range(n)]

    def push(self):
        self.marks.append(self.off)

    def pop(self, floor=None):
        self.off = self.marks.pop()
        if floor is not None:
            self.floor = floor


class Psum:
    def __init__(self, nc):
        self._stack = contextlib.ExitStack()
        self.t = self._stack.enter_context(nc.psum_tensor("psum", [128, 4096], F32))
        self.banks = [Buf(self.t[:, i * 512:(i + 1) * 512]) for i in range(8)]

    def bank(self, i, w=512, off=0):
        assert off + w <= 512
        return self.t[:, i * 512 + off: i * 512 + off + w]




class Shared:

    def __init__(self, P, af32, abf):
        self.ones_col_bf = abf.buf(1)
        self.ones_row_bf = abf.buf(512)
        t0 = P.op("dve", lambda e, t=self.ones_col_bf.ap: e.memset(t, 1.0), inc=True)
        t1 = P.op("dve", lambda e, t=self.ones_row_bf.ap[0:1, :]: e.memset(t, 1.0), inc=True)
        self.ones_col_bf.w = t0
        self.ones_row_bf.w = t1
        self.ones_row_f32 = af32.buf(128)
        self.ones_row_f32.w = P.op("dve", lambda e, t=self.ones_row_f32.ap[0:1, :]:
                                   e.memset(t, 1.0), inc=True)
        self.eps = af32.buf(1)
        self.eps.w = P.op("dve", lambda e, t=self.eps.ap[0:1, :]: e.memset(t, 1e-5), inc=True)
        self.xb = abf.bufs(4, TS)
        self.x2b = abf.bufs(4, TS)
        self.rows = af32.buf(4 * TS)
        self.bcast = af32.buf(512 + TS)
        self.scr = {"dve": af32.buf(TS), "pool": af32.buf(TS)}


def emit_norm(P, ps, sh, xs, out, banks=(4, 5, 6), affine=None, ts=TS):
    bS, bS2, bB = banks
    sb, sb2, bb = ps.banks[bS], ps.banks[bS2], ps.banks[bB]
    sb.rs and None
    first_deps = [sb.w] + sb.rs + [sb2.w] + sb2.rs + [sh.ones_col_bf.w]
    for ci in range(NCI):
        k = ci % 4
        eng = "dve" if ci % 2 == 0 else "pool"
        P.op(eng, lambda e, o=sh.xb[k].ap, i=xs[ci].ap: e.tensor_copy(o, i),
             reads=[xs[ci]], writes=[sh.xb[k]])
        P.op("act", lambda e, o=sh.x2b[k].ap, i=xs[ci].ap: e.activation(o, i, AF.Square),
             reads=[xs[ci]], writes=[sh.x2b[k]])
        d0 = first_deps if ci == 0 else ()
        t1 = P.op("pe", lambda e, o=ps.bank(bS, ts), l=sh.ones_col_bf.ap, r=sh.xb[k].ap,
                  s=(ci == 0), p=(ci == NCI - 1): e.matmul(o[0:1, :], l, r, start=s, stop=p),
                  deps=list(d0) + [sh.xb[k].w], inc=True)
        t2 = P.op("pe", lambda e, o=ps.bank(bS2, ts), l=sh.ones_col_bf.ap, r=sh.x2b[k].ap,
                  s=(ci == 0), p=(ci == NCI - 1): e.matmul(o[0:1, :], l, r, start=s, stop=p),
                  deps=[sh.x2b[k].w], inc=True)
        sh.xb[k].rs.append(t1)
        sh.x2b[k].rs.append(t2)
    sb.w = t1; sb.rs = []
    sb2.w = t2; sb2.rs = []
    rows = sh.rows.ap
    nmu = rows[0:1, 0:ts]
    mu2 = rows[0:1, ts:2 * ts]
    var = rows[0:1, 2 * ts:3 * ts]
    rstd = rows[0:1, 3 * ts:4 * ts]
    P.op("dve", lambda e, o=nmu, i=ps.bank(bS, ts)[0:1, :]: e.tensor_scalar_mul(o, i, -1.0 / C),
         reads=[sb], writes=[sh.rows])
    P.op("dve", lambda e, o=mu2, a=nmu: e.tensor_mul(o, a, a), writes=[sh.rows])
    P.op("dve", lambda e, o=var, i=ps.bank(bS2, ts)[0:1, :], m=mu2:
         e.scalar_tensor_tensor(o, i, 1.0 / C, m, ALU.mult, ALU.subtract),
         reads=[sb2], writes=[sh.rows])
    P.op("act", lambda e, o=var, i=var, b=sh.eps.ap[0:1, 0:1]:
         e.activation(o, i, AF.Sqrt, bias=b), reads=[sh.rows, sh.eps], writes=[sh.rows])
    P.op("dve", lambda e, o=rstd, i=var: e.reciprocal(o, i), writes=[sh.rows])
    P.op("dve", lambda e, o=nmu, a=nmu, b=rstd: e.tensor_mul(o, a, b), writes=[sh.rows])
    bb2 = ps.banks[bB + 1]
    tk = P.mm_group(
        [lambda e, o=ps.bank(bB, ts), l=sh.ones_row_f32.ap[0:1, 0:128], r=rstd:
         e.matmul(o, l, r, start=True, stop=True),
         lambda e, o=ps.bank(bB + 1, ts), l=sh.ones_row_f32.ap[0:1, 0:128],
         r=nmu: e.matmul(o, l, r, start=True, stop=True)],
        reads=[sh.rows, sh.ones_row_f32], out_buf=bb)
    bb2.w = tk; bb2.rs = []
    P.op("act", lambda e, o=sh.bcast.ap[:, 0:2 * ts], i=ps.t[:, bB * 512:bB * 512 + 512 + ts]:
         e.copy(o, i), reads=[bb, bb2], writes=[sh.bcast])
    rb = sh.bcast.ap
    for ci in range(NCI):
        eng = "dve" if ci % 2 == 0 else "pool"
        scr = sh.scr[eng]
        P.op(eng, lambda e, o=scr.ap[:, 0:ts], a=xs[ci].ap, b=rb[:, 0:ts]: e.tensor_mul(o, a, b),
             reads=[xs[ci], sh.bcast], writes=[scr])
        if affine is None:
            P.op(eng, lambda e, o=out[ci].ap, a=scr.ap[:, 0:ts], b=rb[:, 512:512 + ts]:
                 e.tensor_add(o, a, b), reads=[scr, sh.bcast], writes=[out[ci]])
        else:
            wcol, bcol = affine
            P.op(eng, lambda e, o=scr.ap[:, 0:ts], a=scr.ap[:, 0:ts], b=rb[:, 512:512 + ts]:
                 e.tensor_add(o, a, b), reads=[sh.bcast], writes=[scr])
            P.op("dve", lambda e, o=out[ci].ap, i=scr.ap[:, 0:ts],
                 w=wcol.ap[:, ci:ci + 1], b=bcol.ap[:, ci:ci + 1]:
                 e.tensor_scalar(o, i, w, b, ALU.mult, ALU.add),
                 reads=[scr, wcol, bcol], writes=[out[ci]])




def load_const_bf16(P, abf, src_ap, w, chunks=None):
    b = abf.buf(w)
    out = b.ap if chunks is None else b.ap.rearrange("p (g n) -> p g n", g=chunks[0])
    P.dma("pool", out, src_ap, writes=[b])
    return b


def emit_ffn0(P, ps, sh, af32, abf, d, xin, xout):
    af32.push(); abf.push()
    rw1b = load_const_bf16(P, abf, d["f0_rw1t"].rearrange("(ci p) h -> p ci h", p=128),
                           NCI * 64, chunks=(NCI, 64))
    atb = load_const_bf16(P, abf, d["f0_at"].rearrange("(ci p) r -> p ci r", p=128),
                          NCI * 256, chunks=(NCI, 256))
    breb = load_const_bf16(P, abf, d["f0_bre"].rearrange("(g p) e -> p g e", p=128),
                           2 * C, chunks=(2, C))
    identb = load_const_bf16(P, abf, d["ident"].ap(), 128)
    eexpb = abf.buf(256)
    P.dma("pool", eexpb.ap[0:16, :], d["f0_eexp"].ap(), writes=[eexpb])
    biasb = abf.buf(C)
    P.dma("pool", biasb.ap[0:1, :], d["f0_bias"].ap(), writes=[biasb])
    rb1c = af32.buf(1)
    P.dma("sp", rb1c.ap[0:64, 0:1], d["f0_rb1c"].ap(), writes=[rb1c])
    rw2x = abf.buf(32)
    P.dma("pool", rw2x.ap[0:64, :], d["f0_rw2t"].ap(), writes=[rw2x])
    r2row = af32.buf(32)
    grow = af32.buf(16)
    P.dma("sp", r2row.ap[0:1, :], d["f0_rb2"].ap(), writes=[r2row])
    P.dma("sp", grow.ap[0:1, :], d["f0_gates"].ap(), writes=[grow])
    P.op("dve", lambda e, o=r2row.ap[0:1, 0:16], a=r2row.ap[0:1, 0:16], b=grow.ap[0:1, :]:
         e.tensor_add(o, a, b), reads=[grow], writes=[r2row])
    P.op("dve", lambda e, o=rw2x.ap[64:65, :], i=r2row.ap[0:1, :]: e.tensor_copy(o, i),
         reads=[r2row], writes=[rw2x])

    xs = af32.bufs(NCI, TS)
    xn = [abf.bufs(NCI, TS) for _ in range(NSL)]
    hx = abf.buf(TS)
    P.op("dve", lambda e, o=hx.ap[64:65, :]: e.memset(o, 1.0), writes=[hx])
    mt = af32.buf(8)
    mask1 = af32.buf(16)
    x2t = af32.buf(16)
    coeff = abf.buf(64)
    ctb = abf.buf(TS)
    cexp = abf.buf(TS)
    wxa = [abf.buf(T) for _ in range(2)]
    wbuf = [abf.buf(NCI * 128) for _ in range(2)]
    xnew = af32.bufs(2, TS)
    wt_r = d["f0_wt"].rearrange("(ci p) e -> p ci e", p=128)

    for s in range(NSL):
        tsl = slice(s * TS, (s + 1) * TS)
        for ci in range(NCI):
            P.dma("sp", xs[ci].ap, xin[ci * 128:(ci + 1) * 128, tsl], writes=[xs[ci]], deps=[P.xdep])
        emit_norm(P, ps, sh, xs, xn[s])
        bH = ps.banks[2]
        P.mm_group([
            (lambda e, o=ps.bank(2, TS), l=rw1b.ap[:, ci * 64:(ci + 1) * 64], r=xn[s][ci].ap,
             st=(ci == 0), sp=(ci == NCI - 1):
             e.matmul(o[0:64, :], l, r, start=st, stop=sp)) for ci in range(NCI)],
            reads=[rw1b] + xn[s], out_buf=bH)
        P.op("act", lambda e, o=hx.ap[0:64, :], i=ps.bank(2, TS)[0:64, :], b=rb1c.ap[0:64, 0:1]:
             e.activation(o, i, AF.Silu, bias=b), reads=[bH, rb1c], writes=[hx])
        for tc in range(TS // 128):
            P.mm_group([lambda e, o=ps.bank(2, 32, off=tc * 32),
                        l=hx.ap[0:65, tc * 128:(tc + 1) * 128], r=rw2x.ap[0:65, :]:
                        e.matmul(o, l, r, start=True, stop=True)],
                       reads=[hx, rw2x], out_buf=bH)
        for tc in range(TS // 128):
            lg = ps.bank(2, 16, off=tc * 32)
            m1 = mt.ap[:, 0:1]; m2 = mt.ap[:, 1:2]; dd = mt.ap[:, 2:3]
            w1 = mt.ap[:, 3:4]; w2 = mt.ap[:, 4:5]
            P.op("dve", lambda e, o=m1, i=lg: e.reduce_max(o, i, axis=AX.X),
                 reads=[bH], writes=[mt])
            P.op("dve", lambda e, o=mask1.ap, i=lg, s_=m1:
                 e.tensor_single_scalar(o, i, s_, ALU.is_ge), reads=[bH, mt], writes=[mask1])
            P.op("dve", lambda e, o=x2t.ap, a=mask1.ap, i=lg:
                 e.scalar_tensor_tensor(o, a, -1e30, i, ALU.mult, ALU.add),
                 reads=[bH, mask1], writes=[x2t])
            P.op("dve", lambda e, o=m2, i=x2t.ap: e.reduce_max(o, i, axis=AX.X),
                 reads=[x2t], writes=[mt])
            P.op("dve", lambda e, o=dd, a=m1, b=m2: e.tensor_sub(o, a, b), writes=[mt])
            P.op("act", lambda e, o=w1, i=dd: e.activation(o, i, AF.Sigmoid),
                 reads=[mt], writes=[mt])
            P.op("dve", lambda e, o=w2, i=w1: e.tensor_scalar(o, i, -1.0, 1.0, ALU.mult, ALU.add),
                 reads=[mt], writes=[mt])
            P.op("dve", lambda e, o=x2t.ap, i=x2t.ap, s_=m2:
                 e.tensor_single_scalar(o, i, s_, ALU.is_ge), reads=[mt], writes=[x2t])
            P.op("dve", lambda e, o=mask1.ap, i=mask1.ap, s_=w1:
                 e.tensor_single_scalar(o, i, s_, ALU.mult), reads=[mt], writes=[mask1])
            P.op("dve", lambda e, o=coeff.ap[:, tc * 16:(tc + 1) * 16], i=x2t.ap, s_=w2,
                 a=mask1.ap: e.scalar_tensor_tensor(o, i, s_, a, ALU.mult, ALU.add),
                 reads=[mt, x2t, mask1], writes=[coeff])
        b3 = ps.banks[3]
        P.mm_group([
            (lambda e, o=ps.bank(3, 128, off=tc * 128), l=coeff.ap[:, tc * 16:(tc + 1) * 16],
             r=identb.ap: e.matmul(o[0:16, :], l, r, start=True, stop=True))
            for tc in range(TS // 128)], reads=[coeff, identb], out_buf=b3)
        P.op("dve", lambda e, o=ctb.ap[0:16, :], i=ps.bank(3, TS)[0:16, :]: e.tensor_copy(o, i),
             reads=[b3], writes=[ctb])
        b4 = ps.banks[4]
        for er in range(2):
            P.mm_group([
                (lambda e, o=ps.bank(5 + er, TS),
                 l=atb.ap[:, ci * 256 + er * 128: ci * 256 + (er + 1) * 128], r=xn[s][ci].ap,
                 st=(ci == 0), sp=(ci == NCI - 1): e.matmul(o, l, r, start=st, stop=sp))
                for ci in range(NCI)], reads=[atb] + xn[s], out_buf=ps.banks[5 + er])
            P.mm_group([lambda e, o=ps.bank(4, TS),
                        l=eexpb.ap[0:16, er * 128:(er + 1) * 128], r=ctb.ap[0:16, :]:
                        e.matmul(o, l, r, start=True, stop=True)],
                       reads=[eexpb, ctb], out_buf=b4)
            P.op("dve", lambda e, o=cexp.ap, i=ps.bank(4, TS): e.tensor_copy(o, i),
                 reads=[b4], writes=[cexp])
            P.op("dve", lambda e, o=wxa[er].ap[:, s * TS:(s + 1) * TS], a=ps.bank(5 + er, TS),
                 b=cexp.ap: e.tensor_mul(o, a, b),
                 reads=[ps.banks[5 + er], cexp], writes=[wxa[er]])

    for dc in range(NCI):
        par = dc % 2
        wb = wbuf[par]
        P.dma("pool", wb.ap.rearrange("p (ci e) -> p ci e", ci=NCI),
              wt_r[:, :, dc * 128:(dc + 1) * 128], writes=[wb])
        for s in range(NSL):
            tsl = slice(s * TS, (s + 1) * TS)
            xr = xs[dc % len(xs)]
            P.dma("sp", xr.ap, xin[dc * 128:(dc + 1) * 128, tsl], writes=[xr], deps=[P.xdep])
            bD = ps.banks[par]
            mms = [
                (lambda e, o=ps.bank(par, TS), l=wb.ap[:, ci * 128:(ci + 1) * 128],
                 r=xn[s][ci].ap, st=(ci == 0): e.matmul(o, l, r, start=st, stop=False))
                for ci in range(NCI)]
            mms.append(lambda e, o=ps.bank(par, TS), l=biasb.ap[0:1, dc * 128:(dc + 1) * 128],
                       r=sh.ones_row_bf.ap[0:1, 0:TS]: e.matmul(o, l, r, start=False, stop=False))
            for er in range(2):
                mms.append(lambda e, o=ps.bank(par, TS),
                           l=breb.ap[:, er * C + dc * 128: er * C + (dc + 1) * 128],
                           r=wxa[er].ap[:, s * TS:(s + 1) * TS], sp=(er == 1):
                           e.matmul(o, l, r, start=False, stop=sp))
            P.mm_group(mms, reads=[wb, biasb, sh.ones_row_bf, breb, wxa[0], wxa[1]] + xn[s],
                       out_buf=bD)
            xo = xnew[par]
            P.op("dve", lambda e, o=xo.ap, i=ps.bank(par, TS), x=xr.ap:
                 e.scalar_tensor_tensor(o, i, 0.5, x, ALU.mult, ALU.add),
                 reads=[bD, xr], writes=[xo])
            P.dma("sp", xout[dc * 128:(dc + 1) * 128, tsl], xo.ap, reads=[xo], deps=[P.xdep])
    af32.pop(P.stage_floor()); abf.pop(P.stage_floor())


def prep_ffn0(inputs, pre):
    f32 = np.float32
    pre["f0_wt"] = np.ascontiguousarray(inputs["L0_ffn_W"].T).astype(f32)
    pre["f0_bias"] = inputs["L0_ffn_bias"].reshape(1, C).astype(f32)
    pre["f0_rw1t"] = np.ascontiguousarray(inputs["L0_ffn_rw1"].T).astype(f32)
    pre["f0_rb1c"] = inputs["L0_ffn_rb1"].reshape(64, 1).astype(f32)
    pre["f0_rw2t"] = np.ascontiguousarray(inputs["L0_ffn_rw2"].T).astype(f32)
    pre["f0_rb2"] = inputs["L0_ffn_rb2"].reshape(1, 32).astype(f32)
    pre["f0_gates"] = inputs["L0_ffn_gates"].reshape(1, 16).astype(f32)
    pre["f0_at"] = np.ascontiguousarray(inputs["L0_ffn_A"].reshape(256, C).T).astype(f32)
    pre["f0_bre"] = np.ascontiguousarray(
        inputs["L0_ffn_B"].transpose(0, 2, 1).reshape(256, C)).astype(f32)
    ident = np.eye(128, dtype=f32)
    pre["ident"] = ident
    eexp = np.zeros((16, 256), dtype=f32)
    for e_ in range(16):
        eexp[e_, e_ * 16:(e_ + 1) * 16] = 1.0
    pre["f0_eexp"] = eexp


FFN0_NAMES = ["f0_wt", "f0_bias", "f0_rw1t", "f0_rb1c", "f0_rw2t", "f0_rb2", "f0_gates",
              "f0_at", "f0_bre", "ident", "f0_eexp"]


def declare_drams(nc, pre, names):
    d = {}
    for n in names:
        a = pre[n]
        dt = {np.dtype(np.float32): F32, np.dtype(np.int32): I32}[a.dtype]
        d[n] = nc.dram_tensor(n, list(a.shape), dt, kind="ExternalInput")
    return d


def build_single_stage(stage_fns, names, scratch=(), sizes=(15000, 72000), x2=False):
    nc = bass.Bass("TRN2", target_bir_lowering=False)
    xin_t = nc.dram_tensor("x_in", [C, T], F32, kind="ExternalInput")
    xout_t = nc.dram_tensor("x_out", [C, T], F32, kind="ExternalOutput")
    P = Prog(nc)
    ps = Psum(nc)
    af32 = Arena(nc, "af32", sizes[0], F32)
    abf = Arena(nc, "abf", sizes[1], BF16)
    sh = Shared(P, af32, abf)
    d = declare_drams(nc, PRE_CACHE, names)
    if x2:
        d["a1_xorig"] = nc.dram_tensor("x_in2", [C, T], F32, kind="ExternalInput")
    for sc_ in scratch:
        n, shp = sc_[0], sc_[1]
        dt = F32 if (len(sc_) > 2 and sc_[2] == "f32") else BF16
        d[n] = nc.dram_tensor(n, list(shp), dt, kind="Internal")
    for fn in stage_fns:
        fn(P, ps, sh, af32, abf, d, xin_t.ap(), xout_t.ap())
    P.finalize([(s, P.cnt[s]) for s in P.cnt])
    return nc


PRE_CACHE = {}


def kernel(**inputs):
    pre = PRE_CACHE
    pre.clear()
    for prep in PREPS:
        prep(inputs, pre)
    x = inputs["L0_attn_x"]
    xf = [np.ascontiguousarray(x[b].T).astype(np.float32) for b in range(8)]
    hist = [xf]
    for spec in LAUNCHES:
        stage_fns, names, scratch = spec[0], spec[1], spec[2]
        sizes = spec[3] if len(spec) > 3 else (15000, 72000)
        x2_from = spec[4] if len(spec) > 4 else None
        nc = build_single_stage(stage_fns, names, scratch, sizes, x2=(x2_from is not None))
        in_maps = []
        for b in range(8):
            m = {n: pre[n] for n in names}
            m["x_in"] = hist[-1][b]
            if x2_from is not None:
                m["x_in2"] = hist[x2_from][b]
            in_maps.append(m)
        res = run_bass_kernel_spmd(nc, in_maps, core_ids=list(range(8)))
        hist.append([np.ascontiguousarray(res.results[b]["x_out"]) for b in range(8)])
    xf = hist[-1]
    out = np.stack([xf[b].T for b in range(8)])
    return np.ascontiguousarray(out).astype(np.float32)


PREPS = [prep_ffn0]
LAUNCHES = [([emit_ffn0], FFN0_NAMES, ())]


def _default_chain():
    global PREPS, LAUNCHES
    PREPS = [prep_ffn0, prep_attn0, prep_ffn1, prep_attn1]
    LAUNCHES = [
        ([emit_ffn0], FFN0_NAMES, (), (15000, 72000)),
        ([emit_attn0], ATTN0_NAMES, ATTN0_SCRATCH, (18000, 70000)),
        ([emit_ffn0], FFN0_NAMES, (), (15000, 72000)),
        ([emit_wconv, emit_ffn1], FFN1_NAMES, FFN1_SCRATCH, (15000, 72000)),
        ([emit_wconv_a1a, emit_attn1a], ATTN1A_NAMES, ATTN1A_SCRATCH, (19500, 64000)),
        ([emit_wconv_a1b, emit_attn1b], ATTN1B_NAMES, ATTN1B_SCRATCH, (17000, 70500), 4),
        ([emit_wconv, emit_ffn1], FFN1_NAMES, FFN1_SCRATCH, (15000, 72000)),
    ]




def emit_wconv(P, ps, sh, af32, abf, d, xin, xout, jobs=None):
    af32.push(); abf.push()
    bin_ = abf.bufs(2, 8192)
    bout = abf.bufs(2, 8192)
    if jobs is None:
        jobs = [("f1_gq", "f1_gbf", 2048, 8192), ("f1_uq", "f1_ubf", 2048, 8192),
                ("f1_dq", "f1_dbf", 8192, 2048)]
    k = 0
    for srcn, dst, n, m in jobs:
        for r in range(n // 128):
            bi, bo = bin_[k % 2], bout[k % 2]
            k += 1
            P.dma("pool", bi.ap[:, 0:m], d[srcn][r * 128:(r + 1) * 128, :], writes=[bi])
            eng = "dve" if k % 2 == 0 else "pool"
            P.op(eng, lambda e, o=bo.ap[:, 0:m], t=bi.ap[:, 0:m]:
                 e.tensor_scalar_add(o, t, -128.0), reads=[bi], writes=[bo])
            P.dma("act", d[dst][:, :, r, :].rearrange("oc p e -> p oc e"),
                  bo.ap[:, 0:m].rearrange("p (oc e) -> p oc e", e=128),
                  reads=[bo], writes=[P.drambuf(dst)])
    af32.pop(P.stage_floor()); abf.pop(P.stage_floor())


def emit_ffn1(P, ps, sh, af32, abf, d, xin, xout):
    af32.push(); abf.push()
    gsc = af32.buf(64)
    usc = af32.buf(64)
    dsc = af32.buf(16)
    P.dma("sp", gsc.ap, d["f1_gsc"].ap(), writes=[gsc])
    P.dma("sp", usc.ap, d["f1_usc"].ap(), writes=[usc])
    P.dma("sp", dsc.ap, d["f1_dsc"].ap(), writes=[dsc])
    P.op("dve", lambda e, t=dsc.ap: e.tensor_scalar_mul(t, t, 0.5), reads=[dsc], writes=[dsc])

    xs = af32.bufs(NCI, TS)
    xn = abf.bufs(NCI, TS)
    h = abf.bufs(64, TS)
    wg = [abf.buf(NCI * 128) for _ in range(2)]
    wu = [abf.buf(NCI * 128) for _ in range(2)]
    wd = [abf.buf(64 * 128) for _ in range(2)]
    hg = af32.bufs(2, TS)
    xnew = af32.bufs(2, TS)

    gq = d["f1_gbf"]
    uq = d["f1_ubf"]
    dq = d["f1_dbf"]

    for s in range(NSL):
        tsl = slice(s * TS, (s + 1) * TS)
        for ci in range(NCI):
            P.dma("sp", xs[ci].ap, xin[ci * 128:(ci + 1) * 128, tsl], writes=[xs[ci]], deps=[P.xdep])
        emit_norm(P, ps, sh, xs, xn)
        for ic in range(64):
            par = ic % 2
            wgb, wub = wg[par], wu[par]
            P.dma("sp", wgb.ap.rearrange("p (hc e) -> p hc e", hc=NCI),
                  gq[ic], reads=[P.drambuf("f1_gbf")], writes=[wgb])
            P.dma("sp", wub.ap.rearrange("p (hc e) -> p hc e", hc=NCI),
                  uq[ic], reads=[P.drambuf("f1_ubf")], writes=[wub])
            bG, bU = ps.banks[par], ps.banks[2 + par]
            P.mm_group([
                (lambda e, o=ps.bank(par, TS), l=wgb.ap[:, hc * 128:(hc + 1) * 128],
                 r=xn[hc].ap, st=(hc == 0), sp=(hc == NCI - 1):
                 e.matmul(o, l, r, start=st, stop=sp)) for hc in range(NCI)],
                reads=[wgb] + xn, out_buf=bG)
            P.mm_group([
                (lambda e, o=ps.bank(2 + par, TS), l=wub.ap[:, hc * 128:(hc + 1) * 128],
                 r=xn[hc].ap, st=(hc == 0), sp=(hc == NCI - 1):
                 e.matmul(o, l, r, start=st, stop=sp)) for hc in range(NCI)],
                reads=[wub] + xn, out_buf=bU)
            hgb = hg[par]
            P.op("act", lambda e, o=hgb.ap, i=ps.bank(par, TS), sc=gsc.ap[:, ic:ic + 1]:
                 e.activation(o, i, AF.Silu, scale=sc), reads=[bG, gsc], writes=[hgb])
            P.op("dve", lambda e, o=h[ic].ap, u=ps.bank(2 + par, TS), sc=usc.ap[:, ic:ic + 1],
                 g=hgb.ap: e.scalar_tensor_tensor(o, u, sc, g, ALU.mult, ALU.mult),
                 reads=[bU, usc, hgb], writes=[h[ic]])
        for hc in range(NCI):
            par = hc % 2
            wdb = wd[par]
            P.dma("sp", wdb.ap.rearrange("p (g e) -> p g e", g=64),
                  dq[hc], reads=[P.drambuf("f1_dbf")], writes=[wdb])
            bD = ps.banks[4 + par]
            P.mm_group([
                (lambda e, o=ps.bank(4 + par, TS), l=wdb.ap[:, ic * 128:(ic + 1) * 128],
                 r=h[ic].ap, st=(ic == 0), sp=(ic == 63):
                 e.matmul(o, l, r, start=st, stop=sp)) for ic in range(64)],
                reads=[wdb] + h, out_buf=bD)
            xo = xnew[par]
            P.op("dve", lambda e, o=xo.ap, i=ps.bank(4 + par, TS), sc=dsc.ap[:, hc:hc + 1],
                 x=xs[hc].ap: e.scalar_tensor_tensor(o, i, sc, x, ALU.mult, ALU.add),
                 reads=[bD, dsc, xs[hc]], writes=[xo])
            P.dma("sp", xout[hc * 128:(hc + 1) * 128, tsl], xo.ap, reads=[xo], deps=[P.xdep])
    af32.pop(P.stage_floor()); abf.pop(P.stage_floor())


def prep_ffn1(inputs, pre):
    f32 = np.float32
    pre["f1_gq"] = np.ascontiguousarray(inputs["L1_ffn_gate_w"].T).astype(np.int32)
    pre["f1_uq"] = np.ascontiguousarray(inputs["L1_ffn_up_w"].T).astype(np.int32)
    pre["f1_dq"] = np.ascontiguousarray(inputs["L1_ffn_down_w"].T).astype(np.int32)
    pre["f1_gsc"] = np.ascontiguousarray(inputs["L1_ffn_gate_scale"].reshape(64, 128).T).astype(f32)
    pre["f1_usc"] = np.ascontiguousarray(inputs["L1_ffn_up_scale"].reshape(64, 128).T).astype(f32)
    pre["f1_dsc"] = np.ascontiguousarray(inputs["L1_ffn_down_scale"].reshape(16, 128).T).astype(f32)


FFN1_NAMES = ["f1_gq", "f1_uq", "f1_dq", "f1_gsc", "f1_usc", "f1_dsc"]
FFN1_SCRATCH = [("f1_gbf", [64, 128, 16, 128]), ("f1_ubf", [64, 128, 16, 128]),
                ("f1_dbf", [16, 128, 64, 128])]



EH = [h for h in range(32) if (h // 4) % 2 == 0]
OH = [h for h in range(32) if (h // 4) % 2 == 1]
QC = 512


def emit_attn0(P, ps, sh, af32, abf, d, xin, xout):
    af32.push(); abf.push()
    kt4 = abf.bufs(4, T)
    vt = abf.bufs(NCI, 8 * 65)
    xs = af32.bufs(NCI, TS)
    xn = abf.bufs(NCI, TS)
    wq = [abf.buf(NCI * 128) for _ in range(2)]
    wvb = abf.bufs(2, 512)
    t1k = af32.buf(TS); t2k = af32.buf(TS)
    qf = af32.bufs(2, TS)
    qsw = af32.bufs(2, TS)
    qob = abf.bufs(2, TS)
    qrow = af32.buf(40)
    brow = af32.buf(34)
    bcol = af32.buf(32)
    bcsb = af32.buf(QC)
    rrow = af32.buf(QC)
    xnew = af32.bufs(2, TS)
    bqrow = abf.buf(C)
    bkrow = abf.buf(512)
    bvrow = abf.buf(512)
    borow = abf.buf(C)
    qs = abf.bufs(16, QC)
    pta = abf.buf(8 * QC)
    ptb = abf.buf(8 * QC)
    att = abf.bufs(NCI, QC)
    wo2 = abf.buf(NCI * 128)
    wo = [wo2, wo2]
    abscr = qob[0]
    P.dma("pool", bqrow.ap[0:1, :], d["a0_bq"].ap(), writes=[bqrow])
    P.dma("pool", bkrow.ap[0:1, :], d["a0_bk"].ap(), writes=[bkrow])
    P.dma("pool", bvrow.ap[0:1, :], d["a0_bv"].ap(), writes=[bvrow])
    P.dma("pool", borow.ap[0:1, :], d["a0_bo"].ap(), writes=[borow])
    wv_r = d["a0_wvt"].rearrange("(ci p) e -> p ci e", p=128)
    for ktile in range(NCI):
        for hh in range(8):
            P.op("dve", lambda e, o=vt[ktile].ap[:, hh * 65 + 64: hh * 65 + 65]:
                 e.memset(o, 1.0), writes=[vt[ktile]])

    wq_r = d["a0_wqt"].rearrange("(ci p) e -> p ci e", p=128)
    wk_r = d["a0_wkt"].rearrange("(ci p) e -> p ci e", p=128)
    wo_r = d["a0_wot"].rearrange("(ci p) e -> p ci e", p=128)
    qscr = d["a0_qscr"]

    qm = qrow.ap[0:1, 0:32]
    km = qrow.ap[0:1, 32:40]
    P.op("dve", lambda e, o=qrow.ap[0:1, :]: e.memset(o, 0.0), writes=[qrow])

    for s in range(NSL):
        tsl = slice(s * TS, (s + 1) * TS)
        for ci in range(NCI):
            P.dma("sp", xs[ci].ap, xin[ci * 128:(ci + 1) * 128, tsl], writes=[xs[ci]], deps=[P.xdep])
        emit_norm(P, ps, sh, xs, xn)
        for tb, srcn in ((t1k, "a0_t1k"), (t2k, "a0_t2k")):
            P.dma("sp", tb.ap, d[srcn][:, tsl], writes=[tb])
        for j in range(20):
            isk = j >= 16
            m = j - 16 if isk else j
            wb = wq[j % 2]
            P.dma("pool", wb.ap.rearrange("p (ci e) -> p ci e", ci=NCI),
                  (wk_r if isk else wq_r)[:, :, m * 128:(m + 1) * 128], writes=[wb])
            bank = ps.banks[j % 2]
            mms = [
                (lambda e, o=ps.bank(j % 2, TS), l=wb.ap[:, ci * 128:(ci + 1) * 128],
                 r=xn[ci].ap, st=(ci == 0): e.matmul(o, l, r, start=st, stop=False))
                for ci in range(NCI)]
            br = bkrow if isk else bqrow
            mms.append(lambda e, o=ps.bank(j % 2, TS), l=br.ap[0:1, m * 128:(m + 1) * 128],
                       r=sh.ones_row_bf.ap[0:1, 0:TS]: e.matmul(o, l, r, start=False, stop=True))
            P.mm_group(mms, reads=[wb, br, sh.ones_row_bf] + xn, out_buf=bank)
            qfb, qsb = qf[j % 2], qsw[j % 2]
            P.op("act", lambda e, o=qfb.ap, i=ps.bank(j % 2, TS): e.copy(o, i),
                 reads=[bank], writes=[qfb])
            for blk in range(4):
                sw = (blk ^ 1) * 32
                P.dma("sp", qsb.ap[sw:sw + 32, :], qfb.ap[blk * 32:(blk + 1) * 32, :],
                      reads=[qfb], writes=[qsb])
            T1, T2 = t1k, t2k
            dst = kt4[m] if isk else qob[j % 2]
            dsl = tsl if isk else slice(0, TS)
            eng = "dve" if j % 2 == 0 else "pool"
            scr = sh.scr[eng]
            P.op(eng, lambda e, o=scr.ap[:, 0:TS], a=qfb.ap, b=T1.ap: e.tensor_mul(o, a, b),
                 reads=[qfb, T1], writes=[scr])
            P.op(eng, lambda e, o=qsb.ap, a=qsb.ap, b=T2.ap: e.tensor_mul(o, a, b),
                 reads=[T2, qsb], writes=[qsb])
            P.op(eng, lambda e, o=dst.ap[:, dsl], a=scr.ap[:, 0:TS], b=qsb.ap:
                 e.tensor_add(o, a, b), reads=[scr, qsb], writes=[dst])
            if not isk:
                P.dma("sp", qscr[m * 128:(m + 1) * 128, tsl], dst.ap,
                      reads=[dst], writes=[P.drambuf("a0_qscr")])
            q2 = sh.x2b[j % 4]
            P.op("act", lambda e, o=q2.ap, i=dst.ap[:, dsl]: e.activation(o, i, AF.Square),
                 reads=[dst], writes=[q2])
            for half in range(2):
                bnk = ps.banks[4 + half]
                tk = P.op("pe", lambda e, o=ps.bank(4 + half, TS),
                          l=sh.ones_col_bf.ap[half * 64:(half + 1) * 64, :],
                          r=q2.ap[half * 64:(half + 1) * 64, :]:
                          e.matmul(o[0:1, :], l, r, start=True, stop=True),
                          deps=[q2.w, bnk.w] + bnk.rs, inc=True)
                bnk.w = tk; bnk.rs = []
                q2.rs.append(tk)
                if isk:
                    tgt = km[0:1, 2 * m + half:2 * m + half + 1]
                else:
                    tgt = qm[0:1, (EH if half == 0 else OH)[m]:(EH if half == 0 else OH)[m] + 1]
                P.op("dve", lambda e, o=sh.rows.ap[0:1, 0:1], i=ps.bank(4 + half, TS)[0:1, :]:
                     e.reduce_max(o, i, axis=AX.X), reads=[bnk], writes=[sh.rows])
                P.op("dve", lambda e, o=tgt, a=tgt, b=sh.rows.ap[0:1, 0:1]:
                     e.tensor_max(o, a, b), reads=[sh.rows], writes=[qrow])
        for ci in range(NCI):
            wvc = wvb[ci % 2]
            P.dma("pool", wvc.ap, wv_r[:, ci, :], writes=[wvc])
            for v4 in range(4):
                bnk = ps.banks[v4]
                tk = P.op("pe", lambda e, o=ps.bank(v4, 512),
                          l=xn[ci].ap[:, v4 * 128:(v4 + 1) * 128], r=wvc.ap,
                          st=(ci == 0): e.matmul(o, l, r, start=st, stop=False),
                          deps=([bnk.w] + bnk.rs if ci == 0 else []) + [wvc.w, xn[ci].w],
                          inc=(v4 == 3))
                if ci == 0:
                    bnk.rs = []
            wvc.rs.append(("pe", P.cnt["pe"]))
        for v4 in range(4):
            ktile = s * 4 + v4
            bnk = ps.banks[v4]
            tk = P.op("pe", lambda e, o=ps.bank(v4, 512), l=sh.ones_row_bf.ap[0:1, 0:128],
                      r=bvrow.ap[0:1, :]: e.matmul(o, l, r, start=False, stop=True),
                      deps=[bvrow.w, sh.ones_row_bf.w], inc=True)
            bnk.w = tk
            for hh in range(8):
                P.op("dve", lambda e, o=vt[ktile].ap[:, hh * 65:hh * 65 + 64],
                     i=ps.bank(v4, 64, off=hh * 64): e.tensor_copy(o, i),
                     reads=[bnk], writes=[vt[ktile]])
    P.op("act", lambda e, o=qm, i=qm: e.activation(o, i, AF.Sqrt), reads=[qrow], writes=[qrow])
    P.op("act", lambda e, o=km, i=km: e.activation(o, i, AF.Sqrt), reads=[qrow], writes=[qrow])
    for kv in range(8):
        grp = [h for h in range(32) if h // 4 == kv]
        for h in grp:
            P.op("dve", lambda e, o=brow.ap[0:1, h:h + 1], a=qm[0:1, h:h + 1] if False else
                 qrow.ap[0:1, h:h + 1], b=qrow.ap[0:1, 32 + kv:33 + kv]:
                 e.tensor_mul(o, a, b), reads=[qrow], writes=[brow])
    P.op("dve", lambda e, o=brow.ap[0:1, 0:32], i=brow.ap[0:1, 0:32]:
         e.tensor_scalar_mul(o, i, -1.0 / 8.0), reads=[brow], writes=[brow])
    bb6 = ps.banks[6]
    P.mm_group([lambda e, o=ps.bank(6, 32), l=sh.ones_row_f32.ap[0:1, 0:128],
                r=brow.ap[0:1, 0:32]: e.matmul(o, l, r, start=True, stop=True)],
               reads=[brow, sh.ones_row_f32], out_buf=bb6)
    P.op("act", lambda e, o=bcol.ap, i=ps.bank(6, 32): e.copy(o, i), reads=[bb6], writes=[bcol])

    qscr_r = qscr.rearrange("(m p) t -> p m t", p=128)
    for qc in range(T // QC):
        qsl = slice(qc * QC, (qc + 1) * QC)
        for m in range(16):
            P.dma("sp", qs[m].ap, qscr_r[:, m, qsl], reads=[P.drambuf("a0_qscr")],
                  writes=[qs[m]])
        for m in range(16):
            ha, hb = EH[m], OH[m]
            ka, kb = ha // 4, hb // 4
            for gh in range(2):
                for g in range(4):
                    for half in range(2):
                        kk = 8 * gh + 2 * g + half
                        P.op("pe", lambda e, o=ps.bank(half, QC),
                             l=kt4[ka // 2].ap[0:64, kk * 128:(kk + 1) * 128],
                             r=qs[m].ap[0:64, :]:
                             e.matmul(o, l, r, start=True, stop=True),
                             deps=[kt4[ka // 2].w, qs[m].w, ps.banks[half].w] + ps.banks[half].rs,
                             inc=(half == 1))
                        P.op("pe", lambda e, o=ps.bank(2 + half, QC),
                             l=kt4[kb // 2].ap[64:128, kk * 128:(kk + 1) * 128],
                             r=qs[m].ap[64:128, :]:
                             e.matmul(o, l, r, start=True, stop=True),
                             deps=[kt4[kb // 2].w, ps.banks[2 + half].w] + ps.banks[2 + half].rs,
                             inc=(half == 1))
                    tkS = ("pe", P.cnt["pe"])
                    for bk in range(4):
                        ps.banks[bk].w = tkS
                        ps.banks[bk].rs = []
                    P.op("act", lambda e, o=pta.ap[:, 2 * g * QC:(2 * g + 2) * QC],
                         i=ps.t[:, 0:2 * QC], b=bcol.ap[:, ha:ha + 1]:
                         e.activation(o, i, AF.Exp, bias=b, scale=1.0 / 8.0),
                         reads=[ps.banks[0], ps.banks[1], bcol], writes=[pta])
                    P.op("act", lambda e, o=ptb.ap[:, 2 * g * QC:(2 * g + 2) * QC],
                         i=ps.t[:, 2 * QC:4 * QC], b=bcol.ap[:, hb:hb + 1]:
                         e.activation(o, i, AF.Exp, bias=b, scale=1.0 / 8.0),
                         reads=[ps.banks[2], ps.banks[3], bcol], writes=[ptb])
                for half, (pt, kv, bnk) in enumerate(((pta, ka, 4), (ptb, kb, 5))):
                    P.mm_group([
                        (lambda e, o=ps.bank(bnk, QC)[0:65, :],
                         l=vt[8 * gh + k8].ap[:, kv * 65:(kv + 1) * 65],
                         r=pt.ap[:, k8 * QC:(k8 + 1) * QC],
                         st=(gh == 0 and k8 == 0), sp=(gh == 1 and k8 == 7):
                         e.matmul(o, l, r, start=st, stop=sp)) for k8 in range(8)],
                        reads=[pt] + vt, out_buf=ps.banks[bnk])
            for half, bnk in ((0, 4), (1, 5)):
                P.op("dve", lambda e, o=rrow.ap[0:1, :], i=ps.bank(bnk, QC)[64:65, :]:
                     e.reciprocal(o, i), reads=[ps.banks[bnk]], writes=[rrow])
                bc = ps.banks[6]
                P.mm_group([lambda e, o=ps.bank(6, QC), l=sh.ones_row_f32.ap[0:1, 0:128],
                            r=rrow.ap[0:1, :]: e.matmul(o, l, r, start=True, stop=True)],
                           reads=[rrow, sh.ones_row_f32], out_buf=bc)
                P.op("act", lambda e, o=bcsb.ap, i=ps.bank(6, QC): e.copy(o, i),
                     reads=[bc], writes=[bcsb])
                if half == 0:
                    P.op("dve", lambda e, o=att[m].ap[0:64, :], a=ps.bank(bnk, QC)[0:64, :],
                         b=bcsb.ap[0:64, :]: e.tensor_mul(o, a, b),
                         reads=[ps.banks[bnk], bcsb], writes=[att[m]])
                else:
                    P.op("dve", lambda e, o=abscr.ap[0:64, :], a=ps.bank(bnk, QC)[0:64, :],
                         b=bcsb.ap[0:64, :]: e.tensor_mul(o, a, b),
                         reads=[ps.banks[bnk], bcsb], writes=[abscr])
                    P.dma("sp", att[m].ap[64:128, :], abscr.ap[0:64, :],
                          reads=[abscr], writes=[att[m]])
        for dc in range(NCI):
            par = dc % 2
            wb = wo[par]
            P.dma("pool", wb.ap.rearrange("p (ci e) -> p ci e", ci=NCI),
                  wo_r[:, :, dc * 128:(dc + 1) * 128], writes=[wb])
            xr = xs[dc]
            P.dma("sp", xr.ap[:, 0:QC], xin[dc * 128:(dc + 1) * 128, qsl], writes=[xr], deps=[P.xdep])
            bD = ps.banks[6 + par]
            mms = [
                (lambda e, o=ps.bank(6 + par, QC), l=wb.ap[:, ci * 128:(ci + 1) * 128],
                 r=att[ci].ap, st=(ci == 0): e.matmul(o, l, r, start=st, stop=False))
                for ci in range(NCI)]
            mms.append(lambda e, o=ps.bank(6 + par, QC), l=borow.ap[0:1, dc * 128:(dc + 1) * 128],
                       r=sh.ones_row_bf.ap[0:1, 0:QC]: e.matmul(o, l, r, start=False, stop=True))
            P.mm_group(mms, reads=[wb, borow, sh.ones_row_bf] + att, out_buf=bD)
            xo = xnew[par]
            P.op("dve", lambda e, o=xo.ap[:, 0:QC], i=ps.bank(6 + par, QC), x=xr.ap[:, 0:QC]:
                 e.tensor_add(o, i, x), reads=[bD, xr], writes=[xo])
            P.dma("sp", xout[dc * 128:(dc + 1) * 128, qsl], xo.ap[:, 0:QC], reads=[xo], deps=[P.xdep])
    af32.pop(P.stage_floor()); abf.pop(P.stage_floor())


def _perm64():
    return [2 * i for i in range(32)] + [2 * i + 1 for i in range(32)]


def prep_attn0(inputs, pre):
    f32 = np.float32
    p64 = _perm64()
    qperm = []
    for m in range(16):
        for h in (EH[m], OH[m]):
            qperm.extend([h * 64 + dd for dd in p64])
    kperm = []
    for kv in range(8):
        kperm.extend([kv * 64 + dd for dd in p64])
    operm = []
    for m in range(16):
        for h in (EH[m], OH[m]):
            operm.extend([h * 64 + dd for dd in range(64)])
    wq = inputs["L0_attn_W_q"]
    wkv = inputs["L0_attn_W_kv"]
    wo = inputs["L0_attn_W_out"]
    pre["a0_wqt"] = np.ascontiguousarray(wq[qperm, :].T).astype(f32)
    pre["a0_wkt"] = np.ascontiguousarray(wkv[0:512][kperm, :].T).astype(f32)
    pre["a0_wvt"] = np.ascontiguousarray(wkv[512:1024].T).astype(f32)
    pre["a0_wot"] = np.ascontiguousarray(wo[:, operm].T).astype(f32)
    pre["a0_bq"] = inputs["L0_attn_b_q"][qperm].reshape(1, C).astype(f32)
    pre["a0_bk"] = inputs["L0_attn_b_kv"][0:512][kperm].reshape(1, 512).astype(f32)
    pre["a0_bo"] = inputs["L0_attn_b_out"].reshape(1, C).astype(f32)
    pre["a0_bv"] = inputs["L0_attn_b_kv"][512:1024].reshape(1, 512).astype(f32)
    pos = np.arange(T, dtype=np.float64)
    fr = 1.0 / 10000.0 ** (np.arange(0, 64, 2, dtype=np.float64) / 64.0)
    th = pos[None, :] * fr[:, None]
    cos = np.cos(th); sin = np.sin(th)
    T1 = np.tile(cos, (4, 1)).astype(f32)
    T2 = np.concatenate([-sin, sin, -sin, sin], axis=0).astype(f32)
    pre["a0_t1k"] = T1
    pre["a0_t2k"] = T2


ATTN0_NAMES = ["a0_wqt", "a0_wkt", "a0_wvt", "a0_wot", "a0_bq", "a0_bk", "a0_bo", "a0_bv",
               "a0_t1k", "a0_t2k"]


ATTN0_SCRATCH = [("a0_qscr", [2048, 2048])]




def _wconv_cast(P, abf, d, jobs):
    abf.push()
    buf = abf.bufs(2, 8192)
    k = 0
    for srcn, dstn, n, m in jobs:
        for r in range(n // 128):
            b = buf[k % 2]
            k += 1
            P.dma("pool", b.ap[:, 0:m], d[srcn][r * 128:(r + 1) * 128, :], writes=[b])
            P.dma("act", d[dstn][:, :, r, :].rearrange("oc p e -> p oc e"),
                  b.ap[:, 0:m].rearrange("p (oc e) -> p oc e", e=128),
                  reads=[b], writes=[P.drambuf(dstn)])
    abf.pop(P.stage_floor())


def emit_wconv_a1a(P, ps, sh, af32, abf, d, xin, xout):
    _wconv_cast(P, abf, d, [("a1_wkt", "a1_wkb", C, C), ("a1_wvt", "a1_wvb", C, C),
                            ("a1_wrt", "a1_wrb", C, C), ("a1_wot", "a1_wob", C, C)])


def emit_wconv_a1b(P, ps, sh, af32, abf, d, xin, xout):
    _wconv_cast(P, abf, d, [("a1_wckt", "a1_wckb", C, 8192), ("a1_wcvt", "a1_wcvb", 8192, C),
                            ("a1_wcrt", "a1_wcrb", C, C)])


def _attn1_cols(P, af32, d, names):
    cols = {}
    for n in names:
        cols[n] = af32.buf(16)
        P.dma("sp", cols[n].ap, d["a1_" + n].ap(), writes=[cols[n]])
    return cols


def emit_attn1a(P, ps, sh, af32, abf, d, xin, xout):
    af32.push(); abf.push()
    cols = _attn1_cols(P, af32, d, ("ln1w", "ln1b", "tmk", "tmv", "tmr", "decay", "first"))
    ew = af32.buf(16)
    eu = af32.buf(16)
    P.op("act", lambda e, o=ew.ap, i=cols["decay"].ap: e.activation(o, i, AF.Exp),
         reads=[cols["decay"]], writes=[ew])
    P.op("act", lambda e, o=ew.ap, i=ew.ap: e.activation(o, i, AF.Exp, scale=-1.0),
         reads=[ew], writes=[ew])
    P.op("act", lambda e, o=eu.ap, i=cols["first"].ap: e.activation(o, i, AF.Exp),
         reads=[cols["first"]], writes=[eu])

    xs = af32.bufs(NCI, TS)
    xn1 = abf.bufs(NCI, TS)
    xxt = abf.bufs(2, TS)
    xk = abf.bufs(NCI, TS)
    xv = abf.bufs(NCI, TS)
    xr = abf.bufs(NCI, TS)
    attin = abf.bufs(NCI, TS)
    wkb = [abf.buf(NCI * 128) for _ in range(2)]
    pcol1 = af32.buf(16)
    astate = af32.buf(16)
    bstate = af32.buf(16)
    kf = af32.bufs(1, TS); vf = af32.bufs(1, TS)
    ekt = af32.bufs(1, TS); ekv = af32.bufs(1, TS)
    ewt = af32.buf(TS)
    ahist = af32.bufs(1, TS + 1); bhist = af32.bufs(1, TS + 1)
    numt = af32.bufs(1, TS); dent = af32.bufs(1, TS)
    srt = [abf.buf(TS)]
    xmid = af32.bufs(1, TS)
    for b_ in (pcol1, astate, bstate):
        P.op("dve", lambda e, o=b_.ap: e.memset(o, 0.0), writes=[b_])
    P.op("dve", lambda e, o=ewt.ap: e.memset(o, 0.0), writes=[ewt])

    wk_r = d["a1_wkt"].rearrange("(ci p) e -> p ci e", p=128)
    wv_r = d["a1_wvt"].rearrange("(ci p) e -> p ci e", p=128)
    wr_r = d["a1_wrt"].rearrange("(ci p) e -> p ci e", p=128)
    wo_r = d["a1_wot"].rearrange("(ci p) e -> p ci e", p=128)

    for s in range(NSL):
        tsl = slice(s * TS, (s + 1) * TS)
        for ci in range(NCI):
            P.dma("sp", xs[ci].ap, xin[ci * 128:(ci + 1) * 128, tsl], writes=[xs[ci]], deps=[P.xdep])
        emit_norm(P, ps, sh, xs, xs)
        emit_norm(P, ps, sh, xs, xn1, affine=(cols["ln1w"], cols["ln1b"]))
        for ci in range(NCI):
            xx = xxt[ci % 2]
            P.op("dve", lambda e, o=xx.ap[:, 1:TS], i=xn1[ci].ap[:, 0:TS - 1]:
                 e.tensor_copy(o, i), reads=[xn1[ci]], writes=[xx])
            P.op("dve", lambda e, o=xx.ap[:, 0:1], i=pcol1.ap[:, ci:ci + 1]:
                 e.tensor_copy(o, i), reads=[pcol1], writes=[xx])
            P.op("dve", lambda e, o=pcol1.ap[:, ci:ci + 1], i=xn1[ci].ap[:, TS - 1:TS]:
                 e.tensor_copy(o, i), reads=[xn1[ci]], writes=[pcol1])
            eng = "dve" if ci % 2 == 0 else "pool"
            scr = sh.scr[eng]
            P.op(eng, lambda e, o=scr.ap[:, 0:TS], a=xn1[ci].ap, b=xx.ap: e.tensor_sub(o, a, b),
                 reads=[xn1[ci], xx], writes=[scr])
            for dstb, cn in ((xk[ci], "tmk"), (xv[ci], "tmv"), (xr[ci], "tmr")):
                P.op("dve", lambda e, o=dstb.ap, i=scr.ap[:, 0:TS], sc=cols[cn].ap[:, ci:ci + 1],
                     x=xx.ap: e.scalar_tensor_tensor(o, i, sc, x, ALU.mult, ALU.add),
                     reads=[scr, cols[cn], xx], writes=[dstb])
        for dc in range(NCI):
            par = dc % 2
            for w_r, rhs, bank in ((wk_r, xk, 0), (wv_r, xv, 1), (wr_r, xr, 2)):
                wb = wkb[par if bank != 2 else (par ^ 1)]
                wb = wkb[par]
                P.dma("pool", wb.ap.rearrange("p (ci e) -> p ci e", ci=NCI),
                      w_r[:, :, dc * 128:(dc + 1) * 128], writes=[wb])
                P.mm_group([
                    (lambda e, o=ps.bank(bank, TS), l=wb.ap[:, ci * 128:(ci + 1) * 128],
                     r=rhs[ci].ap, st=(ci == 0), sp=(ci == NCI - 1):
                     e.matmul(o, l, r, start=st, stop=sp)) for ci in range(NCI)],
                    reads=[wb] + rhs, out_buf=ps.banks[bank])
            kb, vb, ekb, evb = kf[0], vf[0], ekt[0], ekv[0]
            ah, bh = ahist[0], bhist[0]
            nm, dn, sr = numt[0], dent[0], srt[0]
            P.op("act", lambda e, o=kb.ap, i=ps.bank(0, TS): e.copy(o, i),
                 reads=[ps.banks[0]], writes=[kb])
            P.op("act", lambda e, o=vb.ap, i=ps.bank(1, TS): e.copy(o, i),
                 reads=[ps.banks[1]], writes=[vb])
            P.op("act", lambda e, o=sr.ap, i=ps.bank(2, TS): e.activation(o, i, AF.Sigmoid),
                 reads=[ps.banks[2]], writes=[sr])
            P.op("act", lambda e, o=ekb.ap, i=kb.ap: e.activation(o, i, AF.Exp),
                 reads=[kb], writes=[ekb])
            P.op("dve", lambda e, o=evb.ap, a=ekb.ap, b=vb.ap: e.tensor_mul(o, a, b),
                 reads=[ekb, vb], writes=[evb])
            P.op("dve", lambda e, o=ewt.ap, i=ewt.ap, sc=ew.ap[:, dc:dc + 1]:
                 e.tensor_scalar(o, i, 0.0, sc, ALU.mult, ALU.add),
                 reads=[ew], writes=[ewt])
            P.op("dve", lambda e, o=ah.ap[:, 0:1], i=astate.ap[:, dc:dc + 1]:
                 e.tensor_copy(o, i), reads=[astate], writes=[ah])
            P.op("dve", lambda e, o=bh.ap[:, 0:1], i=bstate.ap[:, dc:dc + 1]:
                 e.tensor_copy(o, i), reads=[bstate], writes=[bh])
            P.op("dve", lambda e, o=ah.ap[:, 1:TS + 1], d0=ewt.ap, d1=evb.ap,
                 ini=astate.ap[:, dc:dc + 1]:
                 e.tensor_tensor_scan(o, d0, d1, ini, ALU.mult, ALU.add),
                 reads=[ewt, evb, astate], writes=[ah])
            P.op("dve", lambda e, o=bh.ap[:, 1:TS + 1], d0=ewt.ap, d1=ekb.ap,
                 ini=bstate.ap[:, dc:dc + 1]:
                 e.tensor_tensor_scan(o, d0, d1, ini, ALU.mult, ALU.add),
                 reads=[ewt, ekb, bstate], writes=[bh])
            P.op("dve", lambda e, o=astate.ap[:, dc:dc + 1], i=ah.ap[:, TS:TS + 1]:
                 e.tensor_copy(o, i), reads=[ah], writes=[astate])
            P.op("dve", lambda e, o=bstate.ap[:, dc:dc + 1], i=bh.ap[:, TS:TS + 1]:
                 e.tensor_copy(o, i), reads=[bh], writes=[bstate])
            P.op("dve", lambda e, o=nm.ap, i=evb.ap, sc=eu.ap[:, dc:dc + 1],
                 x=ah.ap[:, 0:TS]: e.scalar_tensor_tensor(o, i, sc, x, ALU.mult, ALU.add),
                 reads=[evb, eu, ah], writes=[nm])
            P.op("dve", lambda e, o=dn.ap, i=ekb.ap, sc=eu.ap[:, dc:dc + 1],
                 x=bh.ap[:, 0:TS]: e.scalar_tensor_tensor(o, i, sc, x, ALU.mult, ALU.add),
                 reads=[ekb, eu, bh], writes=[dn])
            P.op("dve", lambda e, o=dn.ap, i=dn.ap: e.reciprocal(o, i),
                 reads=[dn], writes=[dn])
            P.op("dve", lambda e, o=nm.ap, a=nm.ap, b=dn.ap: e.tensor_mul(o, a, b),
                 reads=[dn], writes=[nm])
            P.op("dve", lambda e, o=attin[dc].ap, a=nm.ap, b=sr.ap: e.tensor_mul(o, a, b),
                 reads=[nm, sr], writes=[attin[dc]])
        for dc in range(NCI):
            par = dc % 2
            wb = wkb[par]
            P.dma("pool", wb.ap.rearrange("p (ci e) -> p ci e", ci=NCI),
                  wo_r[:, :, dc * 128:(dc + 1) * 128], writes=[wb])
            bD = ps.banks[4 + par]
            P.mm_group([
                (lambda e, o=ps.bank(4 + par, TS), l=wb.ap[:, ci * 128:(ci + 1) * 128],
                 r=attin[ci].ap, st=(ci == 0), sp=(ci == NCI - 1):
                 e.matmul(o, l, r, start=st, stop=sp)) for ci in range(NCI)],
                reads=[wb] + attin, out_buf=bD)
            xo = xmid[0]
            P.op("dve", lambda e, o=xo.ap, i=ps.bank(4 + par, TS), z=xs[dc].ap:
                 e.tensor_add(o, i, z), reads=[bD, xs[dc]], writes=[xo])
            P.dma("sp", xout[dc * 128:(dc + 1) * 128, tsl], xo.ap, reads=[xo], deps=[P.xdep])
    af32.pop(P.stage_floor()); abf.pop(P.stage_floor())


def emit_attn1b(P, ps, sh, af32, abf, d, xin, xout):
    af32.push(); abf.push()
    cols = _attn1_cols(P, af32, d, ("ln2w", "ln2b", "cmk", "cmr"))
    xorig = d["a1_xorig"]
    xm = af32.bufs(NCI, TS)
    xn2 = abf.bufs(NCI, TS)
    xx2 = abf.bufs(1, TS)
    ck = abf.bufs(NCI, TS)
    cr = abf.bufs(NCI, TS)
    h = abf.bufs(64, TS)
    _wckb = abf.buf(NCI * 128)
    wck = [_wckb, _wckb]
    _wcvb = abf.buf(32 * 128)
    wcv = [_wcvb, _wcvb]
    pcol2 = af32.buf(16)
    hgf = af32.bufs(1, TS)
    sgf = af32.bufs(1, TS)
    xrld = af32.bufs(2, TS)
    xnew = af32.bufs(2, TS)
    P.op("dve", lambda e, o=pcol2.ap: e.memset(o, 0.0), writes=[pcol2])
    wck_r = d["a1_wckt"].rearrange("(ci p) e -> p ci e", p=128)
    wcv_r = d["a1_wcvt"].rearrange("(g p) e -> p g e", p=128)
    wcr_r = d["a1_wcrt"].rearrange("(ci p) e -> p ci e", p=128)
    for s in range(NSL):
        tsl = slice(s * TS, (s + 1) * TS)
        for ci in range(NCI):
            P.dma("sp", xm[ci].ap, xin[ci * 128:(ci + 1) * 128, tsl], writes=[xm[ci]], deps=[P.xdep])
        emit_norm(P, ps, sh, xm, xn2, affine=(cols["ln2w"], cols["ln2b"]))
        for ci in range(NCI):
            xx = xx2[0]
            P.op("dve", lambda e, o=xx.ap[:, 1:TS], i=xn2[ci].ap[:, 0:TS - 1]:
                 e.tensor_copy(o, i), reads=[xn2[ci]], writes=[xx])
            P.op("dve", lambda e, o=xx.ap[:, 0:1], i=pcol2.ap[:, ci:ci + 1]:
                 e.tensor_copy(o, i), reads=[pcol2], writes=[xx])
            P.op("dve", lambda e, o=pcol2.ap[:, ci:ci + 1], i=xn2[ci].ap[:, TS - 1:TS]:
                 e.tensor_copy(o, i), reads=[xn2[ci]], writes=[pcol2])
            eng = "dve" if ci % 2 == 0 else "pool"
            scr = sh.scr[eng]
            P.op(eng, lambda e, o=scr.ap[:, 0:TS], a=xn2[ci].ap, b=xx.ap: e.tensor_sub(o, a, b),
                 reads=[xn2[ci], xx], writes=[scr])
            for dstb, cn in ((ck[ci], "cmk"), (cr[ci], "cmr")):
                P.op("dve", lambda e, o=dstb.ap, i=scr.ap[:, 0:TS], sc=cols[cn].ap[:, ci:ci + 1],
                     x=xx.ap: e.scalar_tensor_tensor(o, i, sc, x, ALU.mult, ALU.add),
                     reads=[scr, cols[cn], xx], writes=[dstb])
        for ic in range(64):
            par = ic % 2
            wb = wck[par]
            P.dma("pool", wb.ap.rearrange("p (ci e) -> p ci e", ci=NCI),
                  wck_r[:, :, ic * 128:(ic + 1) * 128], writes=[wb])
            bG = ps.banks[par]
            P.mm_group([
                (lambda e, o=ps.bank(par, TS), l=wb.ap[:, ci * 128:(ci + 1) * 128],
                 r=ck[ci].ap, st=(ci == 0), sp=(ci == NCI - 1):
                 e.matmul(o, l, r, start=st, stop=sp)) for ci in range(NCI)],
                reads=[wb] + ck, out_buf=bG)
            hb = hgf[0]
            P.op("act", lambda e, o=hb.ap, i=ps.bank(par, TS): e.activation(o, i, AF.Relu),
                 reads=[bG], writes=[hb])
            P.op("dve", lambda e, o=h[ic].ap, a=hb.ap, b=hb.ap: e.tensor_mul(o, a, b),
                 reads=[hb], writes=[h[ic]])
        for hc in range(NCI):
            par = hc % 2
            wb = wcv[0]
            bK = ps.banks[2 + par]
            for gh2 in range(2):
                P.dma("pool", wb.ap.rearrange("p (g e) -> p g e", g=32),
                      wcv_r[:, gh2 * 32:(gh2 + 1) * 32, hc * 128:(hc + 1) * 128],
                      writes=[wb])
                P.mm_group([
                    (lambda e, o=ps.bank(2 + par, TS), l=wb.ap[:, i2 * 128:(i2 + 1) * 128],
                     r=h[gh2 * 32 + i2].ap, st=(gh2 == 0 and i2 == 0),
                     sp=(gh2 == 1 and i2 == 31):
                     e.matmul(o, l, r, start=st, stop=sp)) for i2 in range(32)],
                    reads=[wb] + h[gh2 * 32:(gh2 + 1) * 32], out_buf=bK)
            wb2 = wck[par]
            P.dma("pool", wb2.ap.rearrange("p (ci e) -> p ci e", ci=NCI),
                  wcr_r[:, :, hc * 128:(hc + 1) * 128], writes=[wb2])
            bR = ps.banks[4 + par]
            P.mm_group([
                (lambda e, o=ps.bank(4 + par, TS), l=wb2.ap[:, ci * 128:(ci + 1) * 128],
                 r=cr[ci].ap, st=(ci == 0), sp=(ci == NCI - 1):
                 e.matmul(o, l, r, start=st, stop=sp)) for ci in range(NCI)],
                reads=[wb2] + cr, out_buf=bR)
            sg = sgf[0]
            P.op("act", lambda e, o=sg.ap, i=ps.bank(4 + par, TS): e.activation(o, i, AF.Sigmoid),
                 reads=[bR], writes=[sg])
            xr_ = xrld[par]
            P.dma("sp", xr_.ap, xorig[hc * 128:(hc + 1) * 128, tsl], writes=[xr_], deps=[P.xdep])
            xo = xnew[par]
            P.op("dve", lambda e, o=xo.ap, a=ps.bank(2 + par, TS), b=sg.ap:
                 e.tensor_mul(o, a, b), reads=[bK, sg], writes=[xo])
            P.op("dve", lambda e, o=xo.ap, a=xo.ap, b=xm[hc].ap: e.tensor_add(o, a, b),
                 reads=[xm[hc]], writes=[xo])
            P.op("dve", lambda e, o=xo.ap, a=xo.ap, b=xr_.ap: e.tensor_add(o, a, b),
                 reads=[xr_], writes=[xo])
            P.dma("sp", xout[hc * 128:(hc + 1) * 128, tsl], xo.ap, reads=[xo], deps=[P.xdep])
    af32.pop(P.stage_floor()); abf.pop(P.stage_floor())


def prep_attn1(inputs, pre):
    f32 = np.float32
    for n, key in (("wkt", "Wk"), ("wvt", "Wv"), ("wrt", "Wr"), ("wot", "Wo"),
                   ("wckt", "Wck"), ("wcvt", "Wcv"), ("wcrt", "Wcr")):
        pre["a1_" + n] = np.ascontiguousarray(inputs["L1_attn_" + key].T).astype(f32)
    for n, key in (("ln1w", "ln1_w"), ("ln1b", "ln1_b"), ("ln2w", "ln2_w"),
                   ("ln2b", "ln2_b"), ("tmk", "tmk"), ("tmv", "tmv"), ("tmr", "tmr"),
                   ("cmk", "cmk"), ("cmr", "cmr"), ("decay", "time_decay"),
                   ("first", "time_first")):
        pre["a1_" + n] = np.ascontiguousarray(
            inputs["L1_attn_" + key].reshape(16, 128).T).astype(f32)


ATTN1_NAMES = ["a1_wkt", "a1_wvt", "a1_wrt", "a1_wot", "a1_wckt", "a1_wcvt", "a1_wcrt",
               "a1_ln1w", "a1_ln1b", "a1_ln2w", "a1_ln2b", "a1_tmk", "a1_tmv", "a1_tmr",
               "a1_cmk", "a1_cmr", "a1_decay", "a1_first"]
ATTN1A_NAMES = ["a1_wkt", "a1_wvt", "a1_wrt", "a1_wot", "a1_ln1w", "a1_ln1b",
                "a1_tmk", "a1_tmv", "a1_tmr", "a1_decay", "a1_first"]
ATTN1A_SCRATCH = [("a1_wkb", [16, 128, 16, 128]), ("a1_wvb", [16, 128, 16, 128]),
                  ("a1_wrb", [16, 128, 16, 128]), ("a1_wob", [16, 128, 16, 128])]
ATTN1B_NAMES = ["a1_wckt", "a1_wcvt", "a1_wcrt", "a1_ln2w", "a1_ln2b", "a1_cmk", "a1_cmr"]
ATTN1B_SCRATCH = [("a1_wckb", [64, 128, 16, 128]), ("a1_wcvb", [16, 128, 64, 128]),
                  ("a1_wcrb", [16, 128, 16, 128])]


_default_chain()



ALL_NAMES = sorted(set(FFN0_NAMES + ATTN0_NAMES + FFN1_NAMES + ATTN1A_NAMES + ATTN1B_NAMES))
ALL_SCRATCH = list(ATTN0_SCRATCH) + list(FFN1_SCRATCH) + list(ATTN1A_SCRATCH) + \
    list(ATTN1B_SCRATCH)
FUSED = True


def build_fused():
    nc = bass.Bass("TRN2", target_bir_lowering=False)
    xin_t = nc.dram_tensor("x_in", [C, T], F32, kind="ExternalInput")
    xout_t = nc.dram_tensor("x_out", [C, T], F32, kind="ExternalOutput")
    xa = nc.dram_tensor("xa", [C, T], F32, kind="Internal")
    xb = nc.dram_tensor("xb", [C, T], F32, kind="Internal")
    xc = nc.dram_tensor("xc", [C, T], F32, kind="Internal")
    P = Prog(nc)
    ps = Psum(nc)
    af32 = Arena(nc, "af32", 17750, F32)
    abf = Arena(nc, "abf", 70200, BF16)
    sh = Shared(P, af32, abf)
    d = declare_drams(nc, PRE_CACHE, ALL_NAMES)
    for sc_ in ALL_SCRATCH:
        n, shp = sc_[0], sc_[1]
        dt = F32 if (len(sc_) > 2 and sc_[2] == "f32") else BF16
        d[n] = nc.dram_tensor(n, list(shp), dt, kind="Internal")
    d["a1_xorig"] = xb
    chain = [
        (emit_ffn0, xin_t, xa),
        (emit_wconv_h1, None, None),
        (emit_attn0, xa, xb),
        (emit_wconv_h2, None, None),
        (emit_ffn0, xb, xa),
        (emit_ffn1, xa, xb),
        (emit_attn1a, xb, xc),
        (emit_attn1b, xc, xa),
        (emit_ffn1, xa, xout_t),
    ]
    for fn, xi, xo in chain:
        P.xdep = [(f"q_sp_{i}", P.cnt[f"q_sp_{i}"]) for i in range(P.QW)]
        fn(P, ps, sh, af32, abf, d,
           xi.ap() if xi is not None else None,
           xo.ap() if xo is not None else None)
    P.finalize([(s, P.cnt[s]) for s in P.cnt])
    return nc


def kernel(**inputs):
    pre = PRE_CACHE
    pre.clear()
    for prep in PREPS:
        prep(inputs, pre)
    x = inputs["L0_attn_x"]
    xf = [np.ascontiguousarray(x[b].T).astype(np.float32) for b in range(8)]
    if FUSED:
        nc = build_fused()
        in_maps = []
        for b in range(8):
            m = {n: pre[n] for n in ALL_NAMES}
            m["x_in"] = xf[b]
            in_maps.append(m)
        res = run_bass_kernel_spmd(nc, in_maps, core_ids=list(range(8)))
        xf = [np.ascontiguousarray(res.results[b]["x_out"]) for b in range(8)]
    else:
        hist = [xf]
        for spec in LAUNCHES:
            stage_fns, names, scratch = spec[0], spec[1], spec[2]
            sizes = spec[3] if len(spec) > 3 else (15000, 72000)
            x2_from = spec[4] if len(spec) > 4 else None
            nc = build_single_stage(stage_fns, names, scratch, sizes,
                                    x2=(x2_from is not None))
            in_maps = []
            for b in range(8):
                m = {n: pre[n] for n in names}
                m["x_in"] = hist[-1][b]
                if x2_from is not None:
                    m["x_in2"] = hist[x2_from][b]
                in_maps.append(m)
            res = run_bass_kernel_spmd(nc, in_maps, core_ids=list(range(8)))
            hist.append([np.ascontiguousarray(res.results[b]["x_out"]) for b in range(8)])
        xf = hist[-1]
    out = np.stack([xf[b].T for b in range(8)])
    return np.ascontiguousarray(out).astype(np.float32)


def emit_wconv_h1(P, ps, sh, af32, abf, d, xin, xout):
    emit_wconv(P, ps, sh, af32, abf, d, xin, xout,
               jobs=[("f1_gq", "f1_gbf", 2048, 8192)])


def emit_wconv_h2(P, ps, sh, af32, abf, d, xin, xout):
    emit_wconv(P, ps, sh, af32, abf, d, xin, xout,
               jobs=[("f1_uq", "f1_ubf", 2048, 8192), ("f1_dq", "f1_dbf", 8192, 2048)])


# version 09
# speedup vs baseline: 1.1254x; 1.1254x over previous
import contextlib
import numpy as np
import ml_dtypes
import concourse.bass as bass
import concourse.mybir as mybir
from concourse.bass_utils import run_bass_kernel_spmd

F32 = mybir.dt.float32
BF16 = mybir.dt.bfloat16
I32 = mybir.dt.int32
AF = mybir.ActivationFunctionType
ALU = mybir.AluOpType
AX = mybir.AxisListType

C = 2048
T = 2048
NCI = C // 128
TS = 512
NSL = T // TS


ENGINES = ("pe", "act", "dve", "pool", "sp")


class Buf:
    __slots__ = ("ap", "w", "rs")

    def __init__(self, ap, w=None):
        self.ap = ap
        self.w = w
        self.rs = []


class Prog:
    def __init__(self, nc):
        self.nc = nc
        self.ops = {e: [] for e in ENGINES}
        self.cnt = {}
        self.lastwait = {e: {} for e in ENGINES}
        self.sems = {}
        self._stack = contextlib.ExitStack()
        for s in ("pe", "act", "dve", "pool", "q_sp", "q_pool", "q_act"):
            self.sems[s] = self._stack.enter_context(nc.semaphore("sem_" + s))
            self.cnt[s] = 0
        self._drambufs = {}
        self.xdep = None
        self.QW = 12
        self.qn = {"sp": 0, "pool": 0, "act": 0}
        for q in ("sp", "pool", "act"):
            for i in range(self.QW):
                nme = f"q_{q}_{i}"
                self.sems[nme] = self._stack.enter_context(nc.semaphore("sem_" + nme))
                self.cnt[nme] = 0

    def rotate_queues(self, thresh=0):
        pass

    def stage_floor(self):
        return [(s, self.cnt[s]) for s in self.cnt]

    def drambuf(self, name):
        if name not in self._drambufs:
            self._drambufs[name] = Buf(None)
        return self._drambufs[name]

    _ESEM = {"pe": "pe", "act": "act", "dve": "dve", "pool": "pool"}
    _QENG = {"sp": "sp", "pool": "pool", "act": "act"}

    def _wait(self, eng, tok):
        if tok is None:
            return
        if isinstance(tok, list):
            for t in tok:
                self._wait(eng, t)
            return
        sem, val = tok
        if val <= 0 or self.lastwait[eng].get(sem, 0) >= val:
            return
        self.lastwait[eng][sem] = val
        h = self.sems[sem]
        self.ops[eng].append(lambda e, h=h, v=val: e.wait_ge(h, v))

    def op(self, eng, build, reads=(), writes=(), deps=(), inc=None):
        if inc is None:
            inc = bool(reads or writes)
        deps = list(deps)
        for b in reads:
            deps.append(b.w)
        for b in writes:
            deps.append(b.w)
            deps.extend(b.rs)
        for d in deps:
            self._wait(eng, d)
        tok = None
        if inc:
            s = self._ESEM[eng]
            self.cnt[s] += 1
            tok = (s, self.cnt[s])
            h = self.sems[s]
            self.ops[eng].append(lambda e, b=build, h=h: b(e).then_inc(h, 1))
        else:
            self.ops[eng].append(lambda e, b=build: b(e))
        for b in reads:
            if tok:
                b.rs.append(tok)
        for b in writes:
            b.w = tok
            b.rs = []
        return tok

    def dma(self, q, out, in_, reads=(), writes=(), deps=()):
        eng = self._QENG[q]
        s = f"q_{q}_{self.qn[q] % self.QW}"
        self.qn[q] += 1
        deps = list(deps)
        for b in reads:
            deps.append(b.w)
        for b in writes:
            deps.append(b.w)
            deps.extend(b.rs)
        deps.append((s, self.cnt[s]))
        for d in deps:
            self._wait(eng, d)
        self.cnt[s] += 16
        tok = (s, self.cnt[s])
        h = self.sems[s]
        self.ops[eng].append(
            lambda e, o=out, i=in_, h=h: e.dma_start(out=o, in_=i).then_inc(h, 16))
        for b in reads:
            b.rs.append(tok)
        for b in writes:
            b.w = tok
            b.rs = []
        return tok

    def mm_group(self, mms, reads=(), out_buf=None):
        deps = [b.w for b in reads]
        if out_buf is not None:
            deps.append(out_buf.w)
            deps.extend(out_buf.rs)
        for i, m in enumerate(mms):
            if i < len(mms) - 1:
                self.op("pe", m, deps=deps if i == 0 else (), inc=False)
            else:
                tok = self.op("pe", m, deps=deps if len(mms) == 1 else (), inc=True)
        for b in reads:
            b.rs.append(tok)
        if out_buf is not None:
            out_buf.w = tok
            out_buf.rs = []
        return tok

    def finalize(self, final_deps):
        for d in final_deps:
            self._wait("sp", d)
        nc, ops = self.nc, self.ops
        with nc.Block() as block:
            @block.sync
            def _(e):
                for f in ops["sp"]:
                    f(e)

            @block.tensor
            def _(e):
                for f in ops["pe"]:
                    f(e)

            @block.vector
            def _(e):
                for f in ops["dve"]:
                    f(e)

            @block.scalar
            def _(e):
                for f in ops["act"]:
                    f(e)

            @block.gpsimd
            def _(e):
                for f in ops["pool"]:
                    f(e)
        self._stack.close()


class Arena:
    def __init__(self, nc, name, width, dtype):
        self.t = nc.alloc_sbuf_tensor(name, [128, width], dtype)
        self.width = width
        self.off = 0
        self.marks = []
        self.floor = []

    def alloc(self, w):
        assert self.off + w <= self.width, f"arena {self.off}+{w}>{self.width}"
        ap = self.t[:, self.off:self.off + w]
        self.off += w
        self.hwm = max(getattr(self, "hwm", 0), self.off)
        return ap

    def buf(self, w):
        b = Buf(self.alloc(w))
        b.rs = list(self.floor)
        return b

    def bufs(self, n, w):
        return [self.buf(w) for _ in range(n)]

    def push(self):
        self.marks.append(self.off)

    def pop(self, floor=None):
        self.off = self.marks.pop()
        if floor is not None:
            self.floor = floor


class Psum:
    def __init__(self, nc):
        self._stack = contextlib.ExitStack()
        self.t = self._stack.enter_context(nc.psum_tensor("psum", [128, 4096], F32))
        self.banks = [Buf(self.t[:, i * 512:(i + 1) * 512]) for i in range(8)]

    def bank(self, i, w=512, off=0):
        assert off + w <= 512
        return self.t[:, i * 512 + off: i * 512 + off + w]




class Shared:

    def __init__(self, P, af32, abf):
        self.ones_col_bf = abf.buf(1)
        self.ones_row_bf = abf.buf(512)
        t0 = P.op("dve", lambda e, t=self.ones_col_bf.ap: e.memset(t, 1.0), inc=True)
        t1 = P.op("dve", lambda e, t=self.ones_row_bf.ap[0:1, :]: e.memset(t, 1.0), inc=True)
        self.ones_col_bf.w = t0
        self.ones_row_bf.w = t1
        self.ones_row_f32 = af32.buf(128)
        self.ones_row_f32.w = P.op("dve", lambda e, t=self.ones_row_f32.ap[0:1, :]:
                                   e.memset(t, 1.0), inc=True)
        self.eps = af32.buf(1)
        self.eps.w = P.op("dve", lambda e, t=self.eps.ap[0:1, :]: e.memset(t, 1e-5), inc=True)
        self.xb = abf.bufs(2, TS)
        self.x2b = abf.bufs(2, TS)
        self.rows = af32.buf(4 * TS)
        self.bcast = af32.buf(512 + TS)
        self.scr = {"dve": af32.buf(TS), "pool": af32.buf(TS)}


def emit_norm(P, ps, sh, xs, out, banks=(4, 5, 6), affine=None, ts=TS):
    bS, bS2, bB = banks
    sb, sb2, bb = ps.banks[bS], ps.banks[bS2], ps.banks[bB]
    sb.rs and None
    first_deps = [sb.w] + sb.rs + [sb2.w] + sb2.rs + [sh.ones_col_bf.w]
    for ci in range(NCI):
        k = ci % 2
        eng = "dve" if ci % 2 == 0 else "pool"
        P.op(eng, lambda e, o=sh.xb[k].ap, i=xs[ci].ap: e.tensor_copy(o, i),
             reads=[xs[ci]], writes=[sh.xb[k]])
        P.op("act", lambda e, o=sh.x2b[k].ap, i=xs[ci].ap: e.activation(o, i, AF.Square),
             reads=[xs[ci]], writes=[sh.x2b[k]])
        d0 = first_deps if ci == 0 else ()
        t1 = P.op("pe", lambda e, o=ps.bank(bS, ts), l=sh.ones_col_bf.ap, r=sh.xb[k].ap,
                  s=(ci == 0), p=(ci == NCI - 1): e.matmul(o[0:1, :], l, r, start=s, stop=p),
                  deps=list(d0) + [sh.xb[k].w], inc=True)
        t2 = P.op("pe", lambda e, o=ps.bank(bS2, ts), l=sh.ones_col_bf.ap, r=sh.x2b[k].ap,
                  s=(ci == 0), p=(ci == NCI - 1): e.matmul(o[0:1, :], l, r, start=s, stop=p),
                  deps=[sh.x2b[k].w], inc=True)
        sh.xb[k].rs.append(t1)
        sh.x2b[k].rs.append(t2)
    sb.w = t1; sb.rs = []
    sb2.w = t2; sb2.rs = []
    rows = sh.rows.ap
    nmu = rows[0:1, 0:ts]
    mu2 = rows[0:1, ts:2 * ts]
    var = rows[0:1, 2 * ts:3 * ts]
    rstd = rows[0:1, 3 * ts:4 * ts]
    P.op("dve", lambda e, o=nmu, i=ps.bank(bS, ts)[0:1, :]: e.tensor_scalar_mul(o, i, -1.0 / C),
         reads=[sb], writes=[sh.rows])
    P.op("dve", lambda e, o=mu2, a=nmu: e.tensor_mul(o, a, a), writes=[sh.rows])
    P.op("dve", lambda e, o=var, i=ps.bank(bS2, ts)[0:1, :], m=mu2:
         e.scalar_tensor_tensor(o, i, 1.0 / C, m, ALU.mult, ALU.subtract),
         reads=[sb2], writes=[sh.rows])
    P.op("act", lambda e, o=var, i=var, b=sh.eps.ap[0:1, 0:1]:
         e.activation(o, i, AF.Sqrt, bias=b), reads=[sh.rows, sh.eps], writes=[sh.rows])
    P.op("dve", lambda e, o=rstd, i=var: e.reciprocal(o, i), writes=[sh.rows])
    P.op("dve", lambda e, o=nmu, a=nmu, b=rstd: e.tensor_mul(o, a, b), writes=[sh.rows])
    bb2 = ps.banks[bB + 1]
    tk = P.mm_group(
        [lambda e, o=ps.bank(bB, ts), l=sh.ones_row_f32.ap[0:1, 0:128], r=rstd:
         e.matmul(o, l, r, start=True, stop=True),
         lambda e, o=ps.bank(bB + 1, ts), l=sh.ones_row_f32.ap[0:1, 0:128],
         r=nmu: e.matmul(o, l, r, start=True, stop=True)],
        reads=[sh.rows, sh.ones_row_f32], out_buf=bb)
    bb2.w = tk; bb2.rs = []
    P.op("act", lambda e, o=sh.bcast.ap[:, 0:2 * ts], i=ps.t[:, bB * 512:bB * 512 + 512 + ts]:
         e.copy(o, i), reads=[bb, bb2], writes=[sh.bcast])
    rb = sh.bcast.ap
    for ci in range(NCI):
        eng = "dve" if ci % 2 == 0 else "pool"
        scr = sh.scr[eng]
        P.op(eng, lambda e, o=scr.ap[:, 0:ts], a=xs[ci].ap, b=rb[:, 0:ts]: e.tensor_mul(o, a, b),
             reads=[xs[ci], sh.bcast], writes=[scr])
        if affine is None:
            P.op(eng, lambda e, o=out[ci].ap, a=scr.ap[:, 0:ts], b=rb[:, 512:512 + ts]:
                 e.tensor_add(o, a, b), reads=[scr, sh.bcast], writes=[out[ci]])
        else:
            wcol, bcol = affine
            P.op(eng, lambda e, o=scr.ap[:, 0:ts], a=scr.ap[:, 0:ts], b=rb[:, 512:512 + ts]:
                 e.tensor_add(o, a, b), reads=[sh.bcast], writes=[scr])
            P.op("dve", lambda e, o=out[ci].ap, i=scr.ap[:, 0:ts],
                 w=wcol.ap[:, ci:ci + 1], b=bcol.ap[:, ci:ci + 1]:
                 e.tensor_scalar(o, i, w, b, ALU.mult, ALU.add),
                 reads=[scr, wcol, bcol], writes=[out[ci]])




def load_const_bf16(P, abf, src_ap, w, chunks=None):
    b = abf.buf(w)
    out = b.ap if chunks is None else b.ap.rearrange("p (g n) -> p g n", g=chunks[0])
    P.dma("pool", out, src_ap, writes=[b])
    return b


def emit_ffn0(P, ps, sh, af32, abf, d, xin, xout):
    af32.push(); abf.push()
    rw1b = load_const_bf16(P, abf, d["f0_rw1t"].rearrange("(ci p) h -> p ci h", p=128),
                           NCI * 64, chunks=(NCI, 64))
    atb = load_const_bf16(P, abf, d["f0_at"].rearrange("(ci p) r -> p ci r", p=128),
                          NCI * 256, chunks=(NCI, 256))
    breb = load_const_bf16(P, abf, d["f0_bre"].rearrange("(g p) e -> p g e", p=128),
                           2 * C, chunks=(2, C))
    identb = load_const_bf16(P, abf, d["ident"].ap(), 128)
    eexpb = abf.buf(256)
    P.dma("pool", eexpb.ap[0:16, :], d["f0_eexp"].ap(), writes=[eexpb])
    biasb = abf.buf(C)
    P.dma("pool", biasb.ap[0:1, :], d["f0_bias"].ap(), writes=[biasb])
    rb1c = af32.buf(1)
    P.dma("sp", rb1c.ap[0:64, 0:1], d["f0_rb1c"].ap(), writes=[rb1c])
    rw2x = abf.buf(32)
    P.dma("pool", rw2x.ap[0:64, :], d["f0_rw2t"].ap(), writes=[rw2x])
    r2row = af32.buf(32)
    grow = af32.buf(16)
    P.dma("sp", r2row.ap[0:1, :], d["f0_rb2"].ap(), writes=[r2row])
    P.dma("sp", grow.ap[0:1, :], d["f0_gates"].ap(), writes=[grow])
    P.op("dve", lambda e, o=r2row.ap[0:1, 0:16], a=r2row.ap[0:1, 0:16], b=grow.ap[0:1, :]:
         e.tensor_add(o, a, b), reads=[grow], writes=[r2row])
    P.op("dve", lambda e, o=rw2x.ap[64:65, :], i=r2row.ap[0:1, :]: e.tensor_copy(o, i),
         reads=[r2row], writes=[rw2x])

    xs = af32.bufs(NCI, TS)
    xn = [abf.bufs(NCI, TS) for _ in range(NSL)]
    hx = abf.buf(TS)
    P.op("dve", lambda e, o=hx.ap[64:65, :]: e.memset(o, 1.0), writes=[hx])
    mt = af32.buf(8)
    mask1 = af32.buf(16)
    x2t = af32.buf(16)
    coeff = abf.buf(64)
    ctb = abf.buf(TS)
    cexp = abf.buf(TS)
    wxa = [abf.buf(T) for _ in range(2)]
    wbuf = [abf.buf(NCI * 128) for _ in range(2)]
    xnew = af32.bufs(2, TS)
    wt_r = d["f0_wt"].rearrange("(ci p) e -> p ci e", p=128)

    for s in range(NSL):
        tsl = slice(s * TS, (s + 1) * TS)
        for ci in range(NCI):
            P.dma("sp", xs[ci].ap, xin[ci * 128:(ci + 1) * 128, tsl], writes=[xs[ci]], deps=[P.xdep])
        emit_norm(P, ps, sh, xs, xn[s])
        bH = ps.banks[2]
        P.mm_group([
            (lambda e, o=ps.bank(2, TS), l=rw1b.ap[:, ci * 64:(ci + 1) * 64], r=xn[s][ci].ap,
             st=(ci == 0), sp=(ci == NCI - 1):
             e.matmul(o[0:64, :], l, r, start=st, stop=sp)) for ci in range(NCI)],
            reads=[rw1b] + xn[s], out_buf=bH)
        P.op("act", lambda e, o=hx.ap[0:64, :], i=ps.bank(2, TS)[0:64, :], b=rb1c.ap[0:64, 0:1]:
             e.activation(o, i, AF.Silu, bias=b), reads=[bH, rb1c], writes=[hx])
        for tc in range(TS // 128):
            P.mm_group([lambda e, o=ps.bank(2, 32, off=tc * 32),
                        l=hx.ap[0:65, tc * 128:(tc + 1) * 128], r=rw2x.ap[0:65, :]:
                        e.matmul(o, l, r, start=True, stop=True)],
                       reads=[hx, rw2x], out_buf=bH)
        for tc in range(TS // 128):
            lg = ps.bank(2, 16, off=tc * 32)
            m1 = mt.ap[:, 0:1]; m2 = mt.ap[:, 1:2]; dd = mt.ap[:, 2:3]
            w1 = mt.ap[:, 3:4]; w2 = mt.ap[:, 4:5]
            P.op("dve", lambda e, o=m1, i=lg: e.reduce_max(o, i, axis=AX.X),
                 reads=[bH], writes=[mt])
            P.op("dve", lambda e, o=mask1.ap, i=lg, s_=m1:
                 e.tensor_single_scalar(o, i, s_, ALU.is_ge), reads=[bH, mt], writes=[mask1])
            P.op("dve", lambda e, o=x2t.ap, a=mask1.ap, i=lg:
                 e.scalar_tensor_tensor(o, a, -1e30, i, ALU.mult, ALU.add),
                 reads=[bH, mask1], writes=[x2t])
            P.op("dve", lambda e, o=m2, i=x2t.ap: e.reduce_max(o, i, axis=AX.X),
                 reads=[x2t], writes=[mt])
            P.op("dve", lambda e, o=dd, a=m1, b=m2: e.tensor_sub(o, a, b), writes=[mt])
            P.op("act", lambda e, o=w1, i=dd: e.activation(o, i, AF.Sigmoid),
                 reads=[mt], writes=[mt])
            P.op("dve", lambda e, o=w2, i=w1: e.tensor_scalar(o, i, -1.0, 1.0, ALU.mult, ALU.add),
                 reads=[mt], writes=[mt])
            P.op("dve", lambda e, o=x2t.ap, i=x2t.ap, s_=m2:
                 e.tensor_single_scalar(o, i, s_, ALU.is_ge), reads=[mt], writes=[x2t])
            P.op("dve", lambda e, o=mask1.ap, i=mask1.ap, s_=w1:
                 e.tensor_single_scalar(o, i, s_, ALU.mult), reads=[mt], writes=[mask1])
            P.op("dve", lambda e, o=coeff.ap[:, tc * 16:(tc + 1) * 16], i=x2t.ap, s_=w2,
                 a=mask1.ap: e.scalar_tensor_tensor(o, i, s_, a, ALU.mult, ALU.add),
                 reads=[mt, x2t, mask1], writes=[coeff])
        b3 = ps.banks[3]
        P.mm_group([
            (lambda e, o=ps.bank(3, 128, off=tc * 128), l=coeff.ap[:, tc * 16:(tc + 1) * 16],
             r=identb.ap: e.matmul(o[0:16, :], l, r, start=True, stop=True))
            for tc in range(TS // 128)], reads=[coeff, identb], out_buf=b3)
        P.op("dve", lambda e, o=ctb.ap[0:16, :], i=ps.bank(3, TS)[0:16, :]: e.tensor_copy(o, i),
             reads=[b3], writes=[ctb])
        b4 = ps.banks[4]
        for er in range(2):
            P.mm_group([
                (lambda e, o=ps.bank(5 + er, TS),
                 l=atb.ap[:, ci * 256 + er * 128: ci * 256 + (er + 1) * 128], r=xn[s][ci].ap,
                 st=(ci == 0), sp=(ci == NCI - 1): e.matmul(o, l, r, start=st, stop=sp))
                for ci in range(NCI)], reads=[atb] + xn[s], out_buf=ps.banks[5 + er])
            P.mm_group([lambda e, o=ps.bank(4, TS),
                        l=eexpb.ap[0:16, er * 128:(er + 1) * 128], r=ctb.ap[0:16, :]:
                        e.matmul(o, l, r, start=True, stop=True)],
                       reads=[eexpb, ctb], out_buf=b4)
            P.op("dve", lambda e, o=cexp.ap, i=ps.bank(4, TS): e.tensor_copy(o, i),
                 reads=[b4], writes=[cexp])
            P.op("dve", lambda e, o=wxa[er].ap[:, s * TS:(s + 1) * TS], a=ps.bank(5 + er, TS),
                 b=cexp.ap: e.tensor_mul(o, a, b),
                 reads=[ps.banks[5 + er], cexp], writes=[wxa[er]])

    for dc in range(NCI):
        par = dc % 2
        wb = wbuf[par]
        P.dma("pool", wb.ap.rearrange("p (ci e) -> p ci e", ci=NCI),
              wt_r[:, :, dc * 128:(dc + 1) * 128], writes=[wb])
        for s in range(NSL):
            tsl = slice(s * TS, (s + 1) * TS)
            xr = xs[dc % len(xs)]
            P.dma("sp", xr.ap, xin[dc * 128:(dc + 1) * 128, tsl], writes=[xr], deps=[P.xdep])
            bD = ps.banks[par]
            mms = [
                (lambda e, o=ps.bank(par, TS), l=wb.ap[:, ci * 128:(ci + 1) * 128],
                 r=xn[s][ci].ap, st=(ci == 0): e.matmul(o, l, r, start=st, stop=False))
                for ci in range(NCI)]
            mms.append(lambda e, o=ps.bank(par, TS), l=biasb.ap[0:1, dc * 128:(dc + 1) * 128],
                       r=sh.ones_row_bf.ap[0:1, 0:TS]: e.matmul(o, l, r, start=False, stop=False))
            for er in range(2):
                mms.append(lambda e, o=ps.bank(par, TS),
                           l=breb.ap[:, er * C + dc * 128: er * C + (dc + 1) * 128],
                           r=wxa[er].ap[:, s * TS:(s + 1) * TS], sp=(er == 1):
                           e.matmul(o, l, r, start=False, stop=sp))
            P.mm_group(mms, reads=[wb, biasb, sh.ones_row_bf, breb, wxa[0], wxa[1]] + xn[s],
                       out_buf=bD)
            xo = xnew[par]
            P.op("dve", lambda e, o=xo.ap, i=ps.bank(par, TS), x=xr.ap:
                 e.scalar_tensor_tensor(o, i, 0.5, x, ALU.mult, ALU.add),
                 reads=[bD, xr], writes=[xo])
            P.dma("sp", xout[dc * 128:(dc + 1) * 128, tsl], xo.ap, reads=[xo], deps=[P.xdep])
    af32.pop(P.stage_floor()); abf.pop(P.stage_floor())


def prep_ffn0(inputs, pre):
    f32 = np.float32
    pre["f0_wt"] = np.ascontiguousarray(inputs["L0_ffn_W"].T).astype(f32)
    pre["f0_bias"] = inputs["L0_ffn_bias"].reshape(1, C).astype(f32)
    pre["f0_rw1t"] = np.ascontiguousarray(inputs["L0_ffn_rw1"].T).astype(f32)
    pre["f0_rb1c"] = inputs["L0_ffn_rb1"].reshape(64, 1).astype(f32)
    pre["f0_rw2t"] = np.ascontiguousarray(inputs["L0_ffn_rw2"].T).astype(f32)
    pre["f0_rb2"] = inputs["L0_ffn_rb2"].reshape(1, 32).astype(f32)
    pre["f0_gates"] = inputs["L0_ffn_gates"].reshape(1, 16).astype(f32)
    pre["f0_at"] = np.ascontiguousarray(inputs["L0_ffn_A"].reshape(256, C).T).astype(f32)
    pre["f0_bre"] = np.ascontiguousarray(
        inputs["L0_ffn_B"].transpose(0, 2, 1).reshape(256, C)).astype(f32)
    ident = np.eye(128, dtype=f32)
    pre["ident"] = ident
    eexp = np.zeros((16, 256), dtype=f32)
    for e_ in range(16):
        eexp[e_, e_ * 16:(e_ + 1) * 16] = 1.0
    pre["f0_eexp"] = eexp


FFN0_NAMES = ["f0_wt", "f0_bias", "f0_rw1t", "f0_rb1c", "f0_rw2t", "f0_rb2", "f0_gates",
              "f0_at", "f0_bre", "ident", "f0_eexp"]


def declare_drams(nc, pre, names):
    d = {}
    for n in names:
        a = pre[n]
        dt = {np.dtype(np.float32): F32, np.dtype(np.int32): I32}[a.dtype]
        d[n] = nc.dram_tensor(n, list(a.shape), dt, kind="ExternalInput")
    return d


def build_single_stage(stage_fns, names, scratch=(), sizes=(15000, 72000), x2=False):
    nc = bass.Bass("TRN2", target_bir_lowering=False)
    xin_t = nc.dram_tensor("x_in", [C, T], F32, kind="ExternalInput")
    xout_t = nc.dram_tensor("x_out", [C, T], F32, kind="ExternalOutput")
    P = Prog(nc)
    ps = Psum(nc)
    af32 = Arena(nc, "af32", sizes[0], F32)
    abf = Arena(nc, "abf", sizes[1], BF16)
    sh = Shared(P, af32, abf)
    d = declare_drams(nc, PRE_CACHE, names)
    if x2:
        d["a1_xorig"] = nc.dram_tensor("x_in2", [C, T], F32, kind="ExternalInput")
    for sc_ in scratch:
        n, shp = sc_[0], sc_[1]
        dt = F32 if (len(sc_) > 2 and sc_[2] == "f32") else BF16
        d[n] = nc.dram_tensor(n, list(shp), dt, kind="Internal")
    for fn in stage_fns:
        fn(P, ps, sh, af32, abf, d, xin_t.ap(), xout_t.ap())
    P.finalize([(s, P.cnt[s]) for s in P.cnt])
    return nc


PRE_CACHE = {}


def kernel(**inputs):
    pre = PRE_CACHE
    pre.clear()
    for prep in PREPS:
        prep(inputs, pre)
    x = inputs["L0_attn_x"]
    xf = [np.ascontiguousarray(x[b].T).astype(np.float32) for b in range(8)]
    hist = [xf]
    for spec in LAUNCHES:
        stage_fns, names, scratch = spec[0], spec[1], spec[2]
        sizes = spec[3] if len(spec) > 3 else (15000, 72000)
        x2_from = spec[4] if len(spec) > 4 else None
        nc = build_single_stage(stage_fns, names, scratch, sizes, x2=(x2_from is not None))
        in_maps = []
        for b in range(8):
            m = {n: pre[n] for n in names}
            m["x_in"] = hist[-1][b]
            if x2_from is not None:
                m["x_in2"] = hist[x2_from][b]
            in_maps.append(m)
        res = run_bass_kernel_spmd(nc, in_maps, core_ids=list(range(8)))
        hist.append([np.ascontiguousarray(res.results[b]["x_out"]) for b in range(8)])
    xf = hist[-1]
    out = np.stack([xf[b].T for b in range(8)])
    return np.ascontiguousarray(out).astype(np.float32)


PREPS = [prep_ffn0]
LAUNCHES = [([emit_ffn0], FFN0_NAMES, ())]


def _default_chain():
    global PREPS, LAUNCHES
    PREPS = [prep_ffn0, prep_attn0, prep_ffn1, prep_attn1]
    LAUNCHES = [
        ([emit_ffn0], FFN0_NAMES, (), (15000, 72000)),
        ([emit_attn0], ATTN0_NAMES, ATTN0_SCRATCH, (18000, 70000)),
        ([emit_ffn0], FFN0_NAMES, (), (15000, 72000)),
        ([emit_wconv, emit_ffn1], FFN1_NAMES, FFN1_SCRATCH, (15000, 72000)),
        ([emit_wconv_a1a, emit_attn1a], ATTN1A_NAMES, ATTN1A_SCRATCH, (19500, 64000)),
        ([emit_wconv_a1b, emit_attn1b], ATTN1B_NAMES, ATTN1B_SCRATCH, (17000, 70500), 4),
        ([emit_wconv, emit_ffn1], FFN1_NAMES, FFN1_SCRATCH, (15000, 72000)),
    ]




def emit_wconv(P, ps, sh, af32, abf, d, xin, xout, jobs=None):
    af32.push(); abf.push()
    bin_ = abf.bufs(2, 8192)
    bout = abf.bufs(2, 8192)
    if jobs is None:
        jobs = [("f1_gq", "f1_gbf", 2048, 8192), ("f1_uq", "f1_ubf", 2048, 8192),
                ("f1_dq", "f1_dbf", 8192, 2048)]
    k = 0
    for srcn, dst, n, m in jobs:
        for r in range(n // 128):
            bi, bo = bin_[k % 2], bout[k % 2]
            k += 1
            P.dma("pool", bi.ap[:, 0:m], d[srcn][r * 128:(r + 1) * 128, :], writes=[bi])
            eng = "dve" if k % 2 == 0 else "pool"
            P.op(eng, lambda e, o=bo.ap[:, 0:m], t=bi.ap[:, 0:m]:
                 e.tensor_scalar_add(o, t, -128.0), reads=[bi], writes=[bo])
            P.dma("act", d[dst][:, :, r, :].rearrange("oc p e -> p oc e"),
                  bo.ap[:, 0:m].rearrange("p (oc e) -> p oc e", e=128),
                  reads=[bo], writes=[P.drambuf(dst)])
    af32.pop(P.stage_floor()); abf.pop(P.stage_floor())


def emit_ffn1(P, ps, sh, af32, abf, d, xin, xout):
    af32.push(); abf.push()
    gsc = af32.buf(64)
    usc = af32.buf(64)
    dsc = af32.buf(16)
    P.dma("sp", gsc.ap, d["f1_gsc"].ap(), writes=[gsc])
    P.dma("sp", usc.ap, d["f1_usc"].ap(), writes=[usc])
    P.dma("sp", dsc.ap, d["f1_dsc"].ap(), writes=[dsc])
    P.op("dve", lambda e, t=dsc.ap: e.tensor_scalar_mul(t, t, 0.5), reads=[dsc], writes=[dsc])

    xs = af32.bufs(NCI, TS)
    xn = abf.bufs(NCI, TS)
    h = abf.bufs(64, TS)
    wg = [abf.buf(NCI * 128) for _ in range(2)]
    wu = [abf.buf(NCI * 128) for _ in range(2)]
    wd = [abf.buf(64 * 128) for _ in range(2)]
    hg = af32.bufs(2, TS)
    xnew = af32.bufs(2, TS)

    gq = d["f1_gbf"]
    uq = d["f1_ubf"]
    dq = d["f1_dbf"]

    for s in range(NSL):
        tsl = slice(s * TS, (s + 1) * TS)
        for ci in range(NCI):
            P.dma("sp", xs[ci].ap, xin[ci * 128:(ci + 1) * 128, tsl], writes=[xs[ci]], deps=[P.xdep])
        emit_norm(P, ps, sh, xs, xn)
        for ic in range(64):
            par = ic % 2
            wgb, wub = wg[par], wu[par]
            P.dma("sp", wgb.ap.rearrange("p (hc e) -> p hc e", hc=NCI),
                  gq[ic], reads=[P.drambuf("f1_gbf")], writes=[wgb])
            P.dma("sp", wub.ap.rearrange("p (hc e) -> p hc e", hc=NCI),
                  uq[ic], reads=[P.drambuf("f1_ubf")], writes=[wub])
            bG, bU = ps.banks[par], ps.banks[2 + par]
            P.mm_group([
                (lambda e, o=ps.bank(par, TS), l=wgb.ap[:, hc * 128:(hc + 1) * 128],
                 r=xn[hc].ap, st=(hc == 0), sp=(hc == NCI - 1):
                 e.matmul(o, l, r, start=st, stop=sp)) for hc in range(NCI)],
                reads=[wgb] + xn, out_buf=bG)
            P.mm_group([
                (lambda e, o=ps.bank(2 + par, TS), l=wub.ap[:, hc * 128:(hc + 1) * 128],
                 r=xn[hc].ap, st=(hc == 0), sp=(hc == NCI - 1):
                 e.matmul(o, l, r, start=st, stop=sp)) for hc in range(NCI)],
                reads=[wub] + xn, out_buf=bU)
            hgb = hg[par]
            P.op("act", lambda e, o=hgb.ap, i=ps.bank(par, TS), sc=gsc.ap[:, ic:ic + 1]:
                 e.activation(o, i, AF.Silu, scale=sc), reads=[bG, gsc], writes=[hgb])
            P.op("dve", lambda e, o=h[ic].ap, u=ps.bank(2 + par, TS), sc=usc.ap[:, ic:ic + 1],
                 g=hgb.ap: e.scalar_tensor_tensor(o, u, sc, g, ALU.mult, ALU.mult),
                 reads=[bU, usc, hgb], writes=[h[ic]])
        for hc in range(NCI):
            par = hc % 2
            wdb = wd[par]
            P.dma("sp", wdb.ap.rearrange("p (g e) -> p g e", g=64),
                  dq[hc], reads=[P.drambuf("f1_dbf")], writes=[wdb])
            bD = ps.banks[4 + par]
            P.mm_group([
                (lambda e, o=ps.bank(4 + par, TS), l=wdb.ap[:, ic * 128:(ic + 1) * 128],
                 r=h[ic].ap, st=(ic == 0), sp=(ic == 63):
                 e.matmul(o, l, r, start=st, stop=sp)) for ic in range(64)],
                reads=[wdb] + h, out_buf=bD)
            xo = xnew[par]
            P.op("dve", lambda e, o=xo.ap, i=ps.bank(4 + par, TS), sc=dsc.ap[:, hc:hc + 1],
                 x=xs[hc].ap: e.scalar_tensor_tensor(o, i, sc, x, ALU.mult, ALU.add),
                 reads=[bD, dsc, xs[hc]], writes=[xo])
            P.dma("sp", xout[hc * 128:(hc + 1) * 128, tsl], xo.ap, reads=[xo], deps=[P.xdep])
    af32.pop(P.stage_floor()); abf.pop(P.stage_floor())


def prep_ffn1(inputs, pre):
    f32 = np.float32
    pre["f1_gq"] = np.ascontiguousarray(inputs["L1_ffn_gate_w"].T).astype(np.int32)
    pre["f1_uq"] = np.ascontiguousarray(inputs["L1_ffn_up_w"].T).astype(np.int32)
    pre["f1_dq"] = np.ascontiguousarray(inputs["L1_ffn_down_w"].T).astype(np.int32)
    pre["f1_gsc"] = np.ascontiguousarray(inputs["L1_ffn_gate_scale"].reshape(64, 128).T).astype(f32)
    pre["f1_usc"] = np.ascontiguousarray(inputs["L1_ffn_up_scale"].reshape(64, 128).T).astype(f32)
    pre["f1_dsc"] = np.ascontiguousarray(inputs["L1_ffn_down_scale"].reshape(16, 128).T).astype(f32)


FFN1_NAMES = ["f1_gq", "f1_uq", "f1_dq", "f1_gsc", "f1_usc", "f1_dsc"]
FFN1_SCRATCH = [("f1_gbf", [64, 128, 16, 128]), ("f1_ubf", [64, 128, 16, 128]),
                ("f1_dbf", [16, 128, 64, 128])]



EH = [h for h in range(32) if (h // 4) % 2 == 0]
OH = [h for h in range(32) if (h // 4) % 2 == 1]
QC = 512


def emit_attn0(P, ps, sh, af32, abf, d, xin, xout):
    af32.push(); abf.push()
    kt4 = abf.bufs(4, T)
    vt = abf.bufs(NCI, 8 * 65)
    xs = af32.bufs(NCI, TS)
    xn = abf.bufs(NCI, TS)
    wq = [abf.buf(NCI * 128) for _ in range(2)]
    wvb = abf.bufs(2, 512)
    t1k = af32.buf(TS); t2k = af32.buf(TS)
    qf = af32.bufs(2, TS)
    qsw = af32.bufs(2, TS)
    qob = abf.bufs(2, TS)
    qrow = af32.buf(40)
    brow = af32.buf(34)
    bcol = af32.buf(32)
    bcsb = af32.buf(QC)
    rrow = af32.buf(QC)
    xnew = af32.bufs(2, TS)
    bqrow = abf.buf(C)
    bkrow = abf.buf(512)
    bvrow = abf.buf(512)
    borow = abf.buf(C)
    qs = abf.bufs(16, QC)
    pta = abf.buf(8 * QC)
    ptb = abf.buf(8 * QC)
    att = abf.bufs(NCI, QC)
    wo = [abf.buf(NCI * 128) for _ in range(2)]
    abscr = qob[0]
    P.dma("pool", bqrow.ap[0:1, :], d["a0_bq"].ap(), writes=[bqrow])
    P.dma("pool", bkrow.ap[0:1, :], d["a0_bk"].ap(), writes=[bkrow])
    P.dma("pool", bvrow.ap[0:1, :], d["a0_bv"].ap(), writes=[bvrow])
    P.dma("pool", borow.ap[0:1, :], d["a0_bo"].ap(), writes=[borow])
    wv_r = d["a0_wvt"].rearrange("(ci p) e -> p ci e", p=128)
    for ktile in range(NCI):
        for hh in range(8):
            P.op("dve", lambda e, o=vt[ktile].ap[:, hh * 65 + 64: hh * 65 + 65]:
                 e.memset(o, 1.0), writes=[vt[ktile]])

    wq_r = d["a0_wqt"].rearrange("(ci p) e -> p ci e", p=128)
    wk_r = d["a0_wkt"].rearrange("(ci p) e -> p ci e", p=128)
    wo_r = d["a0_wot"].rearrange("(ci p) e -> p ci e", p=128)
    qscr = d["a0_qscr"]

    qm = qrow.ap[0:1, 0:32]
    km = qrow.ap[0:1, 32:40]
    P.op("dve", lambda e, o=qrow.ap[0:1, :]: e.memset(o, 0.0), writes=[qrow])

    for s in range(NSL):
        tsl = slice(s * TS, (s + 1) * TS)
        for ci in range(NCI):
            P.dma("sp", xs[ci].ap, xin[ci * 128:(ci + 1) * 128, tsl], writes=[xs[ci]], deps=[P.xdep])
        emit_norm(P, ps, sh, xs, xn)
        for tb, srcn in ((t1k, "a0_t1k"), (t2k, "a0_t2k")):
            P.dma("sp", tb.ap, d[srcn][:, tsl], writes=[tb])
        for j in range(20):
            isk = j >= 16
            m = j - 16 if isk else j
            wb = wq[j % 2]
            P.dma("pool", wb.ap.rearrange("p (ci e) -> p ci e", ci=NCI),
                  (wk_r if isk else wq_r)[:, :, m * 128:(m + 1) * 128], writes=[wb])
            bank = ps.banks[j % 2]
            mms = [
                (lambda e, o=ps.bank(j % 2, TS), l=wb.ap[:, ci * 128:(ci + 1) * 128],
                 r=xn[ci].ap, st=(ci == 0): e.matmul(o, l, r, start=st, stop=False))
                for ci in range(NCI)]
            br = bkrow if isk else bqrow
            mms.append(lambda e, o=ps.bank(j % 2, TS), l=br.ap[0:1, m * 128:(m + 1) * 128],
                       r=sh.ones_row_bf.ap[0:1, 0:TS]: e.matmul(o, l, r, start=False, stop=True))
            P.mm_group(mms, reads=[wb, br, sh.ones_row_bf] + xn, out_buf=bank)
            qfb, qsb = qf[j % 2], qsw[j % 2]
            P.op("act", lambda e, o=qfb.ap, i=ps.bank(j % 2, TS): e.copy(o, i),
                 reads=[bank], writes=[qfb])
            for blk in range(4):
                sw = (blk ^ 1) * 32
                P.dma("sp", qsb.ap[sw:sw + 32, :], qfb.ap[blk * 32:(blk + 1) * 32, :],
                      reads=[qfb], writes=[qsb])
            T1, T2 = t1k, t2k
            dst = kt4[m] if isk else qob[j % 2]
            dsl = tsl if isk else slice(0, TS)
            eng = "dve" if j % 2 == 0 else "pool"
            scr = sh.scr[eng]
            P.op(eng, lambda e, o=scr.ap[:, 0:TS], a=qfb.ap, b=T1.ap: e.tensor_mul(o, a, b),
                 reads=[qfb, T1], writes=[scr])
            P.op(eng, lambda e, o=qsb.ap, a=qsb.ap, b=T2.ap: e.tensor_mul(o, a, b),
                 reads=[T2, qsb], writes=[qsb])
            P.op(eng, lambda e, o=dst.ap[:, dsl], a=scr.ap[:, 0:TS], b=qsb.ap:
                 e.tensor_add(o, a, b), reads=[scr, qsb], writes=[dst])
            if not isk:
                P.dma("sp", qscr[m * 128:(m + 1) * 128, tsl], dst.ap,
                      reads=[dst], writes=[P.drambuf("a0_qscr")])
            q2 = sh.x2b[j % 2]
            P.op("act", lambda e, o=q2.ap, i=dst.ap[:, dsl]: e.activation(o, i, AF.Square),
                 reads=[dst], writes=[q2])
            for half in range(2):
                bnk = ps.banks[4 + half]
                tk = P.op("pe", lambda e, o=ps.bank(4 + half, TS),
                          l=sh.ones_col_bf.ap[half * 64:(half + 1) * 64, :],
                          r=q2.ap[half * 64:(half + 1) * 64, :]:
                          e.matmul(o[0:1, :], l, r, start=True, stop=True),
                          deps=[q2.w, bnk.w] + bnk.rs, inc=True)
                bnk.w = tk; bnk.rs = []
                q2.rs.append(tk)
                if isk:
                    tgt = km[0:1, 2 * m + half:2 * m + half + 1]
                else:
                    tgt = qm[0:1, (EH if half == 0 else OH)[m]:(EH if half == 0 else OH)[m] + 1]
                P.op("dve", lambda e, o=sh.rows.ap[0:1, 0:1], i=ps.bank(4 + half, TS)[0:1, :]:
                     e.reduce_max(o, i, axis=AX.X), reads=[bnk], writes=[sh.rows])
                P.op("dve", lambda e, o=tgt, a=tgt, b=sh.rows.ap[0:1, 0:1]:
                     e.tensor_max(o, a, b), reads=[sh.rows], writes=[qrow])
        for ci in range(NCI):
            wvc = wvb[ci % 2]
            P.dma("pool", wvc.ap, wv_r[:, ci, :], writes=[wvc])
            for v4 in range(4):
                bnk = ps.banks[v4]
                tk = P.op("pe", lambda e, o=ps.bank(v4, 512),
                          l=xn[ci].ap[:, v4 * 128:(v4 + 1) * 128], r=wvc.ap,
                          st=(ci == 0): e.matmul(o, l, r, start=st, stop=False),
                          deps=([bnk.w] + bnk.rs if ci == 0 else []) + [wvc.w, xn[ci].w],
                          inc=(v4 == 3))
                if ci == 0:
                    bnk.rs = []
            wvc.rs.append(("pe", P.cnt["pe"]))
        for v4 in range(4):
            ktile = s * 4 + v4
            bnk = ps.banks[v4]
            tk = P.op("pe", lambda e, o=ps.bank(v4, 512), l=sh.ones_row_bf.ap[0:1, 0:128],
                      r=bvrow.ap[0:1, :]: e.matmul(o, l, r, start=False, stop=True),
                      deps=[bvrow.w, sh.ones_row_bf.w], inc=True)
            bnk.w = tk
            for hh in range(8):
                P.op("dve", lambda e, o=vt[ktile].ap[:, hh * 65:hh * 65 + 64],
                     i=ps.bank(v4, 64, off=hh * 64): e.tensor_copy(o, i),
                     reads=[bnk], writes=[vt[ktile]])
    P.op("act", lambda e, o=qm, i=qm: e.activation(o, i, AF.Sqrt), reads=[qrow], writes=[qrow])
    P.op("act", lambda e, o=km, i=km: e.activation(o, i, AF.Sqrt), reads=[qrow], writes=[qrow])
    for kv in range(8):
        grp = [h for h in range(32) if h // 4 == kv]
        for h in grp:
            P.op("dve", lambda e, o=brow.ap[0:1, h:h + 1], a=qm[0:1, h:h + 1] if False else
                 qrow.ap[0:1, h:h + 1], b=qrow.ap[0:1, 32 + kv:33 + kv]:
                 e.tensor_mul(o, a, b), reads=[qrow], writes=[brow])
    P.op("dve", lambda e, o=brow.ap[0:1, 0:32], i=brow.ap[0:1, 0:32]:
         e.tensor_scalar_mul(o, i, -1.0 / 8.0), reads=[brow], writes=[brow])
    bb6 = ps.banks[6]
    P.mm_group([lambda e, o=ps.bank(6, 32), l=sh.ones_row_f32.ap[0:1, 0:128],
                r=brow.ap[0:1, 0:32]: e.matmul(o, l, r, start=True, stop=True)],
               reads=[brow, sh.ones_row_f32], out_buf=bb6)
    P.op("act", lambda e, o=bcol.ap, i=ps.bank(6, 32): e.copy(o, i), reads=[bb6], writes=[bcol])

    qscr_r = qscr.rearrange("(m p) t -> p m t", p=128)
    for qc in range(T // QC):
        qsl = slice(qc * QC, (qc + 1) * QC)
        for m in range(16):
            P.dma("sp", qs[m].ap, qscr_r[:, m, qsl], reads=[P.drambuf("a0_qscr")],
                  writes=[qs[m]])
        for m in range(16):
            ha, hb = EH[m], OH[m]
            ka, kb = ha // 4, hb // 4
            for gh in range(2):
                for g in range(4):
                    for half in range(2):
                        kk = 8 * gh + 2 * g + half
                        P.op("pe", lambda e, o=ps.bank(half, QC),
                             l=kt4[ka // 2].ap[0:64, kk * 128:(kk + 1) * 128],
                             r=qs[m].ap[0:64, :]:
                             e.matmul(o, l, r, start=True, stop=True),
                             deps=[kt4[ka // 2].w, qs[m].w, ps.banks[half].w] + ps.banks[half].rs,
                             inc=(half == 1))
                        P.op("pe", lambda e, o=ps.bank(2 + half, QC),
                             l=kt4[kb // 2].ap[64:128, kk * 128:(kk + 1) * 128],
                             r=qs[m].ap[64:128, :]:
                             e.matmul(o, l, r, start=True, stop=True),
                             deps=[kt4[kb // 2].w, ps.banks[2 + half].w] + ps.banks[2 + half].rs,
                             inc=(half == 1))
                    tkS = ("pe", P.cnt["pe"])
                    for bk in range(4):
                        ps.banks[bk].w = tkS
                        ps.banks[bk].rs = []
                    P.op("act", lambda e, o=pta.ap[:, 2 * g * QC:(2 * g + 2) * QC],
                         i=ps.t[:, 0:2 * QC], b=bcol.ap[:, ha:ha + 1]:
                         e.activation(o, i, AF.Exp, bias=b, scale=1.0 / 8.0),
                         reads=[ps.banks[0], ps.banks[1], bcol], writes=[pta])
                    P.op("act", lambda e, o=ptb.ap[:, 2 * g * QC:(2 * g + 2) * QC],
                         i=ps.t[:, 2 * QC:4 * QC], b=bcol.ap[:, hb:hb + 1]:
                         e.activation(o, i, AF.Exp, bias=b, scale=1.0 / 8.0),
                         reads=[ps.banks[2], ps.banks[3], bcol], writes=[ptb])
                for half, (pt, kv, bnk) in enumerate(((pta, ka, 4), (ptb, kb, 5))):
                    P.mm_group([
                        (lambda e, o=ps.bank(bnk, QC)[0:65, :],
                         l=vt[8 * gh + k8].ap[:, kv * 65:(kv + 1) * 65],
                         r=pt.ap[:, k8 * QC:(k8 + 1) * QC],
                         st=(gh == 0 and k8 == 0), sp=(gh == 1 and k8 == 7):
                         e.matmul(o, l, r, start=st, stop=sp)) for k8 in range(8)],
                        reads=[pt] + vt, out_buf=ps.banks[bnk])
            for half, bnk in ((0, 4), (1, 5)):
                P.op("dve", lambda e, o=rrow.ap[0:1, :], i=ps.bank(bnk, QC)[64:65, :]:
                     e.reciprocal(o, i), reads=[ps.banks[bnk]], writes=[rrow])
                bc = ps.banks[6]
                P.mm_group([lambda e, o=ps.bank(6, QC), l=sh.ones_row_f32.ap[0:1, 0:128],
                            r=rrow.ap[0:1, :]: e.matmul(o, l, r, start=True, stop=True)],
                           reads=[rrow, sh.ones_row_f32], out_buf=bc)
                P.op("act", lambda e, o=bcsb.ap, i=ps.bank(6, QC): e.copy(o, i),
                     reads=[bc], writes=[bcsb])
                if half == 0:
                    P.op("dve", lambda e, o=att[m].ap[0:64, :], a=ps.bank(bnk, QC)[0:64, :],
                         b=bcsb.ap[0:64, :]: e.tensor_mul(o, a, b),
                         reads=[ps.banks[bnk], bcsb], writes=[att[m]])
                else:
                    P.op("dve", lambda e, o=abscr.ap[0:64, :], a=ps.bank(bnk, QC)[0:64, :],
                         b=bcsb.ap[0:64, :]: e.tensor_mul(o, a, b),
                         reads=[ps.banks[bnk], bcsb], writes=[abscr])
                    P.dma("sp", att[m].ap[64:128, :], abscr.ap[0:64, :],
                          reads=[abscr], writes=[att[m]])
        for dc in range(NCI):
            par = dc % 2
            wb = wo[par]
            P.dma("pool", wb.ap.rearrange("p (ci e) -> p ci e", ci=NCI),
                  wo_r[:, :, dc * 128:(dc + 1) * 128], writes=[wb])
            xr = xs[dc]
            P.dma("sp", xr.ap[:, 0:QC], xin[dc * 128:(dc + 1) * 128, qsl], writes=[xr], deps=[P.xdep])
            bD = ps.banks[6 + par]
            mms = [
                (lambda e, o=ps.bank(6 + par, QC), l=wb.ap[:, ci * 128:(ci + 1) * 128],
                 r=att[ci].ap, st=(ci == 0): e.matmul(o, l, r, start=st, stop=False))
                for ci in range(NCI)]
            mms.append(lambda e, o=ps.bank(6 + par, QC), l=borow.ap[0:1, dc * 128:(dc + 1) * 128],
                       r=sh.ones_row_bf.ap[0:1, 0:QC]: e.matmul(o, l, r, start=False, stop=True))
            P.mm_group(mms, reads=[wb, borow, sh.ones_row_bf] + att, out_buf=bD)
            xo = xnew[par]
            P.op("dve", lambda e, o=xo.ap[:, 0:QC], i=ps.bank(6 + par, QC), x=xr.ap[:, 0:QC]:
                 e.tensor_add(o, i, x), reads=[bD, xr], writes=[xo])
            P.dma("sp", xout[dc * 128:(dc + 1) * 128, qsl], xo.ap[:, 0:QC], reads=[xo], deps=[P.xdep])
    af32.pop(P.stage_floor()); abf.pop(P.stage_floor())


def _perm64():
    return [2 * i for i in range(32)] + [2 * i + 1 for i in range(32)]


def prep_attn0(inputs, pre):
    f32 = np.float32
    p64 = _perm64()
    qperm = []
    for m in range(16):
        for h in (EH[m], OH[m]):
            qperm.extend([h * 64 + dd for dd in p64])
    kperm = []
    for kv in range(8):
        kperm.extend([kv * 64 + dd for dd in p64])
    operm = []
    for m in range(16):
        for h in (EH[m], OH[m]):
            operm.extend([h * 64 + dd for dd in range(64)])
    wq = inputs["L0_attn_W_q"]
    wkv = inputs["L0_attn_W_kv"]
    wo = inputs["L0_attn_W_out"]
    pre["a0_wqt"] = np.ascontiguousarray(wq[qperm, :].T).astype(f32)
    pre["a0_wkt"] = np.ascontiguousarray(wkv[0:512][kperm, :].T).astype(f32)
    pre["a0_wvt"] = np.ascontiguousarray(wkv[512:1024].T).astype(f32)
    pre["a0_wot"] = np.ascontiguousarray(wo[:, operm].T).astype(f32)
    pre["a0_bq"] = inputs["L0_attn_b_q"][qperm].reshape(1, C).astype(f32)
    pre["a0_bk"] = inputs["L0_attn_b_kv"][0:512][kperm].reshape(1, 512).astype(f32)
    pre["a0_bo"] = inputs["L0_attn_b_out"].reshape(1, C).astype(f32)
    pre["a0_bv"] = inputs["L0_attn_b_kv"][512:1024].reshape(1, 512).astype(f32)
    pos = np.arange(T, dtype=np.float64)
    fr = 1.0 / 10000.0 ** (np.arange(0, 64, 2, dtype=np.float64) / 64.0)
    th = pos[None, :] * fr[:, None]
    cos = np.cos(th); sin = np.sin(th)
    T1 = np.tile(cos, (4, 1)).astype(f32)
    T2 = np.concatenate([-sin, sin, -sin, sin], axis=0).astype(f32)
    pre["a0_t1k"] = T1
    pre["a0_t2k"] = T2


ATTN0_NAMES = ["a0_wqt", "a0_wkt", "a0_wvt", "a0_wot", "a0_bq", "a0_bk", "a0_bo", "a0_bv",
               "a0_t1k", "a0_t2k"]


ATTN0_SCRATCH = [("a0_qscr", [2048, 2048])]




def _wconv_cast(P, abf, d, jobs):
    abf.push()
    buf = abf.bufs(2, 8192)
    k = 0
    for srcn, dstn, n, m in jobs:
        for r in range(n // 128):
            b = buf[k % 2]
            k += 1
            P.dma("pool", b.ap[:, 0:m], d[srcn][r * 128:(r + 1) * 128, :], writes=[b])
            P.dma("act", d[dstn][:, :, r, :].rearrange("oc p e -> p oc e"),
                  b.ap[:, 0:m].rearrange("p (oc e) -> p oc e", e=128),
                  reads=[b], writes=[P.drambuf(dstn)])
    abf.pop(P.stage_floor())


def emit_wconv_a1a(P, ps, sh, af32, abf, d, xin, xout):
    _wconv_cast(P, abf, d, [("a1_wkt", "a1_wkb", C, C), ("a1_wvt", "a1_wvb", C, C),
                            ("a1_wrt", "a1_wrb", C, C), ("a1_wot", "a1_wob", C, C)])


def emit_wconv_a1b(P, ps, sh, af32, abf, d, xin, xout):
    _wconv_cast(P, abf, d, [("a1_wckt", "a1_wckb", C, 8192), ("a1_wcvt", "a1_wcvb", 8192, C),
                            ("a1_wcrt", "a1_wcrb", C, C)])


def _attn1_cols(P, af32, d, names):
    cols = {}
    for n in names:
        cols[n] = af32.buf(16)
        P.dma("sp", cols[n].ap, d["a1_" + n].ap(), writes=[cols[n]])
    return cols


def emit_attn1a(P, ps, sh, af32, abf, d, xin, xout):
    af32.push(); abf.push()
    cols = _attn1_cols(P, af32, d, ("ln1w", "ln1b", "tmk", "tmv", "tmr", "decay", "first"))
    ew = af32.buf(16)
    eu = af32.buf(16)
    P.op("act", lambda e, o=ew.ap, i=cols["decay"].ap: e.activation(o, i, AF.Exp),
         reads=[cols["decay"]], writes=[ew])
    P.op("act", lambda e, o=ew.ap, i=ew.ap: e.activation(o, i, AF.Exp, scale=-1.0),
         reads=[ew], writes=[ew])
    P.op("act", lambda e, o=eu.ap, i=cols["first"].ap: e.activation(o, i, AF.Exp),
         reads=[cols["first"]], writes=[eu])

    xs = af32.bufs(NCI, TS)
    xn1 = abf.bufs(NCI, TS)
    xxt = abf.bufs(2, TS)
    xk = abf.bufs(NCI, TS)
    xv = abf.bufs(NCI, TS)
    xr = abf.bufs(NCI, TS)
    attin = abf.bufs(NCI, TS)
    wkb = [abf.buf(NCI * 128) for _ in range(4)]
    pcol1 = af32.buf(16)
    astate = af32.buf(16)
    bstate = af32.buf(16)
    kf = af32.bufs(1, TS); vf = af32.bufs(1, TS)
    ekt = af32.bufs(1, TS); ekv = af32.bufs(1, TS)
    ewt = af32.buf(TS)
    ahist = af32.bufs(1, TS + 1); bhist = af32.bufs(1, TS + 1)
    numt = af32.bufs(1, TS); dent = af32.bufs(1, TS)
    srt = [abf.buf(TS)]
    xmid = af32.bufs(1, TS)
    for b_ in (pcol1, astate, bstate):
        P.op("dve", lambda e, o=b_.ap: e.memset(o, 0.0), writes=[b_])
    P.op("dve", lambda e, o=ewt.ap: e.memset(o, 0.0), writes=[ewt])

    wk_r = d["a1_wkt"].rearrange("(ci p) e -> p ci e", p=128)
    wv_r = d["a1_wvt"].rearrange("(ci p) e -> p ci e", p=128)
    wr_r = d["a1_wrt"].rearrange("(ci p) e -> p ci e", p=128)
    wo_r = d["a1_wot"].rearrange("(ci p) e -> p ci e", p=128)

    for s in range(NSL):
        tsl = slice(s * TS, (s + 1) * TS)
        for ci in range(NCI):
            P.dma("sp", xs[ci].ap, xin[ci * 128:(ci + 1) * 128, tsl], writes=[xs[ci]], deps=[P.xdep])
        emit_norm(P, ps, sh, xs, xs)
        emit_norm(P, ps, sh, xs, xn1, affine=(cols["ln1w"], cols["ln1b"]))
        for ci in range(NCI):
            xx = xxt[ci % 2]
            P.op("dve", lambda e, o=xx.ap[:, 1:TS], i=xn1[ci].ap[:, 0:TS - 1]:
                 e.tensor_copy(o, i), reads=[xn1[ci]], writes=[xx])
            P.op("dve", lambda e, o=xx.ap[:, 0:1], i=pcol1.ap[:, ci:ci + 1]:
                 e.tensor_copy(o, i), reads=[pcol1], writes=[xx])
            P.op("dve", lambda e, o=pcol1.ap[:, ci:ci + 1], i=xn1[ci].ap[:, TS - 1:TS]:
                 e.tensor_copy(o, i), reads=[xn1[ci]], writes=[pcol1])
            eng = "dve" if ci % 2 == 0 else "pool"
            scr = sh.scr[eng]
            P.op(eng, lambda e, o=scr.ap[:, 0:TS], a=xn1[ci].ap, b=xx.ap: e.tensor_sub(o, a, b),
                 reads=[xn1[ci], xx], writes=[scr])
            for dstb, cn in ((xk[ci], "tmk"), (xv[ci], "tmv"), (xr[ci], "tmr")):
                P.op("dve", lambda e, o=dstb.ap, i=scr.ap[:, 0:TS], sc=cols[cn].ap[:, ci:ci + 1],
                     x=xx.ap: e.scalar_tensor_tensor(o, i, sc, x, ALU.mult, ALU.add),
                     reads=[scr, cols[cn], xx], writes=[dstb])
        for dc in range(NCI):
            par = dc % 2
            for w_r, rhs, bank in ((wk_r, xk, 0), (wv_r, xv, 1), (wr_r, xr, 2)):
                wb = wkb[bank]
                P.dma("pool", wb.ap.rearrange("p (ci e) -> p ci e", ci=NCI),
                      w_r[:, :, dc * 128:(dc + 1) * 128], writes=[wb])
                P.mm_group([
                    (lambda e, o=ps.bank(bank, TS), l=wb.ap[:, ci * 128:(ci + 1) * 128],
                     r=rhs[ci].ap, st=(ci == 0), sp=(ci == NCI - 1):
                     e.matmul(o, l, r, start=st, stop=sp)) for ci in range(NCI)],
                    reads=[wb] + rhs, out_buf=ps.banks[bank])
            kb, vb, ekb, evb = kf[0], vf[0], ekt[0], ekv[0]
            ah, bh = ahist[0], bhist[0]
            nm, dn, sr = numt[0], dent[0], srt[0]
            P.op("act", lambda e, o=kb.ap, i=ps.bank(0, TS): e.copy(o, i),
                 reads=[ps.banks[0]], writes=[kb])
            P.op("act", lambda e, o=vb.ap, i=ps.bank(1, TS): e.copy(o, i),
                 reads=[ps.banks[1]], writes=[vb])
            P.op("act", lambda e, o=sr.ap, i=ps.bank(2, TS): e.activation(o, i, AF.Sigmoid),
                 reads=[ps.banks[2]], writes=[sr])
            P.op("act", lambda e, o=ekb.ap, i=kb.ap: e.activation(o, i, AF.Exp),
                 reads=[kb], writes=[ekb])
            P.op("dve", lambda e, o=evb.ap, a=ekb.ap, b=vb.ap: e.tensor_mul(o, a, b),
                 reads=[ekb, vb], writes=[evb])
            P.op("dve", lambda e, o=ewt.ap, i=ewt.ap, sc=ew.ap[:, dc:dc + 1]:
                 e.tensor_scalar(o, i, 0.0, sc, ALU.mult, ALU.add),
                 reads=[ew], writes=[ewt])
            P.op("dve", lambda e, o=ah.ap[:, 0:1], i=astate.ap[:, dc:dc + 1]:
                 e.tensor_copy(o, i), reads=[astate], writes=[ah])
            P.op("dve", lambda e, o=bh.ap[:, 0:1], i=bstate.ap[:, dc:dc + 1]:
                 e.tensor_copy(o, i), reads=[bstate], writes=[bh])
            P.op("dve", lambda e, o=ah.ap[:, 1:TS + 1], d0=ewt.ap, d1=evb.ap,
                 ini=astate.ap[:, dc:dc + 1]:
                 e.tensor_tensor_scan(o, d0, d1, ini, ALU.mult, ALU.add),
                 reads=[ewt, evb, astate], writes=[ah])
            P.op("dve", lambda e, o=bh.ap[:, 1:TS + 1], d0=ewt.ap, d1=ekb.ap,
                 ini=bstate.ap[:, dc:dc + 1]:
                 e.tensor_tensor_scan(o, d0, d1, ini, ALU.mult, ALU.add),
                 reads=[ewt, ekb, bstate], writes=[bh])
            P.op("dve", lambda e, o=astate.ap[:, dc:dc + 1], i=ah.ap[:, TS:TS + 1]:
                 e.tensor_copy(o, i), reads=[ah], writes=[astate])
            P.op("dve", lambda e, o=bstate.ap[:, dc:dc + 1], i=bh.ap[:, TS:TS + 1]:
                 e.tensor_copy(o, i), reads=[bh], writes=[bstate])
            P.op("dve", lambda e, o=nm.ap, i=evb.ap, sc=eu.ap[:, dc:dc + 1],
                 x=ah.ap[:, 0:TS]: e.scalar_tensor_tensor(o, i, sc, x, ALU.mult, ALU.add),
                 reads=[evb, eu, ah], writes=[nm])
            P.op("dve", lambda e, o=dn.ap, i=ekb.ap, sc=eu.ap[:, dc:dc + 1],
                 x=bh.ap[:, 0:TS]: e.scalar_tensor_tensor(o, i, sc, x, ALU.mult, ALU.add),
                 reads=[ekb, eu, bh], writes=[dn])
            P.op("dve", lambda e, o=dn.ap, i=dn.ap: e.reciprocal(o, i),
                 reads=[dn], writes=[dn])
            P.op("dve", lambda e, o=nm.ap, a=nm.ap, b=dn.ap: e.tensor_mul(o, a, b),
                 reads=[dn], writes=[nm])
            P.op("dve", lambda e, o=attin[dc].ap, a=nm.ap, b=sr.ap: e.tensor_mul(o, a, b),
                 reads=[nm, sr], writes=[attin[dc]])
        for dc in range(NCI):
            par = dc % 2
            wb = wkb[3]
            P.dma("pool", wb.ap.rearrange("p (ci e) -> p ci e", ci=NCI),
                  wo_r[:, :, dc * 128:(dc + 1) * 128], writes=[wb])
            bD = ps.banks[4 + par]
            P.mm_group([
                (lambda e, o=ps.bank(4 + par, TS), l=wb.ap[:, ci * 128:(ci + 1) * 128],
                 r=attin[ci].ap, st=(ci == 0), sp=(ci == NCI - 1):
                 e.matmul(o, l, r, start=st, stop=sp)) for ci in range(NCI)],
                reads=[wb] + attin, out_buf=bD)
            xo = xmid[0]
            P.op("dve", lambda e, o=xo.ap, i=ps.bank(4 + par, TS), z=xs[dc].ap:
                 e.tensor_add(o, i, z), reads=[bD, xs[dc]], writes=[xo])
            P.dma("sp", xout[dc * 128:(dc + 1) * 128, tsl], xo.ap, reads=[xo], deps=[P.xdep])
    af32.pop(P.stage_floor()); abf.pop(P.stage_floor())


def emit_attn1b(P, ps, sh, af32, abf, d, xin, xout):
    af32.push(); abf.push()
    cols = _attn1_cols(P, af32, d, ("ln2w", "ln2b", "cmk", "cmr"))
    xorig = d["a1_xorig"]
    xm = af32.bufs(NCI, TS)
    xn2 = abf.bufs(NCI, TS)
    xx2 = abf.bufs(1, TS)
    ck = abf.bufs(NCI, TS)
    cr = abf.bufs(NCI, TS)
    h = abf.bufs(64, TS)
    wck = [abf.buf(NCI * 128) for _ in range(2)]
    wcv = [abf.buf(16 * 128) for _ in range(2)]
    pcol2 = af32.buf(16)
    hgf = af32.bufs(1, TS)
    sgf = af32.bufs(1, TS)
    xrld = af32.bufs(2, TS)
    xnew = af32.bufs(2, TS)
    P.op("dve", lambda e, o=pcol2.ap: e.memset(o, 0.0), writes=[pcol2])
    wck_r = d["a1_wckt"].rearrange("(ci p) e -> p ci e", p=128)
    wcv_r = d["a1_wcvt"].rearrange("(g p) e -> p g e", p=128)
    wcr_r = d["a1_wcrt"].rearrange("(ci p) e -> p ci e", p=128)
    for s in range(NSL):
        tsl = slice(s * TS, (s + 1) * TS)
        for ci in range(NCI):
            P.dma("sp", xm[ci].ap, xin[ci * 128:(ci + 1) * 128, tsl], writes=[xm[ci]], deps=[P.xdep])
        emit_norm(P, ps, sh, xm, xn2, affine=(cols["ln2w"], cols["ln2b"]))
        for ci in range(NCI):
            xx = xx2[0]
            P.op("dve", lambda e, o=xx.ap[:, 1:TS], i=xn2[ci].ap[:, 0:TS - 1]:
                 e.tensor_copy(o, i), reads=[xn2[ci]], writes=[xx])
            P.op("dve", lambda e, o=xx.ap[:, 0:1], i=pcol2.ap[:, ci:ci + 1]:
                 e.tensor_copy(o, i), reads=[pcol2], writes=[xx])
            P.op("dve", lambda e, o=pcol2.ap[:, ci:ci + 1], i=xn2[ci].ap[:, TS - 1:TS]:
                 e.tensor_copy(o, i), reads=[xn2[ci]], writes=[pcol2])
            eng = "dve" if ci % 2 == 0 else "pool"
            scr = sh.scr[eng]
            P.op(eng, lambda e, o=scr.ap[:, 0:TS], a=xn2[ci].ap, b=xx.ap: e.tensor_sub(o, a, b),
                 reads=[xn2[ci], xx], writes=[scr])
            for dstb, cn in ((ck[ci], "cmk"), (cr[ci], "cmr")):
                P.op("dve", lambda e, o=dstb.ap, i=scr.ap[:, 0:TS], sc=cols[cn].ap[:, ci:ci + 1],
                     x=xx.ap: e.scalar_tensor_tensor(o, i, sc, x, ALU.mult, ALU.add),
                     reads=[scr, cols[cn], xx], writes=[dstb])
        for ic in range(64):
            par = ic % 2
            wb = wck[par]
            P.dma("pool", wb.ap.rearrange("p (ci e) -> p ci e", ci=NCI),
                  wck_r[:, :, ic * 128:(ic + 1) * 128], writes=[wb])
            bG = ps.banks[par]
            P.mm_group([
                (lambda e, o=ps.bank(par, TS), l=wb.ap[:, ci * 128:(ci + 1) * 128],
                 r=ck[ci].ap, st=(ci == 0), sp=(ci == NCI - 1):
                 e.matmul(o, l, r, start=st, stop=sp)) for ci in range(NCI)],
                reads=[wb] + ck, out_buf=bG)
            hb = hgf[0]
            P.op("act", lambda e, o=hb.ap, i=ps.bank(par, TS): e.activation(o, i, AF.Relu),
                 reads=[bG], writes=[hb])
            P.op("dve", lambda e, o=h[ic].ap, a=hb.ap, b=hb.ap: e.tensor_mul(o, a, b),
                 reads=[hb], writes=[h[ic]])
        for hc in range(NCI):
            par = hc % 2
            bK = ps.banks[2 + par]
            for gh2 in range(4):
                wb = wcv[gh2 % 2]
                P.dma("pool", wb.ap.rearrange("p (g e) -> p g e", g=16),
                      wcv_r[:, gh2 * 16:(gh2 + 1) * 16, hc * 128:(hc + 1) * 128],
                      writes=[wb])
                P.mm_group([
                    (lambda e, o=ps.bank(2 + par, TS), l=wb.ap[:, i2 * 128:(i2 + 1) * 128],
                     r=h[gh2 * 16 + i2].ap, st=(gh2 == 0 and i2 == 0),
                     sp=(gh2 == 3 and i2 == 15):
                     e.matmul(o, l, r, start=st, stop=sp)) for i2 in range(16)],
                    reads=[wb] + h[gh2 * 16:(gh2 + 1) * 16], out_buf=bK)
            wb2 = wck[par]
            P.dma("pool", wb2.ap.rearrange("p (ci e) -> p ci e", ci=NCI),
                  wcr_r[:, :, hc * 128:(hc + 1) * 128], writes=[wb2])
            bR = ps.banks[4 + par]
            P.mm_group([
                (lambda e, o=ps.bank(4 + par, TS), l=wb2.ap[:, ci * 128:(ci + 1) * 128],
                 r=cr[ci].ap, st=(ci == 0), sp=(ci == NCI - 1):
                 e.matmul(o, l, r, start=st, stop=sp)) for ci in range(NCI)],
                reads=[wb2] + cr, out_buf=bR)
            sg = sgf[0]
            P.op("act", lambda e, o=sg.ap, i=ps.bank(4 + par, TS): e.activation(o, i, AF.Sigmoid),
                 reads=[bR], writes=[sg])
            xr_ = xrld[par]
            P.dma("sp", xr_.ap, xorig[hc * 128:(hc + 1) * 128, tsl], writes=[xr_], deps=[P.xdep])
            xo = xnew[par]
            P.op("dve", lambda e, o=xo.ap, a=ps.bank(2 + par, TS), b=sg.ap:
                 e.tensor_mul(o, a, b), reads=[bK, sg], writes=[xo])
            P.op("dve", lambda e, o=xo.ap, a=xo.ap, b=xm[hc].ap: e.tensor_add(o, a, b),
                 reads=[xm[hc]], writes=[xo])
            P.op("dve", lambda e, o=xo.ap, a=xo.ap, b=xr_.ap: e.tensor_add(o, a, b),
                 reads=[xr_], writes=[xo])
            P.dma("sp", xout[hc * 128:(hc + 1) * 128, tsl], xo.ap, reads=[xo], deps=[P.xdep])
    af32.pop(P.stage_floor()); abf.pop(P.stage_floor())


def prep_attn1(inputs, pre):
    f32 = np.float32
    for n, key in (("wkt", "Wk"), ("wvt", "Wv"), ("wrt", "Wr"), ("wot", "Wo"),
                   ("wckt", "Wck"), ("wcvt", "Wcv"), ("wcrt", "Wcr")):
        pre["a1_" + n] = np.ascontiguousarray(inputs["L1_attn_" + key].T).astype(f32)
    for n, key in (("ln1w", "ln1_w"), ("ln1b", "ln1_b"), ("ln2w", "ln2_w"),
                   ("ln2b", "ln2_b"), ("tmk", "tmk"), ("tmv", "tmv"), ("tmr", "tmr"),
                   ("cmk", "cmk"), ("cmr", "cmr"), ("decay", "time_decay"),
                   ("first", "time_first")):
        pre["a1_" + n] = np.ascontiguousarray(
            inputs["L1_attn_" + key].reshape(16, 128).T).astype(f32)


ATTN1_NAMES = ["a1_wkt", "a1_wvt", "a1_wrt", "a1_wot", "a1_wckt", "a1_wcvt", "a1_wcrt",
               "a1_ln1w", "a1_ln1b", "a1_ln2w", "a1_ln2b", "a1_tmk", "a1_tmv", "a1_tmr",
               "a1_cmk", "a1_cmr", "a1_decay", "a1_first"]
ATTN1A_NAMES = ["a1_wkt", "a1_wvt", "a1_wrt", "a1_wot", "a1_ln1w", "a1_ln1b",
                "a1_tmk", "a1_tmv", "a1_tmr", "a1_decay", "a1_first"]
ATTN1A_SCRATCH = [("a1_wkb", [16, 128, 16, 128]), ("a1_wvb", [16, 128, 16, 128]),
                  ("a1_wrb", [16, 128, 16, 128]), ("a1_wob", [16, 128, 16, 128])]
ATTN1B_NAMES = ["a1_wckt", "a1_wcvt", "a1_wcrt", "a1_ln2w", "a1_ln2b", "a1_cmk", "a1_cmr"]
ATTN1B_SCRATCH = [("a1_wckb", [64, 128, 16, 128]), ("a1_wcvb", [16, 128, 64, 128]),
                  ("a1_wcrb", [16, 128, 16, 128])]


_default_chain()



ALL_NAMES = sorted(set(FFN0_NAMES + ATTN0_NAMES + FFN1_NAMES + ATTN1A_NAMES + ATTN1B_NAMES))
ALL_SCRATCH = list(ATTN0_SCRATCH) + list(FFN1_SCRATCH) + list(ATTN1A_SCRATCH) + \
    list(ATTN1B_SCRATCH)
FUSED = True


def build_fused():
    nc = bass.Bass("TRN2", target_bir_lowering=False)
    xin_t = nc.dram_tensor("x_in", [C, T], F32, kind="ExternalInput")
    xout_t = nc.dram_tensor("x_out", [C, T], F32, kind="ExternalOutput")
    xa = nc.dram_tensor("xa", [C, T], F32, kind="Internal")
    xb = nc.dram_tensor("xb", [C, T], F32, kind="Internal")
    xc = nc.dram_tensor("xc", [C, T], F32, kind="Internal")
    P = Prog(nc)
    ps = Psum(nc)
    af32 = Arena(nc, "af32", 17750, F32)
    abf = Arena(nc, "abf", 70200, BF16)
    sh = Shared(P, af32, abf)
    d = declare_drams(nc, PRE_CACHE, ALL_NAMES)
    for sc_ in ALL_SCRATCH:
        n, shp = sc_[0], sc_[1]
        dt = F32 if (len(sc_) > 2 and sc_[2] == "f32") else BF16
        d[n] = nc.dram_tensor(n, list(shp), dt, kind="Internal")
    d["a1_xorig"] = xb
    chain = [
        (emit_ffn0, xin_t, xa),
        (emit_wconv_h1, None, None),
        (emit_attn0, xa, xb),
        (emit_wconv_h2, None, None),
        (emit_ffn0, xb, xa),
        (emit_ffn1, xa, xb),
        (emit_attn1a, xb, xc),
        (emit_attn1b, xc, xa),
        (emit_ffn1, xa, xout_t),
    ]
    for fn, xi, xo in chain:
        P.xdep = [(f"q_sp_{i}", P.cnt[f"q_sp_{i}"]) for i in range(P.QW)]
        fn(P, ps, sh, af32, abf, d,
           xi.ap() if xi is not None else None,
           xo.ap() if xo is not None else None)
    P.finalize([(s, P.cnt[s]) for s in P.cnt])
    return nc


def kernel(**inputs):
    pre = PRE_CACHE
    pre.clear()
    for prep in PREPS:
        prep(inputs, pre)
    x = inputs["L0_attn_x"]
    xf = [np.ascontiguousarray(x[b].T).astype(np.float32) for b in range(8)]
    if FUSED:
        nc = build_fused()
        in_maps = []
        for b in range(8):
            m = {n: pre[n] for n in ALL_NAMES}
            m["x_in"] = xf[b]
            in_maps.append(m)
        res = run_bass_kernel_spmd(nc, in_maps, core_ids=list(range(8)))
        xf = [np.ascontiguousarray(res.results[b]["x_out"]) for b in range(8)]
    else:
        hist = [xf]
        for spec in LAUNCHES:
            stage_fns, names, scratch = spec[0], spec[1], spec[2]
            sizes = spec[3] if len(spec) > 3 else (15000, 72000)
            x2_from = spec[4] if len(spec) > 4 else None
            nc = build_single_stage(stage_fns, names, scratch, sizes,
                                    x2=(x2_from is not None))
            in_maps = []
            for b in range(8):
                m = {n: pre[n] for n in names}
                m["x_in"] = hist[-1][b]
                if x2_from is not None:
                    m["x_in2"] = hist[x2_from][b]
                in_maps.append(m)
            res = run_bass_kernel_spmd(nc, in_maps, core_ids=list(range(8)))
            hist.append([np.ascontiguousarray(res.results[b]["x_out"]) for b in range(8)])
        xf = hist[-1]
    out = np.stack([xf[b].T for b in range(8)])
    return np.ascontiguousarray(out).astype(np.float32)


def emit_wconv_h1(P, ps, sh, af32, abf, d, xin, xout):
    emit_wconv(P, ps, sh, af32, abf, d, xin, xout,
               jobs=[("f1_gq", "f1_gbf", 2048, 8192)])


def emit_wconv_h2(P, ps, sh, af32, abf, d, xin, xout):
    emit_wconv(P, ps, sh, af32, abf, d, xin, xout,
               jobs=[("f1_uq", "f1_ubf", 2048, 8192), ("f1_dq", "f1_dbf", 8192, 2048)])


# version 10
# speedup vs baseline: 1.1962x; 1.0629x over previous
import contextlib
import numpy as np
import ml_dtypes
import concourse.bass as bass
import concourse.mybir as mybir
from concourse.bass_utils import run_bass_kernel_spmd

F32 = mybir.dt.float32
BF16 = mybir.dt.bfloat16
I32 = mybir.dt.int32
AF = mybir.ActivationFunctionType
ALU = mybir.AluOpType
AX = mybir.AxisListType

C = 2048
T = 2048
NCI = C // 128
TS = 512
NSL = T // TS


ENGINES = ("pe", "act", "dve", "pool", "sp")


class Buf:
    __slots__ = ("ap", "w", "rs")

    def __init__(self, ap, w=None):
        self.ap = ap
        self.w = w
        self.rs = []


class Prog:
    def __init__(self, nc):
        self.nc = nc
        self.ops = {e: [] for e in ENGINES}
        self.cnt = {}
        self.lastwait = {e: {} for e in ENGINES}
        self.sems = {}
        self._stack = contextlib.ExitStack()
        for s in ("pe", "act", "dve", "pool", "q_sp", "q_pool", "q_act"):
            self.sems[s] = self._stack.enter_context(nc.semaphore("sem_" + s))
            self.cnt[s] = 0
        self._drambufs = {}
        self.xdep = None
        self.QW = 12
        self.qn = {"sp": 0, "pool": 0, "act": 0}
        for q in ("sp", "pool", "act"):
            for i in range(self.QW):
                nme = f"q_{q}_{i}"
                self.sems[nme] = self._stack.enter_context(nc.semaphore("sem_" + nme))
                self.cnt[nme] = 0

    def rotate_queues(self, thresh=0):
        pass

    def stage_floor(self):
        return [(s, self.cnt[s]) for s in self.cnt]

    def drambuf(self, name):
        if name not in self._drambufs:
            self._drambufs[name] = Buf(None)
        return self._drambufs[name]

    _ESEM = {"pe": "pe", "act": "act", "dve": "dve", "pool": "pool"}
    _QENG = {"sp": "sp", "pool": "pool", "act": "act"}

    def _wait(self, eng, tok):
        if tok is None:
            return
        if isinstance(tok, list):
            for t in tok:
                self._wait(eng, t)
            return
        sem, val = tok
        if val <= 0 or self.lastwait[eng].get(sem, 0) >= val:
            return
        self.lastwait[eng][sem] = val
        h = self.sems[sem]
        self.ops[eng].append(lambda e, h=h, v=val: e.wait_ge(h, v))

    def op(self, eng, build, reads=(), writes=(), deps=(), inc=None):
        if inc is None:
            inc = bool(reads or writes)
        deps = list(deps)
        for b in reads:
            deps.append(b.w)
        for b in writes:
            deps.append(b.w)
            deps.extend(b.rs)
        for d in deps:
            self._wait(eng, d)
        tok = None
        if inc:
            s = self._ESEM[eng]
            self.cnt[s] += 1
            tok = (s, self.cnt[s])
            h = self.sems[s]
            self.ops[eng].append(lambda e, b=build, h=h: b(e).then_inc(h, 1))
        else:
            self.ops[eng].append(lambda e, b=build: b(e))
        for b in reads:
            if tok:
                b.rs.append(tok)
        for b in writes:
            b.w = tok
            b.rs = []
        return tok

    def dma(self, q, out, in_, reads=(), writes=(), deps=()):
        eng = self._QENG[q]
        s = f"q_{q}_{self.qn[q] % self.QW}"
        self.qn[q] += 1
        deps = list(deps)
        for b in reads:
            deps.append(b.w)
        for b in writes:
            deps.append(b.w)
            deps.extend(b.rs)
        deps.append((s, self.cnt[s]))
        for d in deps:
            self._wait(eng, d)
        self.cnt[s] += 16
        tok = (s, self.cnt[s])
        h = self.sems[s]
        self.ops[eng].append(
            lambda e, o=out, i=in_, h=h: e.dma_start(out=o, in_=i).then_inc(h, 16))
        for b in reads:
            b.rs.append(tok)
        for b in writes:
            b.w = tok
            b.rs = []
        return tok

    def mm_group(self, mms, reads=(), out_buf=None):
        deps = [b.w for b in reads]
        if out_buf is not None:
            deps.append(out_buf.w)
            deps.extend(out_buf.rs)
        for i, m in enumerate(mms):
            if i < len(mms) - 1:
                self.op("pe", m, deps=deps if i == 0 else (), inc=False)
            else:
                tok = self.op("pe", m, deps=deps if len(mms) == 1 else (), inc=True)
        for b in reads:
            b.rs.append(tok)
        if out_buf is not None:
            out_buf.w = tok
            out_buf.rs = []
        return tok

    def finalize(self, final_deps):
        for d in final_deps:
            self._wait("sp", d)
        nc, ops = self.nc, self.ops
        with nc.Block() as block:
            @block.sync
            def _(e):
                for f in ops["sp"]:
                    f(e)

            @block.tensor
            def _(e):
                for f in ops["pe"]:
                    f(e)

            @block.vector
            def _(e):
                for f in ops["dve"]:
                    f(e)

            @block.scalar
            def _(e):
                for f in ops["act"]:
                    f(e)

            @block.gpsimd
            def _(e):
                for f in ops["pool"]:
                    f(e)
        self._stack.close()


class Arena:
    def __init__(self, nc, name, width, dtype):
        self.t = nc.alloc_sbuf_tensor(name, [128, width], dtype)
        self.width = width
        self.off = 0
        self.marks = []
        self.floor = []

    def alloc(self, w):
        assert self.off + w <= self.width, f"arena {self.off}+{w}>{self.width}"
        ap = self.t[:, self.off:self.off + w]
        self.off += w
        self.hwm = max(getattr(self, "hwm", 0), self.off)
        return ap

    def buf(self, w):
        b = Buf(self.alloc(w))
        b.rs = list(self.floor)
        return b

    def bufs(self, n, w):
        return [self.buf(w) for _ in range(n)]

    def push(self):
        self.marks.append(self.off)

    def pop(self, floor=None):
        self.off = self.marks.pop()
        if floor is not None:
            self.floor = floor


class Psum:
    def __init__(self, nc):
        self._stack = contextlib.ExitStack()
        self.t = self._stack.enter_context(nc.psum_tensor("psum", [128, 4096], F32))
        self.banks = [Buf(self.t[:, i * 512:(i + 1) * 512]) for i in range(8)]

    def bank(self, i, w=512, off=0):
        assert off + w <= 512
        return self.t[:, i * 512 + off: i * 512 + off + w]




class Shared:

    def __init__(self, P, af32, abf):
        self.ones_col_bf = abf.buf(1)
        self.ones_row_bf = abf.buf(512)
        t0 = P.op("dve", lambda e, t=self.ones_col_bf.ap: e.memset(t, 1.0), inc=True)
        t1 = P.op("dve", lambda e, t=self.ones_row_bf.ap[0:1, :]: e.memset(t, 1.0), inc=True)
        self.ones_col_bf.w = t0
        self.ones_row_bf.w = t1
        self.ones_row_f32 = af32.buf(128)
        self.ones_row_f32.w = P.op("dve", lambda e, t=self.ones_row_f32.ap[0:1, :]:
                                   e.memset(t, 1.0), inc=True)
        self.eps = af32.buf(1)
        self.eps.w = P.op("dve", lambda e, t=self.eps.ap[0:1, :]: e.memset(t, 1e-5), inc=True)
        self.xb = abf.bufs(2, TS)
        self.x2b = abf.bufs(2, TS)
        self.rows = af32.buf(4 * TS)
        self.bcast = af32.buf(512 + TS)
        self.scr = {"dve": af32.buf(TS), "pool": af32.buf(TS)}


def emit_norm(P, ps, sh, xs, out, banks=(4, 5, 6), affine=None, ts=TS):
    bS, bS2, bB = banks
    sb, sb2, bb = ps.banks[bS], ps.banks[bS2], ps.banks[bB]
    sb.rs and None
    first_deps = [sb.w] + sb.rs + [sb2.w] + sb2.rs + [sh.ones_col_bf.w]
    for ci in range(NCI):
        k = ci % 2
        eng = "dve" if ci % 2 == 0 else "pool"
        P.op(eng, lambda e, o=sh.xb[k].ap, i=xs[ci].ap: e.tensor_copy(o, i),
             reads=[xs[ci]], writes=[sh.xb[k]])
        P.op("act", lambda e, o=sh.x2b[k].ap, i=xs[ci].ap: e.activation(o, i, AF.Square),
             reads=[xs[ci]], writes=[sh.x2b[k]])
        d0 = first_deps if ci == 0 else ()
        t1 = P.op("pe", lambda e, o=ps.bank(bS, ts), l=sh.ones_col_bf.ap, r=sh.xb[k].ap,
                  s=(ci == 0), p=(ci == NCI - 1): e.matmul(o[0:1, :], l, r, start=s, stop=p),
                  deps=list(d0) + [sh.xb[k].w], inc=True)
        t2 = P.op("pe", lambda e, o=ps.bank(bS2, ts), l=sh.ones_col_bf.ap, r=sh.x2b[k].ap,
                  s=(ci == 0), p=(ci == NCI - 1): e.matmul(o[0:1, :], l, r, start=s, stop=p),
                  deps=[sh.x2b[k].w], inc=True)
        sh.xb[k].rs.append(t1)
        sh.x2b[k].rs.append(t2)
    sb.w = t1; sb.rs = []
    sb2.w = t2; sb2.rs = []
    rows = sh.rows.ap
    nmu = rows[0:1, 0:ts]
    mu2 = rows[0:1, ts:2 * ts]
    var = rows[0:1, 2 * ts:3 * ts]
    rstd = rows[0:1, 3 * ts:4 * ts]
    P.op("dve", lambda e, o=nmu, i=ps.bank(bS, ts)[0:1, :]: e.tensor_scalar_mul(o, i, -1.0 / C),
         reads=[sb], writes=[sh.rows])
    P.op("dve", lambda e, o=mu2, a=nmu: e.tensor_mul(o, a, a), writes=[sh.rows])
    P.op("dve", lambda e, o=var, i=ps.bank(bS2, ts)[0:1, :], m=mu2:
         e.scalar_tensor_tensor(o, i, 1.0 / C, m, ALU.mult, ALU.subtract),
         reads=[sb2], writes=[sh.rows])
    P.op("act", lambda e, o=var, i=var, b=sh.eps.ap[0:1, 0:1]:
         e.activation(o, i, AF.Sqrt, bias=b), reads=[sh.rows, sh.eps], writes=[sh.rows])
    P.op("dve", lambda e, o=rstd, i=var: e.reciprocal(o, i), writes=[sh.rows])
    P.op("dve", lambda e, o=nmu, a=nmu, b=rstd: e.tensor_mul(o, a, b), writes=[sh.rows])
    bb2 = ps.banks[bB + 1]
    tk = P.mm_group(
        [lambda e, o=ps.bank(bB, ts), l=sh.ones_row_f32.ap[0:1, 0:128], r=rstd:
         e.matmul(o, l, r, start=True, stop=True),
         lambda e, o=ps.bank(bB + 1, ts), l=sh.ones_row_f32.ap[0:1, 0:128],
         r=nmu: e.matmul(o, l, r, start=True, stop=True)],
        reads=[sh.rows, sh.ones_row_f32], out_buf=bb)
    bb2.w = tk; bb2.rs = []
    P.op("act", lambda e, o=sh.bcast.ap[:, 0:2 * ts], i=ps.t[:, bB * 512:bB * 512 + 512 + ts]:
         e.copy(o, i), reads=[bb, bb2], writes=[sh.bcast])
    rb = sh.bcast.ap
    for ci in range(NCI):
        eng = "dve" if ci % 2 == 0 else "pool"
        scr = sh.scr[eng]
        P.op(eng, lambda e, o=scr.ap[:, 0:ts], a=xs[ci].ap, b=rb[:, 0:ts]: e.tensor_mul(o, a, b),
             reads=[xs[ci], sh.bcast], writes=[scr])
        if affine is None:
            P.op(eng, lambda e, o=out[ci].ap, a=scr.ap[:, 0:ts], b=rb[:, 512:512 + ts]:
                 e.tensor_add(o, a, b), reads=[scr, sh.bcast], writes=[out[ci]])
        else:
            wcol, bcol = affine
            P.op(eng, lambda e, o=scr.ap[:, 0:ts], a=scr.ap[:, 0:ts], b=rb[:, 512:512 + ts]:
                 e.tensor_add(o, a, b), reads=[sh.bcast], writes=[scr])
            P.op("dve", lambda e, o=out[ci].ap, i=scr.ap[:, 0:ts],
                 w=wcol.ap[:, ci:ci + 1], b=bcol.ap[:, ci:ci + 1]:
                 e.tensor_scalar(o, i, w, b, ALU.mult, ALU.add),
                 reads=[scr, wcol, bcol], writes=[out[ci]])




def load_const_bf16(P, abf, src_ap, w, chunks=None):
    b = abf.buf(w)
    out = b.ap if chunks is None else b.ap.rearrange("p (g n) -> p g n", g=chunks[0])
    P.dma("pool", out, src_ap, writes=[b])
    return b


def emit_ffn0(P, ps, sh, af32, abf, d, xin, xout):
    af32.push(); abf.push()
    rw1b = load_const_bf16(P, abf, d["f0_rw1t"].rearrange("(ci p) h -> p ci h", p=128),
                           NCI * 64, chunks=(NCI, 64))
    atb = load_const_bf16(P, abf, d["f0_at"].rearrange("(ci p) r -> p ci r", p=128),
                          NCI * 256, chunks=(NCI, 256))
    breb = load_const_bf16(P, abf, d["f0_bre"].rearrange("(g p) e -> p g e", p=128),
                           2 * C, chunks=(2, C))
    identb = load_const_bf16(P, abf, d["ident"].ap(), 128)
    eexpb = abf.buf(256)
    P.dma("pool", eexpb.ap[0:16, :], d["f0_eexp"].ap(), writes=[eexpb])
    biasb = abf.buf(C)
    P.dma("pool", biasb.ap[0:1, :], d["f0_bias"].ap(), writes=[biasb])
    rb1c = af32.buf(1)
    P.dma("sp", rb1c.ap[0:64, 0:1], d["f0_rb1c"].ap(), writes=[rb1c])
    rw2x = abf.buf(32)
    P.dma("pool", rw2x.ap[0:64, :], d["f0_rw2t"].ap(), writes=[rw2x])
    r2row = af32.buf(32)
    grow = af32.buf(16)
    P.dma("sp", r2row.ap[0:1, :], d["f0_rb2"].ap(), writes=[r2row])
    P.dma("sp", grow.ap[0:1, :], d["f0_gates"].ap(), writes=[grow])
    P.op("dve", lambda e, o=r2row.ap[0:1, 0:16], a=r2row.ap[0:1, 0:16], b=grow.ap[0:1, :]:
         e.tensor_add(o, a, b), reads=[grow], writes=[r2row])
    P.op("dve", lambda e, o=rw2x.ap[64:65, :], i=r2row.ap[0:1, :]: e.tensor_copy(o, i),
         reads=[r2row], writes=[rw2x])

    xs = af32.bufs(NCI, TS)
    xn = [abf.bufs(NCI, TS) for _ in range(NSL)]
    hx = abf.buf(TS)
    P.op("dve", lambda e, o=hx.ap[64:65, :]: e.memset(o, 1.0), writes=[hx])
    mt = af32.buf(8)
    mask1 = af32.buf(16)
    x2t = af32.buf(16)
    coeff = abf.buf(64)
    ctb = abf.buf(TS)
    cexp = abf.buf(TS)
    wxa = [abf.buf(T) for _ in range(2)]
    wbuf = [abf.buf(NCI * 128) for _ in range(2)]
    xnew = af32.bufs(2, TS)
    wt_r = d["f0_wt"].rearrange("(ci p) e -> p ci e", p=128)

    for s in range(NSL):
        tsl = slice(s * TS, (s + 1) * TS)
        for ci in range(NCI):
            P.dma("sp", xs[ci].ap, xin[ci * 128:(ci + 1) * 128, tsl], writes=[xs[ci]], deps=[P.xdep])
        emit_norm(P, ps, sh, xs, xn[s])
        bH = ps.banks[2]
        P.mm_group([
            (lambda e, o=ps.bank(2, TS), l=rw1b.ap[:, ci * 64:(ci + 1) * 64], r=xn[s][ci].ap,
             st=(ci == 0), sp=(ci == NCI - 1):
             e.matmul(o[0:64, :], l, r, start=st, stop=sp)) for ci in range(NCI)],
            reads=[rw1b] + xn[s], out_buf=bH)
        P.op("act", lambda e, o=hx.ap[0:64, :], i=ps.bank(2, TS)[0:64, :], b=rb1c.ap[0:64, 0:1]:
             e.activation(o, i, AF.Silu, bias=b), reads=[bH, rb1c], writes=[hx])
        for tc in range(TS // 128):
            P.mm_group([lambda e, o=ps.bank(2, 32, off=tc * 32),
                        l=hx.ap[0:65, tc * 128:(tc + 1) * 128], r=rw2x.ap[0:65, :]:
                        e.matmul(o, l, r, start=True, stop=True)],
                       reads=[hx, rw2x], out_buf=bH)
        for tc in range(TS // 128):
            lg = ps.bank(2, 16, off=tc * 32)
            m1 = mt.ap[:, 0:1]; m2 = mt.ap[:, 1:2]; dd = mt.ap[:, 2:3]
            w1 = mt.ap[:, 3:4]; w2 = mt.ap[:, 4:5]
            P.op("dve", lambda e, o=m1, i=lg: e.reduce_max(o, i, axis=AX.X),
                 reads=[bH], writes=[mt])
            P.op("dve", lambda e, o=mask1.ap, i=lg, s_=m1:
                 e.tensor_single_scalar(o, i, s_, ALU.is_ge), reads=[bH, mt], writes=[mask1])
            P.op("dve", lambda e, o=x2t.ap, a=mask1.ap, i=lg:
                 e.scalar_tensor_tensor(o, a, -1e30, i, ALU.mult, ALU.add),
                 reads=[bH, mask1], writes=[x2t])
            P.op("dve", lambda e, o=m2, i=x2t.ap: e.reduce_max(o, i, axis=AX.X),
                 reads=[x2t], writes=[mt])
            P.op("dve", lambda e, o=dd, a=m1, b=m2: e.tensor_sub(o, a, b), writes=[mt])
            P.op("act", lambda e, o=w1, i=dd: e.activation(o, i, AF.Sigmoid),
                 reads=[mt], writes=[mt])
            P.op("dve", lambda e, o=w2, i=w1: e.tensor_scalar(o, i, -1.0, 1.0, ALU.mult, ALU.add),
                 reads=[mt], writes=[mt])
            P.op("dve", lambda e, o=x2t.ap, i=x2t.ap, s_=m2:
                 e.tensor_single_scalar(o, i, s_, ALU.is_ge), reads=[mt], writes=[x2t])
            P.op("dve", lambda e, o=mask1.ap, i=mask1.ap, s_=w1:
                 e.tensor_single_scalar(o, i, s_, ALU.mult), reads=[mt], writes=[mask1])
            P.op("dve", lambda e, o=coeff.ap[:, tc * 16:(tc + 1) * 16], i=x2t.ap, s_=w2,
                 a=mask1.ap: e.scalar_tensor_tensor(o, i, s_, a, ALU.mult, ALU.add),
                 reads=[mt, x2t, mask1], writes=[coeff])
        b3 = ps.banks[3]
        P.mm_group([
            (lambda e, o=ps.bank(3, 128, off=tc * 128), l=coeff.ap[:, tc * 16:(tc + 1) * 16],
             r=identb.ap: e.matmul(o[0:16, :], l, r, start=True, stop=True))
            for tc in range(TS // 128)], reads=[coeff, identb], out_buf=b3)
        P.op("dve", lambda e, o=ctb.ap[0:16, :], i=ps.bank(3, TS)[0:16, :]: e.tensor_copy(o, i),
             reads=[b3], writes=[ctb])
        b4 = ps.banks[4]
        for er in range(2):
            P.mm_group([
                (lambda e, o=ps.bank(5 + er, TS),
                 l=atb.ap[:, ci * 256 + er * 128: ci * 256 + (er + 1) * 128], r=xn[s][ci].ap,
                 st=(ci == 0), sp=(ci == NCI - 1): e.matmul(o, l, r, start=st, stop=sp))
                for ci in range(NCI)], reads=[atb] + xn[s], out_buf=ps.banks[5 + er])
            P.mm_group([lambda e, o=ps.bank(4, TS),
                        l=eexpb.ap[0:16, er * 128:(er + 1) * 128], r=ctb.ap[0:16, :]:
                        e.matmul(o, l, r, start=True, stop=True)],
                       reads=[eexpb, ctb], out_buf=b4)
            P.op("dve", lambda e, o=cexp.ap, i=ps.bank(4, TS): e.tensor_copy(o, i),
                 reads=[b4], writes=[cexp])
            P.op("dve", lambda e, o=wxa[er].ap[:, s * TS:(s + 1) * TS], a=ps.bank(5 + er, TS),
                 b=cexp.ap: e.tensor_mul(o, a, b),
                 reads=[ps.banks[5 + er], cexp], writes=[wxa[er]])

    for dc in range(NCI):
        par = dc % 2
        wb = wbuf[par]
        P.dma("pool", wb.ap.rearrange("p (ci e) -> p ci e", ci=NCI),
              wt_r[:, :, dc * 128:(dc + 1) * 128], writes=[wb])
        for s in range(NSL):
            tsl = slice(s * TS, (s + 1) * TS)
            xr = xs[dc % len(xs)]
            P.dma("sp", xr.ap, xin[dc * 128:(dc + 1) * 128, tsl], writes=[xr], deps=[P.xdep])
            bD = ps.banks[par]
            mms = [
                (lambda e, o=ps.bank(par, TS), l=wb.ap[:, ci * 128:(ci + 1) * 128],
                 r=xn[s][ci].ap, st=(ci == 0): e.matmul(o, l, r, start=st, stop=False))
                for ci in range(NCI)]
            mms.append(lambda e, o=ps.bank(par, TS), l=biasb.ap[0:1, dc * 128:(dc + 1) * 128],
                       r=sh.ones_row_bf.ap[0:1, 0:TS]: e.matmul(o, l, r, start=False, stop=False))
            for er in range(2):
                mms.append(lambda e, o=ps.bank(par, TS),
                           l=breb.ap[:, er * C + dc * 128: er * C + (dc + 1) * 128],
                           r=wxa[er].ap[:, s * TS:(s + 1) * TS], sp=(er == 1):
                           e.matmul(o, l, r, start=False, stop=sp))
            P.mm_group(mms, reads=[wb, biasb, sh.ones_row_bf, breb, wxa[0], wxa[1]] + xn[s],
                       out_buf=bD)
            xo = xnew[par]
            P.op("dve", lambda e, o=xo.ap, i=ps.bank(par, TS), x=xr.ap:
                 e.scalar_tensor_tensor(o, i, 0.5, x, ALU.mult, ALU.add),
                 reads=[bD, xr], writes=[xo])
            P.dma("sp", xout[dc * 128:(dc + 1) * 128, tsl], xo.ap, reads=[xo], deps=[P.xdep])
    af32.pop(P.stage_floor()); abf.pop(P.stage_floor())


def prep_ffn0(inputs, pre):
    f32 = np.float32
    pre["f0_wt"] = np.ascontiguousarray(inputs["L0_ffn_W"].T).astype(f32)
    pre["f0_bias"] = inputs["L0_ffn_bias"].reshape(1, C).astype(f32)
    pre["f0_rw1t"] = np.ascontiguousarray(inputs["L0_ffn_rw1"].T).astype(f32)
    pre["f0_rb1c"] = inputs["L0_ffn_rb1"].reshape(64, 1).astype(f32)
    pre["f0_rw2t"] = np.ascontiguousarray(inputs["L0_ffn_rw2"].T).astype(f32)
    pre["f0_rb2"] = inputs["L0_ffn_rb2"].reshape(1, 32).astype(f32)
    pre["f0_gates"] = inputs["L0_ffn_gates"].reshape(1, 16).astype(f32)
    pre["f0_at"] = np.ascontiguousarray(inputs["L0_ffn_A"].reshape(256, C).T).astype(f32)
    pre["f0_bre"] = np.ascontiguousarray(
        inputs["L0_ffn_B"].transpose(0, 2, 1).reshape(256, C)).astype(f32)
    ident = np.eye(128, dtype=f32)
    pre["ident"] = ident
    eexp = np.zeros((16, 256), dtype=f32)
    for e_ in range(16):
        eexp[e_, e_ * 16:(e_ + 1) * 16] = 1.0
    pre["f0_eexp"] = eexp


FFN0_NAMES = ["f0_wt", "f0_bias", "f0_rw1t", "f0_rb1c", "f0_rw2t", "f0_rb2", "f0_gates",
              "f0_at", "f0_bre", "ident", "f0_eexp"]


def declare_drams(nc, pre, names):
    d = {}
    for n in names:
        a = pre[n]
        dt = {np.dtype(np.float32): F32, np.dtype(np.int32): I32}[a.dtype]
        d[n] = nc.dram_tensor(n, list(a.shape), dt, kind="ExternalInput")
    return d


def build_single_stage(stage_fns, names, scratch=(), sizes=(15000, 72000), x2=False):
    nc = bass.Bass("TRN2", target_bir_lowering=False)
    xin_t = nc.dram_tensor("x_in", [C, T], F32, kind="ExternalInput")
    xout_t = nc.dram_tensor("x_out", [C, T], F32, kind="ExternalOutput")
    P = Prog(nc)
    ps = Psum(nc)
    af32 = Arena(nc, "af32", sizes[0], F32)
    abf = Arena(nc, "abf", sizes[1], BF16)
    sh = Shared(P, af32, abf)
    d = declare_drams(nc, PRE_CACHE, names)
    if x2:
        d["a1_xorig"] = nc.dram_tensor("x_in2", [C, T], F32, kind="ExternalInput")
    for sc_ in scratch:
        n, shp = sc_[0], sc_[1]
        dt = F32 if (len(sc_) > 2 and sc_[2] == "f32") else BF16
        d[n] = nc.dram_tensor(n, list(shp), dt, kind="Internal")
    for fn in stage_fns:
        fn(P, ps, sh, af32, abf, d, xin_t.ap(), xout_t.ap())
    P.finalize([(s, P.cnt[s]) for s in P.cnt])
    return nc


PRE_CACHE = {}


def kernel(**inputs):
    pre = PRE_CACHE
    pre.clear()
    for prep in PREPS:
        prep(inputs, pre)
    x = inputs["L0_attn_x"]
    xf = [np.ascontiguousarray(x[b].T).astype(np.float32) for b in range(8)]
    hist = [xf]
    for spec in LAUNCHES:
        stage_fns, names, scratch = spec[0], spec[1], spec[2]
        sizes = spec[3] if len(spec) > 3 else (15000, 72000)
        x2_from = spec[4] if len(spec) > 4 else None
        nc = build_single_stage(stage_fns, names, scratch, sizes, x2=(x2_from is not None))
        in_maps = []
        for b in range(8):
            m = {n: pre[n] for n in names}
            m["x_in"] = hist[-1][b]
            if x2_from is not None:
                m["x_in2"] = hist[x2_from][b]
            in_maps.append(m)
        res = run_bass_kernel_spmd(nc, in_maps, core_ids=list(range(8)))
        hist.append([np.ascontiguousarray(res.results[b]["x_out"]) for b in range(8)])
    xf = hist[-1]
    out = np.stack([xf[b].T for b in range(8)])
    return np.ascontiguousarray(out).astype(np.float32)


PREPS = [prep_ffn0]
LAUNCHES = [([emit_ffn0], FFN0_NAMES, ())]


def _default_chain():
    global PREPS, LAUNCHES
    PREPS = [prep_ffn0, prep_attn0, prep_ffn1, prep_attn1]
    LAUNCHES = [
        ([emit_ffn0], FFN0_NAMES, (), (15000, 72000)),
        ([emit_attn0], ATTN0_NAMES, ATTN0_SCRATCH, (18000, 70000)),
        ([emit_ffn0], FFN0_NAMES, (), (15000, 72000)),
        ([emit_wconv, emit_ffn1], FFN1_NAMES, FFN1_SCRATCH, (15000, 72000)),
        ([emit_wconv_a1a, emit_attn1a], ATTN1A_NAMES, ATTN1A_SCRATCH, (19500, 64000)),
        ([emit_wconv_a1b, emit_attn1b], ATTN1B_NAMES, ATTN1B_SCRATCH, (17000, 70500), 4),
        ([emit_wconv, emit_ffn1], FFN1_NAMES, FFN1_SCRATCH, (15000, 72000)),
    ]




def emit_wconv(P, ps, sh, af32, abf, d, xin, xout, jobs=None):
    af32.push(); abf.push()
    bin_ = abf.bufs(2, 8192)
    bout = abf.bufs(2, 8192)
    if jobs is None:
        jobs = [("f1_gq", "f1_gbf", 2048, 8192), ("f1_uq", "f1_ubf", 2048, 8192),
                ("f1_dq", "f1_dbf", 8192, 2048)]
    k = 0
    for srcn, dst, n, m in jobs:
        for r in range(n // 128):
            bi, bo = bin_[k % 2], bout[k % 2]
            k += 1
            P.dma("pool", bi.ap[:, 0:m], d[srcn][r * 128:(r + 1) * 128, :], writes=[bi])
            eng = "dve" if k % 2 == 0 else "pool"
            P.op(eng, lambda e, o=bo.ap[:, 0:m], t=bi.ap[:, 0:m]:
                 e.tensor_scalar_add(o, t, -128.0), reads=[bi], writes=[bo])
            P.dma("act", d[dst][:, :, r, :].rearrange("oc p e -> p oc e"),
                  bo.ap[:, 0:m].rearrange("p (oc e) -> p oc e", e=128),
                  reads=[bo], writes=[P.drambuf(dst)])
    af32.pop(P.stage_floor()); abf.pop(P.stage_floor())


def emit_ffn1(P, ps, sh, af32, abf, d, xin, xout):
    af32.push(); abf.push()
    gsc = af32.buf(64)
    usc = af32.buf(64)
    dsc = af32.buf(16)
    P.dma("sp", gsc.ap, d["f1_gsc"].ap(), writes=[gsc])
    P.dma("sp", usc.ap, d["f1_usc"].ap(), writes=[usc])
    P.dma("sp", dsc.ap, d["f1_dsc"].ap(), writes=[dsc])
    P.op("dve", lambda e, t=dsc.ap: e.tensor_scalar_mul(t, t, 0.5), reads=[dsc], writes=[dsc])

    xs = af32.bufs(NCI, TS)
    xn = abf.bufs(NCI, TS)
    h = abf.bufs(64, TS)
    wg = [abf.buf(NCI * 128) for _ in range(2)]
    wu = [abf.buf(NCI * 128) for _ in range(2)]
    wd = [abf.buf(64 * 128) for _ in range(2)]
    hg = af32.bufs(2, TS)
    xnew = af32.bufs(2, TS)

    gq = d["f1_gbf"]
    uq = d["f1_ubf"]
    dq = d["f1_dbf"]

    for s in range(NSL):
        tsl = slice(s * TS, (s + 1) * TS)
        for ci in range(NCI):
            P.dma("sp", xs[ci].ap, xin[ci * 128:(ci + 1) * 128, tsl], writes=[xs[ci]], deps=[P.xdep])
        emit_norm(P, ps, sh, xs, xn)
        for ic in range(64):
            par = ic % 2
            wgb, wub = wg[par], wu[par]
            P.dma("sp", wgb.ap.rearrange("p (hc e) -> p hc e", hc=NCI),
                  gq[ic], reads=[P.drambuf("f1_gbf")], writes=[wgb])
            P.dma("sp", wub.ap.rearrange("p (hc e) -> p hc e", hc=NCI),
                  uq[ic], reads=[P.drambuf("f1_ubf")], writes=[wub])
            bG, bU = ps.banks[par], ps.banks[2 + par]
            P.mm_group([
                (lambda e, o=ps.bank(par, TS), l=wgb.ap[:, hc * 128:(hc + 1) * 128],
                 r=xn[hc].ap, st=(hc == 0), sp=(hc == NCI - 1):
                 e.matmul(o, l, r, start=st, stop=sp)) for hc in range(NCI)],
                reads=[wgb] + xn, out_buf=bG)
            P.mm_group([
                (lambda e, o=ps.bank(2 + par, TS), l=wub.ap[:, hc * 128:(hc + 1) * 128],
                 r=xn[hc].ap, st=(hc == 0), sp=(hc == NCI - 1):
                 e.matmul(o, l, r, start=st, stop=sp)) for hc in range(NCI)],
                reads=[wub] + xn, out_buf=bU)
            hgb = hg[par]
            P.op("act", lambda e, o=hgb.ap, i=ps.bank(par, TS), sc=gsc.ap[:, ic:ic + 1]:
                 e.activation(o, i, AF.Silu, scale=sc), reads=[bG, gsc], writes=[hgb])
            P.op("dve", lambda e, o=h[ic].ap, u=ps.bank(2 + par, TS), sc=usc.ap[:, ic:ic + 1],
                 g=hgb.ap: e.scalar_tensor_tensor(o, u, sc, g, ALU.mult, ALU.mult),
                 reads=[bU, usc, hgb], writes=[h[ic]])
        for hc in range(NCI):
            par = hc % 2
            wdb = wd[par]
            P.dma("sp", wdb.ap.rearrange("p (g e) -> p g e", g=64),
                  dq[hc], reads=[P.drambuf("f1_dbf")], writes=[wdb])
            bD = ps.banks[4 + par]
            P.mm_group([
                (lambda e, o=ps.bank(4 + par, TS), l=wdb.ap[:, ic * 128:(ic + 1) * 128],
                 r=h[ic].ap, st=(ic == 0), sp=(ic == 63):
                 e.matmul(o, l, r, start=st, stop=sp)) for ic in range(64)],
                reads=[wdb] + h, out_buf=bD)
            xo = xnew[par]
            P.op("dve", lambda e, o=xo.ap, i=ps.bank(4 + par, TS), sc=dsc.ap[:, hc:hc + 1],
                 x=xs[hc].ap: e.scalar_tensor_tensor(o, i, sc, x, ALU.mult, ALU.add),
                 reads=[bD, dsc, xs[hc]], writes=[xo])
            P.dma("sp", xout[hc * 128:(hc + 1) * 128, tsl], xo.ap, reads=[xo], deps=[P.xdep])
    af32.pop(P.stage_floor()); abf.pop(P.stage_floor())


def prep_ffn1(inputs, pre):
    f32 = np.float32
    pre["f1_gq"] = np.ascontiguousarray(inputs["L1_ffn_gate_w"].T).astype(np.int32)
    pre["f1_uq"] = np.ascontiguousarray(inputs["L1_ffn_up_w"].T).astype(np.int32)
    pre["f1_dq"] = np.ascontiguousarray(inputs["L1_ffn_down_w"].T).astype(np.int32)
    pre["f1_gsc"] = np.ascontiguousarray(inputs["L1_ffn_gate_scale"].reshape(64, 128).T).astype(f32)
    pre["f1_usc"] = np.ascontiguousarray(inputs["L1_ffn_up_scale"].reshape(64, 128).T).astype(f32)
    pre["f1_dsc"] = np.ascontiguousarray(inputs["L1_ffn_down_scale"].reshape(16, 128).T).astype(f32)


FFN1_NAMES = ["f1_gq", "f1_uq", "f1_dq", "f1_gsc", "f1_usc", "f1_dsc"]
FFN1_SCRATCH = [("f1_gbf", [64, 128, 16, 128]), ("f1_ubf", [64, 128, 16, 128]),
                ("f1_dbf", [16, 128, 64, 128])]



EH = [h for h in range(32) if (h // 4) % 2 == 0]
OH = [h for h in range(32) if (h // 4) % 2 == 1]
QC = 512


def emit_attn0(P, ps, sh, af32, abf, d, xin, xout):
    af32.push(); abf.push()
    kt4 = abf.bufs(4, T)
    vt = abf.bufs(NCI, 8 * 65)
    xs = af32.bufs(NCI, TS)
    xn = abf.bufs(NCI, TS)
    wq = [abf.buf(NCI * 128) for _ in range(2)]
    wvb = abf.bufs(2, 512)
    t1k = af32.buf(TS); t2k = af32.buf(TS)
    qf = af32.bufs(2, TS)
    qsw = af32.bufs(2, TS)
    qob = abf.bufs(2, TS)
    qrow = af32.buf(40)
    brow = af32.buf(34)
    bcol = af32.buf(32)
    bcsb = af32.buf(QC)
    rrow = af32.buf(QC)
    xnew = af32.bufs(2, TS)
    bqrow = abf.buf(C)
    bkrow = abf.buf(512)
    bvrow = abf.buf(512)
    borow = abf.buf(C)
    qs = abf.bufs(16, QC)
    pta = abf.buf(8 * QC)
    ptb = abf.buf(8 * QC)
    att = abf.bufs(NCI, QC)
    wo = [abf.buf(NCI * 128) for _ in range(2)]
    abscr = qob[0]
    P.dma("pool", bqrow.ap[0:1, :], d["a0_bq"].ap(), writes=[bqrow])
    P.dma("pool", bkrow.ap[0:1, :], d["a0_bk"].ap(), writes=[bkrow])
    P.dma("pool", bvrow.ap[0:1, :], d["a0_bv"].ap(), writes=[bvrow])
    P.dma("pool", borow.ap[0:1, :], d["a0_bo"].ap(), writes=[borow])
    wv_r = d["a0_wvt"].rearrange("(ci p) e -> p ci e", p=128)
    for ktile in range(NCI):
        for hh in range(8):
            P.op("dve", lambda e, o=vt[ktile].ap[:, hh * 65 + 64: hh * 65 + 65]:
                 e.memset(o, 1.0), writes=[vt[ktile]])

    wq_r = d["a0_wqt"].rearrange("(ci p) e -> p ci e", p=128)
    wk_r = d["a0_wkt"].rearrange("(ci p) e -> p ci e", p=128)
    wo_r = d["a0_wot"].rearrange("(ci p) e -> p ci e", p=128)
    qscr = d["a0_qscr"]

    qm = qrow.ap[0:1, 0:32]
    km = qrow.ap[0:1, 32:40]
    P.op("dve", lambda e, o=qrow.ap[0:1, :]: e.memset(o, 0.0), writes=[qrow])

    for s in range(NSL):
        tsl = slice(s * TS, (s + 1) * TS)
        for ci in range(NCI):
            P.dma("sp", xs[ci].ap, xin[ci * 128:(ci + 1) * 128, tsl], writes=[xs[ci]], deps=[P.xdep])
        emit_norm(P, ps, sh, xs, xn)
        for tb, srcn in ((t1k, "a0_t1k"), (t2k, "a0_t2k")):
            P.dma("sp", tb.ap, d[srcn][:, tsl], writes=[tb])
        for j in range(20):
            isk = j >= 16
            m = j - 16 if isk else j
            wb = wq[j % 2]
            P.dma("pool", wb.ap.rearrange("p (ci e) -> p ci e", ci=NCI),
                  (wk_r if isk else wq_r)[:, :, m * 128:(m + 1) * 128], writes=[wb])
            bank = ps.banks[j % 2]
            mms = [
                (lambda e, o=ps.bank(j % 2, TS), l=wb.ap[:, ci * 128:(ci + 1) * 128],
                 r=xn[ci].ap, st=(ci == 0): e.matmul(o, l, r, start=st, stop=False))
                for ci in range(NCI)]
            br = bkrow if isk else bqrow
            mms.append(lambda e, o=ps.bank(j % 2, TS), l=br.ap[0:1, m * 128:(m + 1) * 128],
                       r=sh.ones_row_bf.ap[0:1, 0:TS]: e.matmul(o, l, r, start=False, stop=True))
            P.mm_group(mms, reads=[wb, br, sh.ones_row_bf] + xn, out_buf=bank)
            qfb, qsb = qf[j % 2], qsw[j % 2]
            P.op("act", lambda e, o=qfb.ap, i=ps.bank(j % 2, TS): e.copy(o, i),
                 reads=[bank], writes=[qfb])
            for blk in range(4):
                sw = (blk ^ 1) * 32
                P.dma("sp", qsb.ap[sw:sw + 32, :], qfb.ap[blk * 32:(blk + 1) * 32, :],
                      reads=[qfb], writes=[qsb])
            T1, T2 = t1k, t2k
            dst = kt4[m] if isk else qob[j % 2]
            dsl = tsl if isk else slice(0, TS)
            eng = "dve" if j % 2 == 0 else "pool"
            scr = sh.scr[eng]
            P.op(eng, lambda e, o=scr.ap[:, 0:TS], a=qfb.ap, b=T1.ap: e.tensor_mul(o, a, b),
                 reads=[qfb, T1], writes=[scr])
            P.op(eng, lambda e, o=qsb.ap, a=qsb.ap, b=T2.ap: e.tensor_mul(o, a, b),
                 reads=[T2, qsb], writes=[qsb])
            P.op(eng, lambda e, o=dst.ap[:, dsl], a=scr.ap[:, 0:TS], b=qsb.ap:
                 e.tensor_add(o, a, b), reads=[scr, qsb], writes=[dst])
            if not isk:
                P.dma("sp", qscr[m * 128:(m + 1) * 128, tsl], dst.ap,
                      reads=[dst], writes=[P.drambuf("a0_qscr")])
            q2 = sh.x2b[j % 2]
            P.op("act", lambda e, o=q2.ap, i=dst.ap[:, dsl]: e.activation(o, i, AF.Square),
                 reads=[dst], writes=[q2])
            for half in range(2):
                bnk = ps.banks[4 + half]
                tk = P.op("pe", lambda e, o=ps.bank(4 + half, TS),
                          l=sh.ones_col_bf.ap[half * 64:(half + 1) * 64, :],
                          r=q2.ap[half * 64:(half + 1) * 64, :]:
                          e.matmul(o[0:1, :], l, r, start=True, stop=True),
                          deps=[q2.w, bnk.w] + bnk.rs, inc=True)
                bnk.w = tk; bnk.rs = []
                q2.rs.append(tk)
                if isk:
                    tgt = km[0:1, 2 * m + half:2 * m + half + 1]
                else:
                    tgt = qm[0:1, (EH if half == 0 else OH)[m]:(EH if half == 0 else OH)[m] + 1]
                P.op("dve", lambda e, o=sh.rows.ap[0:1, 0:1], i=ps.bank(4 + half, TS)[0:1, :]:
                     e.reduce_max(o, i, axis=AX.X), reads=[bnk], writes=[sh.rows])
                P.op("dve", lambda e, o=tgt, a=tgt, b=sh.rows.ap[0:1, 0:1]:
                     e.tensor_max(o, a, b), reads=[sh.rows], writes=[qrow])
        for ci in range(NCI):
            wvc = wvb[ci % 2]
            P.dma("pool", wvc.ap, wv_r[:, ci, :], writes=[wvc])
            for v4 in range(4):
                bnk = ps.banks[v4]
                tk = P.op("pe", lambda e, o=ps.bank(v4, 512),
                          l=xn[ci].ap[:, v4 * 128:(v4 + 1) * 128], r=wvc.ap,
                          st=(ci == 0): e.matmul(o, l, r, start=st, stop=False),
                          deps=([bnk.w] + bnk.rs if ci == 0 else []) + [wvc.w, xn[ci].w],
                          inc=(v4 == 3))
                if ci == 0:
                    bnk.rs = []
            wvc.rs.append(("pe", P.cnt["pe"]))
        for v4 in range(4):
            ktile = s * 4 + v4
            bnk = ps.banks[v4]
            tk = P.op("pe", lambda e, o=ps.bank(v4, 512), l=sh.ones_row_bf.ap[0:1, 0:128],
                      r=bvrow.ap[0:1, :]: e.matmul(o, l, r, start=False, stop=True),
                      deps=[bvrow.w, sh.ones_row_bf.w], inc=True)
            bnk.w = tk
            for hh in range(8):
                P.op("dve", lambda e, o=vt[ktile].ap[:, hh * 65:hh * 65 + 64],
                     i=ps.bank(v4, 64, off=hh * 64): e.tensor_copy(o, i),
                     reads=[bnk], writes=[vt[ktile]])
    P.op("act", lambda e, o=qm, i=qm: e.activation(o, i, AF.Sqrt), reads=[qrow], writes=[qrow])
    P.op("act", lambda e, o=km, i=km: e.activation(o, i, AF.Sqrt), reads=[qrow], writes=[qrow])
    for kv in range(8):
        grp = [h for h in range(32) if h // 4 == kv]
        for h in grp:
            P.op("dve", lambda e, o=brow.ap[0:1, h:h + 1], a=qm[0:1, h:h + 1] if False else
                 qrow.ap[0:1, h:h + 1], b=qrow.ap[0:1, 32 + kv:33 + kv]:
                 e.tensor_mul(o, a, b), reads=[qrow], writes=[brow])
    P.op("dve", lambda e, o=brow.ap[0:1, 0:32], i=brow.ap[0:1, 0:32]:
         e.tensor_scalar_mul(o, i, -1.0 / 8.0), reads=[brow], writes=[brow])
    bb6 = ps.banks[6]
    P.mm_group([lambda e, o=ps.bank(6, 32), l=sh.ones_row_f32.ap[0:1, 0:128],
                r=brow.ap[0:1, 0:32]: e.matmul(o, l, r, start=True, stop=True)],
               reads=[brow, sh.ones_row_f32], out_buf=bb6)
    P.op("act", lambda e, o=bcol.ap, i=ps.bank(6, 32): e.copy(o, i), reads=[bb6], writes=[bcol])

    qscr_r = qscr.rearrange("(m p) t -> p m t", p=128)
    for qc in range(T // QC):
        qsl = slice(qc * QC, (qc + 1) * QC)
        for m in range(16):
            P.dma("sp", qs[m].ap, qscr_r[:, m, qsl], reads=[P.drambuf("a0_qscr")],
                  writes=[qs[m]])
        for m in range(16):
            ha, hb = EH[m], OH[m]
            ka, kb = ha // 4, hb // 4
            for gh in range(2):
                for g8 in range(8):
                    kk = 8 * gh + g8
                    bs = 2 * (g8 % 2)
                    bA, bB = ps.banks[bs], ps.banks[bs + 1]
                    tA = P.op("pe", lambda e, o=ps.bank(bs, QC),
                              l=kt4[ka // 2].ap[0:64, kk * 128:(kk + 1) * 128],
                              r=qs[m].ap[0:64, :]:
                              e.matmul(o, l, r, start=True, stop=True),
                              deps=[kt4[ka // 2].w, qs[m].w, bA.w] + bA.rs, inc=True)
                    bA.w = tA
                    bA.rs = []
                    tB = P.op("pe", lambda e, o=ps.bank(bs + 1, QC),
                              l=kt4[kb // 2].ap[64:128, kk * 128:(kk + 1) * 128],
                              r=qs[m].ap[64:128, :]:
                              e.matmul(o, l, r, start=True, stop=True),
                              deps=[kt4[kb // 2].w, qs[m].w, bB.w] + bB.rs, inc=True)
                    bB.w = tB
                    bB.rs = []
                    P.op("act", lambda e, o=pta.ap[:, g8 * QC:(g8 + 1) * QC],
                         i=ps.bank(bs, QC), b=bcol.ap[:, ha:ha + 1]:
                         e.activation(o, i, AF.Exp, bias=b, scale=1.0 / 8.0),
                         reads=[bA, bcol], writes=[pta])
                    P.op("act", lambda e, o=ptb.ap[:, g8 * QC:(g8 + 1) * QC],
                         i=ps.bank(bs + 1, QC), b=bcol.ap[:, hb:hb + 1]:
                         e.activation(o, i, AF.Exp, bias=b, scale=1.0 / 8.0),
                         reads=[bB, bcol], writes=[ptb])
                for half, (pt, kv, bnk) in enumerate(((pta, ka, 4), (ptb, kb, 5))):
                    P.mm_group([
                        (lambda e, o=ps.bank(bnk, QC)[0:65, :],
                         l=vt[8 * gh + k8].ap[:, kv * 65:(kv + 1) * 65],
                         r=pt.ap[:, k8 * QC:(k8 + 1) * QC],
                         st=(gh == 0 and k8 == 0), sp=(gh == 1 and k8 == 7):
                         e.matmul(o, l, r, start=st, stop=sp)) for k8 in range(8)],
                        reads=[pt] + vt, out_buf=ps.banks[bnk])
            for half, bnk in ((0, 4), (1, 5)):
                P.op("dve", lambda e, o=rrow.ap[0:1, :], i=ps.bank(bnk, QC)[64:65, :]:
                     e.reciprocal(o, i), reads=[ps.banks[bnk]], writes=[rrow])
                bc = ps.banks[6]
                P.mm_group([lambda e, o=ps.bank(6, QC), l=sh.ones_row_f32.ap[0:1, 0:128],
                            r=rrow.ap[0:1, :]: e.matmul(o, l, r, start=True, stop=True)],
                           reads=[rrow, sh.ones_row_f32], out_buf=bc)
                P.op("act", lambda e, o=bcsb.ap, i=ps.bank(6, QC): e.copy(o, i),
                     reads=[bc], writes=[bcsb])
                if half == 0:
                    P.op("dve", lambda e, o=att[m].ap[0:64, :], a=ps.bank(bnk, QC)[0:64, :],
                         b=bcsb.ap[0:64, :]: e.tensor_mul(o, a, b),
                         reads=[ps.banks[bnk], bcsb], writes=[att[m]])
                else:
                    P.op("dve", lambda e, o=abscr.ap[0:64, :], a=ps.bank(bnk, QC)[0:64, :],
                         b=bcsb.ap[0:64, :]: e.tensor_mul(o, a, b),
                         reads=[ps.banks[bnk], bcsb], writes=[abscr])
                    P.dma("sp", att[m].ap[64:128, :], abscr.ap[0:64, :],
                          reads=[abscr], writes=[att[m]])
        for dc in range(NCI):
            par = dc % 2
            wb = wo[par]
            P.dma("pool", wb.ap.rearrange("p (ci e) -> p ci e", ci=NCI),
                  wo_r[:, :, dc * 128:(dc + 1) * 128], writes=[wb])
            xr = xs[dc]
            P.dma("sp", xr.ap[:, 0:QC], xin[dc * 128:(dc + 1) * 128, qsl], writes=[xr], deps=[P.xdep])
            bD = ps.banks[6 + par]
            mms = [
                (lambda e, o=ps.bank(6 + par, QC), l=wb.ap[:, ci * 128:(ci + 1) * 128],
                 r=att[ci].ap, st=(ci == 0): e.matmul(o, l, r, start=st, stop=False))
                for ci in range(NCI)]
            mms.append(lambda e, o=ps.bank(6 + par, QC), l=borow.ap[0:1, dc * 128:(dc + 1) * 128],
                       r=sh.ones_row_bf.ap[0:1, 0:QC]: e.matmul(o, l, r, start=False, stop=True))
            P.mm_group(mms, reads=[wb, borow, sh.ones_row_bf] + att, out_buf=bD)
            xo = xnew[par]
            P.op("dve", lambda e, o=xo.ap[:, 0:QC], i=ps.bank(6 + par, QC), x=xr.ap[:, 0:QC]:
                 e.tensor_add(o, i, x), reads=[bD, xr], writes=[xo])
            P.dma("sp", xout[dc * 128:(dc + 1) * 128, qsl], xo.ap[:, 0:QC], reads=[xo], deps=[P.xdep])
    af32.pop(P.stage_floor()); abf.pop(P.stage_floor())


def _perm64():
    return [2 * i for i in range(32)] + [2 * i + 1 for i in range(32)]


def prep_attn0(inputs, pre):
    f32 = np.float32
    p64 = _perm64()
    qperm = []
    for m in range(16):
        for h in (EH[m], OH[m]):
            qperm.extend([h * 64 + dd for dd in p64])
    kperm = []
    for kv in range(8):
        kperm.extend([kv * 64 + dd for dd in p64])
    operm = []
    for m in range(16):
        for h in (EH[m], OH[m]):
            operm.extend([h * 64 + dd for dd in range(64)])
    wq = inputs["L0_attn_W_q"]
    wkv = inputs["L0_attn_W_kv"]
    wo = inputs["L0_attn_W_out"]
    pre["a0_wqt"] = np.ascontiguousarray(wq[qperm, :].T).astype(f32)
    pre["a0_wkt"] = np.ascontiguousarray(wkv[0:512][kperm, :].T).astype(f32)
    pre["a0_wvt"] = np.ascontiguousarray(wkv[512:1024].T).astype(f32)
    pre["a0_wot"] = np.ascontiguousarray(wo[:, operm].T).astype(f32)
    pre["a0_bq"] = inputs["L0_attn_b_q"][qperm].reshape(1, C).astype(f32)
    pre["a0_bk"] = inputs["L0_attn_b_kv"][0:512][kperm].reshape(1, 512).astype(f32)
    pre["a0_bo"] = inputs["L0_attn_b_out"].reshape(1, C).astype(f32)
    pre["a0_bv"] = inputs["L0_attn_b_kv"][512:1024].reshape(1, 512).astype(f32)
    pos = np.arange(T, dtype=np.float64)
    fr = 1.0 / 10000.0 ** (np.arange(0, 64, 2, dtype=np.float64) / 64.0)
    th = pos[None, :] * fr[:, None]
    cos = np.cos(th); sin = np.sin(th)
    T1 = np.tile(cos, (4, 1)).astype(f32)
    T2 = np.concatenate([-sin, sin, -sin, sin], axis=0).astype(f32)
    pre["a0_t1k"] = T1
    pre["a0_t2k"] = T2


ATTN0_NAMES = ["a0_wqt", "a0_wkt", "a0_wvt", "a0_wot", "a0_bq", "a0_bk", "a0_bo", "a0_bv",
               "a0_t1k", "a0_t2k"]


ATTN0_SCRATCH = [("a0_qscr", [2048, 2048])]




def _wconv_cast(P, abf, d, jobs):
    abf.push()
    buf = abf.bufs(2, 8192)
    k = 0
    for srcn, dstn, n, m in jobs:
        for r in range(n // 128):
            b = buf[k % 2]
            k += 1
            P.dma("pool", b.ap[:, 0:m], d[srcn][r * 128:(r + 1) * 128, :], writes=[b])
            P.dma("act", d[dstn][:, :, r, :].rearrange("oc p e -> p oc e"),
                  b.ap[:, 0:m].rearrange("p (oc e) -> p oc e", e=128),
                  reads=[b], writes=[P.drambuf(dstn)])
    abf.pop(P.stage_floor())


def emit_wconv_a1a(P, ps, sh, af32, abf, d, xin, xout):
    _wconv_cast(P, abf, d, [("a1_wkt", "a1_wkb", C, C), ("a1_wvt", "a1_wvb", C, C),
                            ("a1_wrt", "a1_wrb", C, C), ("a1_wot", "a1_wob", C, C)])


def emit_wconv_a1b(P, ps, sh, af32, abf, d, xin, xout):
    _wconv_cast(P, abf, d, [("a1_wckt", "a1_wckb", C, 8192), ("a1_wcvt", "a1_wcvb", 8192, C),
                            ("a1_wcrt", "a1_wcrb", C, C)])


def _attn1_cols(P, af32, d, names):
    cols = {}
    for n in names:
        cols[n] = af32.buf(16)
        P.dma("sp", cols[n].ap, d["a1_" + n].ap(), writes=[cols[n]])
    return cols


def emit_attn1a(P, ps, sh, af32, abf, d, xin, xout):
    af32.push(); abf.push()
    cols = _attn1_cols(P, af32, d, ("ln1w", "ln1b", "tmk", "tmv", "tmr", "decay", "first"))
    ew = af32.buf(16)
    eu = af32.buf(16)
    P.op("act", lambda e, o=ew.ap, i=cols["decay"].ap: e.activation(o, i, AF.Exp),
         reads=[cols["decay"]], writes=[ew])
    P.op("act", lambda e, o=ew.ap, i=ew.ap: e.activation(o, i, AF.Exp, scale=-1.0),
         reads=[ew], writes=[ew])
    P.op("act", lambda e, o=eu.ap, i=cols["first"].ap: e.activation(o, i, AF.Exp),
         reads=[cols["first"]], writes=[eu])

    xs = af32.bufs(NCI, TS)
    xn1 = abf.bufs(NCI, TS)
    xxt = abf.bufs(2, TS)
    xk = abf.bufs(NCI, TS)
    xv = abf.bufs(NCI, TS)
    xr = abf.bufs(NCI, TS)
    attin = abf.bufs(NCI, TS)
    wkb = [abf.buf(NCI * 128) for _ in range(4)]
    pcol1 = af32.buf(16)
    astate = af32.buf(16)
    bstate = af32.buf(16)
    kf = af32.bufs(1, TS); vf = af32.bufs(1, TS)
    ekt = af32.bufs(1, TS); ekv = af32.bufs(1, TS)
    ewt = af32.buf(TS)
    ahist = af32.bufs(1, TS + 1); bhist = af32.bufs(1, TS + 1)
    numt = af32.bufs(1, TS); dent = af32.bufs(1, TS)
    srt = [abf.buf(TS)]
    xmid = af32.bufs(1, TS)
    for b_ in (pcol1, astate, bstate):
        P.op("dve", lambda e, o=b_.ap: e.memset(o, 0.0), writes=[b_])
    P.op("dve", lambda e, o=ewt.ap: e.memset(o, 0.0), writes=[ewt])

    wk_r = d["a1_wkt"].rearrange("(ci p) e -> p ci e", p=128)
    wv_r = d["a1_wvt"].rearrange("(ci p) e -> p ci e", p=128)
    wr_r = d["a1_wrt"].rearrange("(ci p) e -> p ci e", p=128)
    wo_r = d["a1_wot"].rearrange("(ci p) e -> p ci e", p=128)

    for s in range(NSL):
        tsl = slice(s * TS, (s + 1) * TS)
        for ci in range(NCI):
            P.dma("sp", xs[ci].ap, xin[ci * 128:(ci + 1) * 128, tsl], writes=[xs[ci]], deps=[P.xdep])
        emit_norm(P, ps, sh, xs, xs)
        emit_norm(P, ps, sh, xs, xn1, affine=(cols["ln1w"], cols["ln1b"]))
        for ci in range(NCI):
            xx = xxt[ci % 2]
            P.op("dve", lambda e, o=xx.ap[:, 1:TS], i=xn1[ci].ap[:, 0:TS - 1]:
                 e.tensor_copy(o, i), reads=[xn1[ci]], writes=[xx])
            P.op("dve", lambda e, o=xx.ap[:, 0:1], i=pcol1.ap[:, ci:ci + 1]:
                 e.tensor_copy(o, i), reads=[pcol1], writes=[xx])
            P.op("dve", lambda e, o=pcol1.ap[:, ci:ci + 1], i=xn1[ci].ap[:, TS - 1:TS]:
                 e.tensor_copy(o, i), reads=[xn1[ci]], writes=[pcol1])
            eng = "dve" if ci % 2 == 0 else "pool"
            scr = sh.scr[eng]
            P.op(eng, lambda e, o=scr.ap[:, 0:TS], a=xn1[ci].ap, b=xx.ap: e.tensor_sub(o, a, b),
                 reads=[xn1[ci], xx], writes=[scr])
            for dstb, cn in ((xk[ci], "tmk"), (xv[ci], "tmv"), (xr[ci], "tmr")):
                P.op("dve", lambda e, o=dstb.ap, i=scr.ap[:, 0:TS], sc=cols[cn].ap[:, ci:ci + 1],
                     x=xx.ap: e.scalar_tensor_tensor(o, i, sc, x, ALU.mult, ALU.add),
                     reads=[scr, cols[cn], xx], writes=[dstb])
        for dc in range(NCI):
            par = dc % 2
            for w_r, rhs, bank in ((wk_r, xk, 0), (wv_r, xv, 1), (wr_r, xr, 2)):
                wb = wkb[bank]
                P.dma("pool", wb.ap.rearrange("p (ci e) -> p ci e", ci=NCI),
                      w_r[:, :, dc * 128:(dc + 1) * 128], writes=[wb])
                P.mm_group([
                    (lambda e, o=ps.bank(bank, TS), l=wb.ap[:, ci * 128:(ci + 1) * 128],
                     r=rhs[ci].ap, st=(ci == 0), sp=(ci == NCI - 1):
                     e.matmul(o, l, r, start=st, stop=sp)) for ci in range(NCI)],
                    reads=[wb] + rhs, out_buf=ps.banks[bank])
            kb, vb, ekb, evb = kf[0], vf[0], ekt[0], ekv[0]
            ah, bh = ahist[0], bhist[0]
            nm, dn, sr = numt[0], dent[0], srt[0]
            P.op("act", lambda e, o=kb.ap, i=ps.bank(0, TS): e.copy(o, i),
                 reads=[ps.banks[0]], writes=[kb])
            P.op("act", lambda e, o=vb.ap, i=ps.bank(1, TS): e.copy(o, i),
                 reads=[ps.banks[1]], writes=[vb])
            P.op("act", lambda e, o=sr.ap, i=ps.bank(2, TS): e.activation(o, i, AF.Sigmoid),
                 reads=[ps.banks[2]], writes=[sr])
            P.op("act", lambda e, o=ekb.ap, i=kb.ap: e.activation(o, i, AF.Exp),
                 reads=[kb], writes=[ekb])
            P.op("dve", lambda e, o=evb.ap, a=ekb.ap, b=vb.ap: e.tensor_mul(o, a, b),
                 reads=[ekb, vb], writes=[evb])
            P.op("dve", lambda e, o=ewt.ap, i=ewt.ap, sc=ew.ap[:, dc:dc + 1]:
                 e.tensor_scalar(o, i, 0.0, sc, ALU.mult, ALU.add),
                 reads=[ew], writes=[ewt])
            P.op("dve", lambda e, o=ah.ap[:, 0:1], i=astate.ap[:, dc:dc + 1]:
                 e.tensor_copy(o, i), reads=[astate], writes=[ah])
            P.op("dve", lambda e, o=bh.ap[:, 0:1], i=bstate.ap[:, dc:dc + 1]:
                 e.tensor_copy(o, i), reads=[bstate], writes=[bh])
            P.op("dve", lambda e, o=ah.ap[:, 1:TS + 1], d0=ewt.ap, d1=evb.ap,
                 ini=astate.ap[:, dc:dc + 1]:
                 e.tensor_tensor_scan(o, d0, d1, ini, ALU.mult, ALU.add),
                 reads=[ewt, evb, astate], writes=[ah])
            P.op("dve", lambda e, o=bh.ap[:, 1:TS + 1], d0=ewt.ap, d1=ekb.ap,
                 ini=bstate.ap[:, dc:dc + 1]:
                 e.tensor_tensor_scan(o, d0, d1, ini, ALU.mult, ALU.add),
                 reads=[ewt, ekb, bstate], writes=[bh])
            P.op("dve", lambda e, o=astate.ap[:, dc:dc + 1], i=ah.ap[:, TS:TS + 1]:
                 e.tensor_copy(o, i), reads=[ah], writes=[astate])
            P.op("dve", lambda e, o=bstate.ap[:, dc:dc + 1], i=bh.ap[:, TS:TS + 1]:
                 e.tensor_copy(o, i), reads=[bh], writes=[bstate])
            P.op("dve", lambda e, o=nm.ap, i=evb.ap, sc=eu.ap[:, dc:dc + 1],
                 x=ah.ap[:, 0:TS]: e.scalar_tensor_tensor(o, i, sc, x, ALU.mult, ALU.add),
                 reads=[evb, eu, ah], writes=[nm])
            P.op("dve", lambda e, o=dn.ap, i=ekb.ap, sc=eu.ap[:, dc:dc + 1],
                 x=bh.ap[:, 0:TS]: e.scalar_tensor_tensor(o, i, sc, x, ALU.mult, ALU.add),
                 reads=[ekb, eu, bh], writes=[dn])
            P.op("dve", lambda e, o=dn.ap, i=dn.ap: e.reciprocal(o, i),
                 reads=[dn], writes=[dn])
            P.op("dve", lambda e, o=nm.ap, a=nm.ap, b=dn.ap: e.tensor_mul(o, a, b),
                 reads=[dn], writes=[nm])
            P.op("dve", lambda e, o=attin[dc].ap, a=nm.ap, b=sr.ap: e.tensor_mul(o, a, b),
                 reads=[nm, sr], writes=[attin[dc]])
        for dc in range(NCI):
            par = dc % 2
            wb = wkb[3]
            P.dma("pool", wb.ap.rearrange("p (ci e) -> p ci e", ci=NCI),
                  wo_r[:, :, dc * 128:(dc + 1) * 128], writes=[wb])
            bD = ps.banks[4 + par]
            P.mm_group([
                (lambda e, o=ps.bank(4 + par, TS), l=wb.ap[:, ci * 128:(ci + 1) * 128],
                 r=attin[ci].ap, st=(ci == 0), sp=(ci == NCI - 1):
                 e.matmul(o, l, r, start=st, stop=sp)) for ci in range(NCI)],
                reads=[wb] + attin, out_buf=bD)
            xo = xmid[0]
            P.op("dve", lambda e, o=xo.ap, i=ps.bank(4 + par, TS), z=xs[dc].ap:
                 e.tensor_add(o, i, z), reads=[bD, xs[dc]], writes=[xo])
            P.dma("sp", xout[dc * 128:(dc + 1) * 128, tsl], xo.ap, reads=[xo], deps=[P.xdep])
    af32.pop(P.stage_floor()); abf.pop(P.stage_floor())


def emit_attn1b(P, ps, sh, af32, abf, d, xin, xout):
    af32.push(); abf.push()
    cols = _attn1_cols(P, af32, d, ("ln2w", "ln2b", "cmk", "cmr"))
    xorig = d["a1_xorig"]
    xm = af32.bufs(NCI, TS)
    xn2 = abf.bufs(NCI, TS)
    xx2 = abf.bufs(1, TS)
    ck = abf.bufs(NCI, TS)
    cr = abf.bufs(NCI, TS)
    h = abf.bufs(64, TS)
    wck = [abf.buf(NCI * 128) for _ in range(2)]
    wcv = [abf.buf(16 * 128) for _ in range(2)]
    pcol2 = af32.buf(16)
    hgf = af32.bufs(1, TS)
    sgf = af32.bufs(1, TS)
    xrld = af32.bufs(2, TS)
    xnew = af32.bufs(2, TS)
    P.op("dve", lambda e, o=pcol2.ap: e.memset(o, 0.0), writes=[pcol2])
    wck_r = d["a1_wckt"].rearrange("(ci p) e -> p ci e", p=128)
    wcv_r = d["a1_wcvt"].rearrange("(g p) e -> p g e", p=128)
    wcr_r = d["a1_wcrt"].rearrange("(ci p) e -> p ci e", p=128)
    for s in range(NSL):
        tsl = slice(s * TS, (s + 1) * TS)
        for ci in range(NCI):
            P.dma("sp", xm[ci].ap, xin[ci * 128:(ci + 1) * 128, tsl], writes=[xm[ci]], deps=[P.xdep])
        emit_norm(P, ps, sh, xm, xn2, affine=(cols["ln2w"], cols["ln2b"]))
        for ci in range(NCI):
            xx = xx2[0]
            P.op("dve", lambda e, o=xx.ap[:, 1:TS], i=xn2[ci].ap[:, 0:TS - 1]:
                 e.tensor_copy(o, i), reads=[xn2[ci]], writes=[xx])
            P.op("dve", lambda e, o=xx.ap[:, 0:1], i=pcol2.ap[:, ci:ci + 1]:
                 e.tensor_copy(o, i), reads=[pcol2], writes=[xx])
            P.op("dve", lambda e, o=pcol2.ap[:, ci:ci + 1], i=xn2[ci].ap[:, TS - 1:TS]:
                 e.tensor_copy(o, i), reads=[xn2[ci]], writes=[pcol2])
            eng = "dve" if ci % 2 == 0 else "pool"
            scr = sh.scr[eng]
            P.op(eng, lambda e, o=scr.ap[:, 0:TS], a=xn2[ci].ap, b=xx.ap: e.tensor_sub(o, a, b),
                 reads=[xn2[ci], xx], writes=[scr])
            for dstb, cn in ((ck[ci], "cmk"), (cr[ci], "cmr")):
                P.op("dve", lambda e, o=dstb.ap, i=scr.ap[:, 0:TS], sc=cols[cn].ap[:, ci:ci + 1],
                     x=xx.ap: e.scalar_tensor_tensor(o, i, sc, x, ALU.mult, ALU.add),
                     reads=[scr, cols[cn], xx], writes=[dstb])
        for ic in range(64):
            par = ic % 2
            wb = wck[par]
            P.dma("pool", wb.ap.rearrange("p (ci e) -> p ci e", ci=NCI),
                  wck_r[:, :, ic * 128:(ic + 1) * 128], writes=[wb])
            bG = ps.banks[par]
            P.mm_group([
                (lambda e, o=ps.bank(par, TS), l=wb.ap[:, ci * 128:(ci + 1) * 128],
                 r=ck[ci].ap, st=(ci == 0), sp=(ci == NCI - 1):
                 e.matmul(o, l, r, start=st, stop=sp)) for ci in range(NCI)],
                reads=[wb] + ck, out_buf=bG)
            hb = hgf[0]
            P.op("act", lambda e, o=hb.ap, i=ps.bank(par, TS): e.activation(o, i, AF.Relu),
                 reads=[bG], writes=[hb])
            P.op("dve", lambda e, o=h[ic].ap, a=hb.ap, b=hb.ap: e.tensor_mul(o, a, b),
                 reads=[hb], writes=[h[ic]])
        for hc in range(NCI):
            par = hc % 2
            bK = ps.banks[2 + par]
            for gh2 in range(4):
                wb = wcv[gh2 % 2]
                P.dma("pool", wb.ap.rearrange("p (g e) -> p g e", g=16),
                      wcv_r[:, gh2 * 16:(gh2 + 1) * 16, hc * 128:(hc + 1) * 128],
                      writes=[wb])
                P.mm_group([
                    (lambda e, o=ps.bank(2 + par, TS), l=wb.ap[:, i2 * 128:(i2 + 1) * 128],
                     r=h[gh2 * 16 + i2].ap, st=(gh2 == 0 and i2 == 0),
                     sp=(gh2 == 3 and i2 == 15):
                     e.matmul(o, l, r, start=st, stop=sp)) for i2 in range(16)],
                    reads=[wb] + h[gh2 * 16:(gh2 + 1) * 16], out_buf=bK)
            wb2 = wck[par]
            P.dma("pool", wb2.ap.rearrange("p (ci e) -> p ci e", ci=NCI),
                  wcr_r[:, :, hc * 128:(hc + 1) * 128], writes=[wb2])
            bR = ps.banks[4 + par]
            P.mm_group([
                (lambda e, o=ps.bank(4 + par, TS), l=wb2.ap[:, ci * 128:(ci + 1) * 128],
                 r=cr[ci].ap, st=(ci == 0), sp=(ci == NCI - 1):
                 e.matmul(o, l, r, start=st, stop=sp)) for ci in range(NCI)],
                reads=[wb2] + cr, out_buf=bR)
            sg = sgf[0]
            P.op("act", lambda e, o=sg.ap, i=ps.bank(4 + par, TS): e.activation(o, i, AF.Sigmoid),
                 reads=[bR], writes=[sg])
            xr_ = xrld[par]
            P.dma("sp", xr_.ap, xorig[hc * 128:(hc + 1) * 128, tsl], writes=[xr_], deps=[P.xdep])
            xo = xnew[par]
            P.op("dve", lambda e, o=xo.ap, a=ps.bank(2 + par, TS), b=sg.ap:
                 e.tensor_mul(o, a, b), reads=[bK, sg], writes=[xo])
            P.op("dve", lambda e, o=xo.ap, a=xo.ap, b=xm[hc].ap: e.tensor_add(o, a, b),
                 reads=[xm[hc]], writes=[xo])
            P.op("dve", lambda e, o=xo.ap, a=xo.ap, b=xr_.ap: e.tensor_add(o, a, b),
                 reads=[xr_], writes=[xo])
            P.dma("sp", xout[hc * 128:(hc + 1) * 128, tsl], xo.ap, reads=[xo], deps=[P.xdep])
    af32.pop(P.stage_floor()); abf.pop(P.stage_floor())


def prep_attn1(inputs, pre):
    f32 = np.float32
    for n, key in (("wkt", "Wk"), ("wvt", "Wv"), ("wrt", "Wr"), ("wot", "Wo"),
                   ("wckt", "Wck"), ("wcvt", "Wcv"), ("wcrt", "Wcr")):
        pre["a1_" + n] = np.ascontiguousarray(inputs["L1_attn_" + key].T).astype(f32)
    for n, key in (("ln1w", "ln1_w"), ("ln1b", "ln1_b"), ("ln2w", "ln2_w"),
                   ("ln2b", "ln2_b"), ("tmk", "tmk"), ("tmv", "tmv"), ("tmr", "tmr"),
                   ("cmk", "cmk"), ("cmr", "cmr"), ("decay", "time_decay"),
                   ("first", "time_first")):
        pre["a1_" + n] = np.ascontiguousarray(
            inputs["L1_attn_" + key].reshape(16, 128).T).astype(f32)


ATTN1_NAMES = ["a1_wkt", "a1_wvt", "a1_wrt", "a1_wot", "a1_wckt", "a1_wcvt", "a1_wcrt",
               "a1_ln1w", "a1_ln1b", "a1_ln2w", "a1_ln2b", "a1_tmk", "a1_tmv", "a1_tmr",
               "a1_cmk", "a1_cmr", "a1_decay", "a1_first"]
ATTN1A_NAMES = ["a1_wkt", "a1_wvt", "a1_wrt", "a1_wot", "a1_ln1w", "a1_ln1b",
                "a1_tmk", "a1_tmv", "a1_tmr", "a1_decay", "a1_first"]
ATTN1A_SCRATCH = [("a1_wkb", [16, 128, 16, 128]), ("a1_wvb", [16, 128, 16, 128]),
                  ("a1_wrb", [16, 128, 16, 128]), ("a1_wob", [16, 128, 16, 128])]
ATTN1B_NAMES = ["a1_wckt", "a1_wcvt", "a1_wcrt", "a1_ln2w", "a1_ln2b", "a1_cmk", "a1_cmr"]
ATTN1B_SCRATCH = [("a1_wckb", [64, 128, 16, 128]), ("a1_wcvb", [16, 128, 64, 128]),
                  ("a1_wcrb", [16, 128, 16, 128])]


_default_chain()



ALL_NAMES = sorted(set(FFN0_NAMES + ATTN0_NAMES + FFN1_NAMES + ATTN1A_NAMES + ATTN1B_NAMES))
ALL_SCRATCH = list(ATTN0_SCRATCH) + list(FFN1_SCRATCH) + list(ATTN1A_SCRATCH) + \
    list(ATTN1B_SCRATCH)
FUSED = True


def build_fused():
    nc = bass.Bass("TRN2", target_bir_lowering=False)
    xin_t = nc.dram_tensor("x_in", [C, T], F32, kind="ExternalInput")
    xout_t = nc.dram_tensor("x_out", [C, T], F32, kind="ExternalOutput")
    xa = nc.dram_tensor("xa", [C, T], F32, kind="Internal")
    xb = nc.dram_tensor("xb", [C, T], F32, kind="Internal")
    xc = nc.dram_tensor("xc", [C, T], F32, kind="Internal")
    P = Prog(nc)
    ps = Psum(nc)
    af32 = Arena(nc, "af32", 17750, F32)
    abf = Arena(nc, "abf", 70200, BF16)
    sh = Shared(P, af32, abf)
    d = declare_drams(nc, PRE_CACHE, ALL_NAMES)
    for sc_ in ALL_SCRATCH:
        n, shp = sc_[0], sc_[1]
        dt = F32 if (len(sc_) > 2 and sc_[2] == "f32") else BF16
        d[n] = nc.dram_tensor(n, list(shp), dt, kind="Internal")
    d["a1_xorig"] = xb
    chain = [
        (emit_ffn0, xin_t, xa),
        (emit_wconv_h1, None, None),
        (emit_attn0, xa, xb),
        (emit_wconv_h2, None, None),
        (emit_ffn0, xb, xa),
        (emit_ffn1, xa, xb),
        (emit_attn1a, xb, xc),
        (emit_attn1b, xc, xa),
        (emit_ffn1, xa, xout_t),
    ]
    for fn, xi, xo in chain:
        P.xdep = [(f"q_sp_{i}", P.cnt[f"q_sp_{i}"]) for i in range(P.QW)]
        fn(P, ps, sh, af32, abf, d,
           xi.ap() if xi is not None else None,
           xo.ap() if xo is not None else None)
    P.finalize([(s, P.cnt[s]) for s in P.cnt])
    return nc


def kernel(**inputs):
    pre = PRE_CACHE
    pre.clear()
    for prep in PREPS:
        prep(inputs, pre)
    x = inputs["L0_attn_x"]
    xf = [np.ascontiguousarray(x[b].T).astype(np.float32) for b in range(8)]
    if FUSED:
        nc = build_fused()
        in_maps = []
        for b in range(8):
            m = {n: pre[n] for n in ALL_NAMES}
            m["x_in"] = xf[b]
            in_maps.append(m)
        res = run_bass_kernel_spmd(nc, in_maps, core_ids=list(range(8)))
        xf = [np.ascontiguousarray(res.results[b]["x_out"]) for b in range(8)]
    else:
        hist = [xf]
        for spec in LAUNCHES:
            stage_fns, names, scratch = spec[0], spec[1], spec[2]
            sizes = spec[3] if len(spec) > 3 else (15000, 72000)
            x2_from = spec[4] if len(spec) > 4 else None
            nc = build_single_stage(stage_fns, names, scratch, sizes,
                                    x2=(x2_from is not None))
            in_maps = []
            for b in range(8):
                m = {n: pre[n] for n in names}
                m["x_in"] = hist[-1][b]
                if x2_from is not None:
                    m["x_in2"] = hist[x2_from][b]
                in_maps.append(m)
            res = run_bass_kernel_spmd(nc, in_maps, core_ids=list(range(8)))
            hist.append([np.ascontiguousarray(res.results[b]["x_out"]) for b in range(8)])
        xf = hist[-1]
    out = np.stack([xf[b].T for b in range(8)])
    return np.ascontiguousarray(out).astype(np.float32)


def emit_wconv_h1(P, ps, sh, af32, abf, d, xin, xout):
    emit_wconv(P, ps, sh, af32, abf, d, xin, xout,
               jobs=[("f1_gq", "f1_gbf", 2048, 8192)])


def emit_wconv_h2(P, ps, sh, af32, abf, d, xin, xout):
    emit_wconv(P, ps, sh, af32, abf, d, xin, xout,
               jobs=[("f1_uq", "f1_ubf", 2048, 8192), ("f1_dq", "f1_dbf", 8192, 2048)])
